# version 17
# baseline (speedup 1.0000x reference)
import numpy as np
import ml_dtypes
import concourse.bass as bass
import concourse.mybir as mybir
from concourse.bass_utils import run_bass_kernel_spmd

F32 = mybir.dt.float32
F32R = mybir.dt.float32r
BF16 = mybir.dt.bfloat16
ALU = mybir.AluOpType
ACT = mybir.ActivationFunctionType
AX = mybir.AxisListType.X

NCORES = 8
D = 1024
H = 16
DH = 64
DC = D // 128
WIDTHS = [1024] * 4 + [2048] * 4
TOTW = sum(WIDTHS)
NWC = TOTW // 128
WC_EXPERT = [e for e in range(8) for _ in range(WIDTHS[e] // 128)]
LN_EPS = 1e-5
RMS_EPS = 1e-6

T = 2048
TC = T // 128
FG = 512

BUILD_UPTO = 99
DEV_TAP = None
SPW_CAP = 1024


def set_T(t):
    global T, TC, FG
    T = t
    TC = T // 128
    FG = min(512, T)


class Slot:
    __slots__ = ("w", "rs", "name")

    def __init__(self, name="", ghosts=()):
        self.w = None
        self.rs = list(ghosts)
        self.name = name


class Prog:
    CENG = ("tensor", "vector", "scalar", "gpsimd", "sync")

    def __init__(self, nc, dma_pool=14):
        self.nc = nc
        self.instrs = {e: [] for e in self.CENG}
        self._sem_ctxs = []
        self.sems = {}
        for e in self.CENG:
            self._mksem("e_" + e)
        self.dma_sems = {}
        for q in ("sync", "gpsimd", "scalar"):
            names = []
            for i in range(dma_pool):
                n = f"d_{q}{i}"
                self._mksem(n)
                names.append(n)
            self.dma_sems[q] = names
        self.cnt = {e: 0 for e in self.CENG}
        self.wm = {e: {} for e in self.CENG}
        self.dcnt = {n: 0 for q in self.dma_sems for n in self.dma_sems[q]}
        self.dwaited = {n: 0 for n in self.dcnt}
        self.dlive = {n: [] for n in self.dcnt}
        self.dorder = {n: [] for n in self.dcnt}
        self.drr = {q: 0 for q in self.dma_sems}
        self.erun = {e: [] for e in self.CENG}
        self.econs = {e: 0 for e in self.CENG}

    def _mksem(self, name):
        ctx = self.nc.semaphore(name)
        h = ctx.__enter__()
        self._sem_ctxs.append(ctx)
        self.sems[name] = h

    def _wait(self, eng, tok, waited_dma=None):
        sem, val = tok
        if self.wm[eng].get(sem, 0) >= val:
            return
        if sem in self.dcnt:
            assert val == self.dcnt[sem], (
                f"dma wait on {sem} for {val} but counter at {self.dcnt[sem]}")
            self.dwaited[sem] = max(self.dwaited[sem], val)
            if waited_dma is not None:
                waited_dma.append(sem)
        else:
            eng2 = sem[2:]
            if eng2 in self.econs:
                self.econs[eng2] = max(self.econs[eng2], val)
        self.wm[eng][sem] = val
        self.erun[eng] = []
        h = self.sems[sem]
        self.instrs[eng].append(lambda E, h=h, v=val: E.wait_ge(h, v))

    def _pre_inc(self, eng):
        run = self.erun[eng]
        if run and self.econs[eng] >= run[0][1]:
            h = self.sems["e_" + eng]
            v = self.cnt[eng]
            self.instrs[eng].append(lambda E, h=h, v=v: E.wait_ge(h, v))
            self.wm[eng]["e_" + eng] = v
            self.erun[eng] = []

    def _deps(self, reads, writes, war=()):
        deps = []
        for s in reads:
            if s.w is not None:
                deps.append(s.w)
        for s in tuple(writes) + tuple(war):
            deps.extend(s.rs)
            if s.w is not None:
                deps.append(s.w)
        best = {}
        for t in deps:
            if t[0] not in best or best[t[0]][1] < t[1]:
                best[t[0]] = t
        return list(best.values())

    def op(self, eng, emit, reads=(), writes=(), war=(), inc=True):
        wd = []
        for tok in self._deps(reads, writes, war):
            self._wait(eng, tok, waited_dma=wd)
        sem = "e_" + eng
        if inc:
            self._pre_inc(eng)
            self.cnt[eng] += 1
            for t in self.erun[eng]:
                t[1] = self.cnt[eng]
            tok = [sem, self.cnt[eng]]
            self.erun[eng].append(tok)
            h = self.sems[sem]
            self.instrs[eng].append(lambda E, f=emit, h=h: f(E).then_inc(h, 1))
        else:
            tok = [sem, self.cnt[eng]]
            self.instrs[eng].append(lambda E, f=emit: f(E))
        for s in reads:
            s.rs.append(tok)
        for s in writes:
            s.w = tok
            s.rs = []
        for dsem in wd:
            self.dorder[dsem].append(tok)
        return tok

    def mm_group(self, emits, reads=(), write=None, war=()):
        writes = (write,) if write is not None else ()
        wd = []
        for tok in self._deps(reads, writes, war):
            self._wait("tensor", tok, waited_dma=wd)
        sem = "e_tensor"
        h = self.sems[sem]
        for f in emits[:-1]:
            self.instrs["tensor"].append(lambda E, f=f: f(E))
        self._pre_inc("tensor")
        self.cnt["tensor"] += 1
        for t in self.erun["tensor"]:
            t[1] = self.cnt["tensor"]
        tok = [sem, self.cnt["tensor"]]
        self.erun["tensor"].append(tok)
        f = emits[-1]
        self.instrs["tensor"].append(lambda E, f=f, h=h: f(E).then_inc(h, 1))
        for s in reads:
            s.rs.append(tok)
        if write is not None:
            write.w = tok
            write.rs = []
        for dsem in wd:
            self.dorder[dsem].append(tok)
        return tok

    def dma(self, queue, emit, reads=(), writes=(), war=()):
        wd = []
        for tok in self._deps(reads, writes, war):
            self._wait(queue, tok, waited_dma=wd)
        pool = self.dma_sems[queue]
        for _ in range(len(pool)):
            n = pool[self.drr[queue] % len(pool)]
            self.drr[queue] += 1
            if self.dwaited[n] >= self.dcnt[n]:
                break
        else:
            n = pool[self.drr[queue] % len(pool)]
            self.drr[queue] += 1
        if self.dcnt[n] > 0:
            for t in list(self.dorder[n]):
                self._wait(queue, t, waited_dma=wd)
            self.dorder[n] = []
        self.dcnt[n] += 16
        for t in self.dlive[n]:
            t[1] = self.dcnt[n]
        tok = [n, self.dcnt[n]]
        self.dlive[n].append(tok)
        h = self.sems[n]
        self.instrs[queue].append(lambda E, f=emit, h=h: f(E).then_inc(h, 16))
        for s in reads:
            s.rs.append(tok)
        for s in writes:
            s.w = tok
            s.rs = []
        for dsem in wd:
            self.dorder[dsem].append(tok)
        return tok

    def barrier(self, engines=("tensor", "vector", "scalar")):
        toks = []
        for e in engines:
            self._pre_inc(e)
            self.cnt[e] += 1
            for t in self.erun[e]:
                t[1] = self.cnt[e]
            h = self.sems["e_" + e]

            def mk(E, h=h):
                ins = E.engine_nop() if hasattr(E, "engine_nop") else E.nop()
                return ins.then_inc(h, 1)
            self.instrs[e].append(mk)
            tok = ["e_" + e, self.cnt[e]]
            self.erun[e].append(tok)
            toks.append(tok)
        for e in engines:
            for tok in toks:
                self._wait(e, tok)

    def finalize(self, final_tokens):
        for tok in final_tokens:
            self._wait("sync", tok)

    def run_sections(self, block):
        for e in self.CENG:
            lst = self.instrs[e]

            def runner(E, lst=lst):
                for f in lst:
                    f(E)
            getattr(block, e)(runner)


def fslices(total, step=512):
    return [(i, min(step, total - i)) for i in range(0, total, step)]


class Arena:

    def __init__(self, nc, kib):
        self.t = nc.alloc_sbuf_tensor("arena", [128, kib * 1024], mybir.dt.uint8)
        self.free_list = [[0, kib * 1024]]
        self.regions = {}
        self.ghosts = []

    def alloc(self, name, shape, dtype, end=False):
        esz = {F32: 4, F32R: 4, BF16: 2}[dtype]
        size = (shape[1] * esz + 31) // 32 * 32
        frs = self.free_list if not end else list(reversed(self.free_list))
        for fr in frs:
            if fr[1] >= size:
                if end:
                    off = fr[0] + fr[1] - size
                    fr[1] -= size
                else:
                    off = fr[0]
                    fr[0] += size
                    fr[1] -= size
                break
        else:
            raise MemoryError(f"arena: no room for {name} ({size}B) "
                              f"free={self.free_list} regions={self.regions}")
        self.free_list = [f for f in self.free_list if f[1] > 0]
        self.regions[name] = (off, size)
        return self.t[:][0:shape[0], off:off + shape[1] * esz].bitcast(dtype)

    def free(self, name, tokens=()):
        off, size = self.regions.pop(name)
        self.free_list.append([off, size])
        self.free_list.sort()
        merged = []
        for fr in self.free_list:
            if merged and merged[-1][0] + merged[-1][1] == fr[0]:
                merged[-1][1] += fr[1]
            else:
                merged.append(fr)
        self.free_list = merged
        self.ghosts.extend(t for t in tokens if t is not None)

    def take_ghosts(self):
        return list(self.ghosts)


def slot_toks(*slots):
    out = []
    for s in slots:
        if s.w is not None:
            out.append(s.w)
        out.extend(s.rs)
    return out


def build_program():
    nc = bass.Bass("TRN2", target_bir_lowering=False)

    def PSUM(name, rows, cols):
        return nc.psum_tensor(name, [rows, (cols + 511) // 512 * 512], F32)
    p = Prog(nc)
    A = nc.alloc_sbuf_tensor
    ar = Arena(nc, 176)
    dbg = {}

    x_in = nc.dram_tensor("x", [T, D], F32, kind="ExternalInput").ap()
    qkv_w = nc.dram_tensor("qkv_w", [3 * D, D], F32, kind="ExternalInput").ap()
    proj_w = nc.dram_tensor("proj_w", [D, D], F32, kind="ExternalInput").ap()
    proj_b = nc.dram_tensor("proj_b", [1, D], F32, kind="ExternalInput").ap()
    q_norm_w = nc.dram_tensor("q_norm_w", [1, DH], F32, kind="ExternalInput").ap()
    k_norm_w = nc.dram_tensor("k_norm_w", [1, DH], F32, kind="ExternalInput").ap()
    router_w = nc.dram_tensor("router_w", [8, D], F32, kind="ExternalInput").ap()
    w1s = nc.dram_tensor("w1s", [128, 24 * 4096], F32, kind="ExternalInput").ap()
    w2s = nc.dram_tensor("w2s", [128, 96 * 1024], F32, kind="ExternalInput").ap()
    w1bf = nc.dram_tensor("w1bf", [128, 24 * 4096], BF16, kind="Internal").ap()
    w2bf = nc.dram_tensor("w2bf", [128, 96 * 1024], BF16, kind="Internal").ap()
    c_row = nc.dram_tensor("c_row", [1, D], F32, kind="ExternalInput").ap()
    wmod = nc.dram_tensor("wmod", [D, 6 * D], F32, kind="ExternalInput").ap()
    ident_b = nc.dram_tensor("ident_b", [128, 128], BF16, kind="ExternalInput").ap()
    ident_r = nc.dram_tensor("ident_r", [128, 128], F32, kind="ExternalInput").ap()
    eyerows = nc.dram_tensor("eyerows", [1, 256], F32, kind="ExternalInput").ap()
    selpat = nc.dram_tensor("selpat", [16, DC * 128], BF16,
                            kind="ExternalInput").ap()
    out = nc.dram_tensor("out", [T, D], F32, kind="ExternalOutput").ap()
    xfm_d = nc.dram_tensor("xfm_d", [128, DC * T], F32, kind="Internal").ap()
    x1tm_d = nc.dram_tensor("x1tm_d", [T, D], F32, kind="Internal").ap()
    sqc_d = nc.dram_tensor("sqc_d", [8, T], F32, kind="Internal").ap()

    identb = A("identb", [128, 128], BF16)
    identr = A("identr", [128, 128], F32)
    eyer = A("eyer", [1, 256], F32)
    selp = A("selp", [16, DC * 128], BF16)
    onesrow_r = A("onesrow_r", [1, 128], F32)
    onesrow_b = A("onesrow_b", [1, 128], BF16)
    onescol_b2 = A("onescol_b2", [128, 1], BF16)
    mod_fm = A("mod_fm", [128, 64], F32)
    mod_acc = A("mod_acc", [128, 48], F32)
    modfm_r = A("modfm_r", [128, 64], F32)
    modT = A("modT", [64, 128], BF16)
    rows_r = A("rows_r", [1, 24 * 128], BF16)
    sgn = A("sgn", [128, 16], F32)
    silc = A("silc", [128, DC], F32)
    wqpat = A("wqpat", [16, 1024], BF16)
    wkpat = A("wkpat", [16, 1024], BF16)
    sqpat = A("sqpat", [128, 16 * DC], BF16)
    wqsq = A("wqsq", [128, 16 * DC], BF16)
    wksq = A("wksq", [128, 16 * DC], BF16)
    bias_t = A("bias_t", [128, 16], F32)
    cvec = A("cvec", [16, 4], F32)
    crow = A("crow", [1, 16], F32)
    rwT = A("rwT", [128, 64], BF16)
    probs = A("probs", [128, 8 * TC], F32)
    comb = A("comb", [128, 8 * TC], F32)
    m8 = A("m8", [128, 8], F32)
    tt1 = A("tt1", [128, 8], F32)
    tt2 = A("tt2", [128, 8], F32)
    tt3 = A("tt3", [128, 8], F32)
    gbb = A("gbb", [128, D], F32)

    def modc(v, dc):
        return mod_fm[:, dc * 8 + v: dc * 8 + v + 1]

    def caplus_row(dc):
        return rows_r[:, (0 * 8 + dc) * 128:(0 * 8 + dc + 1) * 128]

    def cbplus_row(dc):
        return rows_r[:, (1 * 8 + dc) * 128:(1 * 8 + dc + 1) * 128]

    def gb_row(dc):
        return rows_r[:, (2 * 8 + dc) * 128:(2 * 8 + dc + 1) * 128]

    s_identb, s_identr, s_eyer = Slot("identb"), Slot("identr"), Slot("eyer")
    s_ones = Slot("ones")
    p.dma("sync", lambda E: E.dma_start(out=identb[:], in_=ident_b), writes=(s_identb,))
    p.dma("sync", lambda E: E.dma_start(out=identr[:], in_=ident_r), writes=(s_identr,))
    p.dma("sync", lambda E: E.dma_start(out=eyer[:], in_=eyerows), writes=(s_eyer,))
    s_selp = Slot("selp")
    p.dma("sync", lambda E: E.dma_start(out=selp[:], in_=selpat), writes=(s_selp,))
    p.op("vector", lambda E: E.memset(onesrow_r[:], 1.0), writes=(s_ones,))
    p.op("vector", lambda E: E.memset(onesrow_b[:], 1.0), writes=(s_ones,))
    p.op("vector", lambda E: E.memset(onescol_b2[:], 1.0), writes=(s_ones,))

    x_fm = ar.alloc("x_fm", [128, DC * T], F32)

    def xfm(dc):
        return x_fm[:, dc * T:(dc + 1) * T]
    s_xfm = [Slot(f"x_fm{dc}") for dc in range(DC)]

    xtm_st = [ar.alloc(f"xtm_st{i}", [128, D], F32) for i in range(3)]
    s_xtm = [Slot(f"xtm{i}") for i in range(3)]
    pT_ctx = [PSUM(f"pT{i}", 128, 512) for i in range(2)]
    pT = [c.__enter__() for c in pT_ctx]
    s_pT = [Slot(f"pT{i}") for i in range(2)]

    for tc in range(TC):
        i = tc % 3
        p.dma("sync",
              lambda E, tc=tc, i=i: E.dma_start(
                  out=xtm_st[i][:], in_=x_in[tc * 128:(tc + 1) * 128, :]),
              writes=(s_xtm[i],))
        for g in range(2):
            pj = (tc * 2 + g) % 2
            emits = []
            for k in range(4):
                dc = g * 4 + k
                emits.append(lambda E, i=i, dc=dc, pj=pj, k=k:
                             E.matmul(pT[pj][:, k * 128:(k + 1) * 128],
                                      xtm_st[i][:, dc * 128:(dc + 1) * 128],
                                      identr[:], start=True, stop=True))
            p.mm_group(emits, reads=(s_xtm[i], s_identr), write=s_pT[pj])

            def ev(E, pj=pj, g=g, tc=tc, xf=x_fm):
                return E.tensor_copy(
                    xf[:].rearrange("p (c t) -> p c t", c=DC)[
                        :, g * 4:(g + 1) * 4, tc * 128:(tc + 1) * 128],
                    pT[pj][:].rearrange("p (c t) -> p c t", c=4))
            p.op("vector", ev, reads=(s_pT[pj],),
                 writes=tuple(s_xfm[g * 4 + k] for k in range(4)))
    for c in reversed(pT_ctx):
        c.__exit__(None, None, None)
    p.barrier()
    dbg["x_fm"] = (s_xfm, xfm)

    s_modfm = Slot("mod_fm")
    p.op("vector", lambda E: E.memset(mod_fm[:], 0.0), writes=(s_modfm,))
    c_sb = ar.alloc("c_sb", [1, D], F32)
    csil = ar.alloc("csil", [1, D], F32)
    pb_sb = ar.alloc("pb_sb", [1, D], F32)
    s_c = Slot("c", ar.take_ghosts())
    p.dma("sync", lambda E: E.dma_start(out=c_sb[:], in_=c_row), writes=(s_c,))
    s_csil = Slot("csil", ar.take_ghosts())
    p.op("scalar", lambda E: E.activation(csil[:], c_sb[:], ACT.Silu),
         reads=(s_c,), writes=(s_csil,))
    s_silc = Slot("silc")
    pM_ctx = PSUM("pM", 128, 128)
    pM = pM_ctx.__enter__()
    s_pM = Slot("pM")
    emits = [
        (lambda E, dc=dc: E.matmul(pM[:, dc:dc + 1],
                                   csil[:, dc * 128:(dc + 1) * 128],
                                   onesrow_r[:, 0:1], start=True, stop=True))
        for dc in range(DC)]
    p.mm_group(emits, reads=(s_csil, s_ones), write=s_pM)
    p.op("vector", lambda E: E.tensor_copy(silc[:], pM[:, 0:DC]),
         reads=(s_pM,), writes=(s_silc,))

    wmod_st = [ar.alloc(f"wmod_st{i}", [128, 6 * D], F32) for i in range(2)]
    s_wmod = [Slot(f"wmod{i}") for i in range(2)]
    s_macc = Slot("mod_acc")
    pMD_ctx = PSUM("pMD", 128, 64)
    pMD = pMD_ctx.__enter__()
    s_pMD = Slot("pMD")
    for dci in range(DC):
        i = dci % 2
        p.dma("sync",
              lambda E, dci=dci, i=i: E.dma_start(
                  out=wmod_st[i][:], in_=wmod[dci * 128:(dci + 1) * 128, :]),
              writes=(s_wmod[i],))
        emits = []
        for v in range(6):
            for dc in range(DC):
                emits.append(lambda E, i=i, dci=dci, v=v, dc=dc:
                             E.matmul(pMD[:, dc * 8 + v:dc * 8 + v + 1],
                                      wmod_st[i][:, v * D + dc * 128:v * D + (dc + 1) * 128],
                                      silc[:, dci:dci + 1],
                                      start=True, stop=True))
        p.mm_group(emits, reads=(s_silc, s_wmod[i]), write=s_pMD)
        idx = pMD[:, 0:64].rearrange("p (c v) -> p c v", c=DC)[:, :, 0:6]
        dst = mod_acc[:].rearrange("p (c v) -> p c v", c=DC, v=6)
        if dci == 0:
            p.op("vector", lambda E, idx=idx, dst=dst: E.tensor_copy(dst, idx),
                 reads=(s_pMD,), writes=(s_macc,))
        else:
            p.op("vector", lambda E, idx=idx, dst=dst:
                 E.tensor_tensor(dst, dst, idx, ALU.add),
                 reads=(s_pMD,), writes=(s_macc,))
    ar.free("wmod_st0", slot_toks(s_wmod[0]))
    ar.free("wmod_st1", slot_toks(s_wmod[1]))

    s_pb = Slot("pb", ar.take_ghosts())
    p.dma("sync", lambda E: E.dma_start(out=pb_sb[:], in_=proj_b), writes=(s_pb,))
    p.op("vector",
         lambda E: E.tensor_copy(
             mod_fm[:].rearrange("p (c v) -> p c v", c=DC)[:, :, 0:6],
             mod_acc[:].rearrange("p (c v) -> p c v", c=DC, v=6)),
         reads=(s_macc,), writes=(s_modfm,))
    emits = [(lambda E, dc=dc: E.matmul(pMD[:, dc * 8 + 6:dc * 8 + 7],
                                        pb_sb[:, dc * 128:(dc + 1) * 128],
                                        onesrow_r[:, 0:1], start=True, stop=True))
             for dc in range(DC)]
    p.mm_group(emits, reads=(s_pb, s_ones), write=s_pMD)
    p.op("vector",
         lambda E: E.tensor_copy(
             mod_fm[:].rearrange("p (c v) -> p c v", c=DC)[:, :, 6:7],
             pMD[:, 0:64].rearrange("p (c v) -> p c v", c=DC)[:, :, 6:7]),
         reads=(s_pMD,), writes=(s_modfm,))

    s_sgn = Slot("sgn")

    def sgnv(j):
        return sgn[:, j * 8:(j + 1) * 8].rearrange("p (a b) -> p a b", b=1)
    for j, v in enumerate((2, 5)):
        view = mod_fm[:].rearrange("p (c v) -> p c v", c=DC)[:, :, v:v + 1]
        p.op("vector",
             lambda E, j=j, view=view:
             E.tensor_scalar(sgnv(j), view, 0.0, 2.0, ALU.is_ge, ALU.mult),
             reads=(s_modfm,), writes=(s_sgn,))
        p.op("vector",
             lambda E, j=j: E.tensor_scalar_add(sgnv(j), sgnv(j), -1.0),
             reads=(), writes=(s_sgn,))
        p.op("vector",
             lambda E, j=j, view=view:
             E.tensor_tensor(view, view, sgnv(j), ALU.mult),
             reads=(s_sgn,), writes=(s_modfm,))
        p.op("vector",
             lambda E, view=view:
             E.tensor_scalar_max(view, view, 0.25),
             reads=(), writes=(s_modfm,))
        p.op("vector",
             lambda E, j=j, view=view:
             E.tensor_tensor(view, view, sgnv(j), ALU.mult),
             reads=(s_sgn,), writes=(s_modfm,))
    for v in (1, 4):
        view = mod_fm[:].rearrange("p (c v) -> p c v", c=DC)[:, :, v:v + 1]
        p.op("vector",
             lambda E, view=view: E.tensor_scalar_add(view, view, 1.0),
             reads=(), writes=(s_modfm,))

    s_modT = Slot("modT")
    s_modfm_r = Slot("modfm_r")
    p.op("vector", lambda E: E.tensor_copy(modfm_r[:], mod_fm[:]),
         reads=(s_modfm,), writes=(s_modfm_r,))
    p.mm_group([lambda E: E.matmul(pM[0:64, 0:128], modfm_r[:], identr[:],
                                   start=True, stop=True)],
               reads=(s_modfm_r, s_identr), write=s_pM)
    p.op("vector", lambda E: E.tensor_copy(modT[:], pM[0:64, 0:128]),
         reads=(s_pM,), writes=(s_modT,))
    s_rows = [Slot(f"rows{i}") for i in range(24)]
    for si, v in enumerate((1, 4, 5)):
        for dc in range(DC):
            seg = si * 8 + dc
            p.dma("gpsimd",
                  lambda E, seg=seg, v=v, dc=dc: E.dma_start(
                      out=rows_r[:, seg * 128:(seg + 1) * 128],
                      in_=modT[dc * 8 + v:dc * 8 + v + 1, :]),
                  reads=(s_modT,), writes=(s_rows[seg],))

    wnorm = ar.alloc("wnorm", [1, 10 * 128], F32)
    wn_q = ar.alloc("wn_q", [1, DH], F32)
    wn_k = ar.alloc("wn_k", [1, DH], F32)
    s_wnq, s_wnk = Slot("wn_q", ar.take_ghosts()), Slot("wn_k", ar.take_ghosts())
    p.dma("sync", lambda E: E.dma_start(out=wn_q[:], in_=q_norm_w), writes=(s_wnq,))
    p.dma("sync", lambda E: E.dma_start(out=wn_k[:], in_=k_norm_w), writes=(s_wnk,))
    s_wnorm = Slot("wnorm", ar.take_ghosts())
    p.op("vector", lambda E: E.memset(wnorm[:], 0.0), writes=(s_wnorm,))
    sc = DH ** -0.5

    def wseg(i, hi):
        off = i * 128 + (DH if hi else 0)
        return wnorm[:, off:off + DH]
    p.op("vector", lambda E: E.tensor_scalar_mul(wseg(0, 0), wn_q[:], sc),
         reads=(s_wnq,), writes=(s_wnorm,))
    p.op("vector", lambda E: E.tensor_scalar_mul(wseg(1, 1), wn_q[:], sc),
         reads=(s_wnq,), writes=(s_wnorm,))
    p.op("vector", lambda E: E.tensor_tensor(wseg(2, 0), wseg(0, 0), wseg(0, 0),
                                             ALU.mult),
         reads=(), writes=(s_wnorm,))
    p.op("vector", lambda E: E.tensor_tensor(wseg(3, 1), wseg(0, 0), wseg(0, 0),
                                             ALU.mult),
         reads=(), writes=(s_wnorm,))
    p.op("vector", lambda E: E.tensor_copy(wseg(4, 0), wn_k[:]),
         reads=(s_wnk,), writes=(s_wnorm,))
    p.op("vector", lambda E: E.tensor_copy(wseg(5, 1), wn_k[:]),
         reads=(s_wnk,), writes=(s_wnorm,))
    p.op("vector", lambda E: E.tensor_tensor(wseg(6, 0), wseg(4, 0), wseg(4, 0),
                                             ALU.mult),
         reads=(), writes=(s_wnorm,))
    p.op("vector", lambda E: E.tensor_tensor(wseg(7, 1), wseg(4, 0), wseg(4, 0),
                                             ALU.mult),
         reads=(), writes=(s_wnorm,))
    p.op("vector", lambda E: E.memset(wseg(8, 0), 1.0), reads=(), writes=(s_wnorm,))
    p.op("vector", lambda E: E.memset(wseg(9, 1), 1.0), reads=(), writes=(s_wnorm,))

    s_wqpat, s_wkpat = Slot("wqpat"), Slot("wkpat")
    s_sqpat, s_wqsq, s_wksq = Slot("sqpat"), Slot("wqsq"), Slot("wksq")
    for dst, sl, ra, rb in ((wqpat, s_wqpat, 0, 1), (wkpat, s_wkpat, 4, 5)):
        for c in range(DC):
            emits = [
                (lambda E, c=c, ra=ra:
                 E.matmul(pM[0:16, 0:128], eyer[:, 16 * (2 * c):16 * (2 * c) + 16],
                          wnorm[:, ra * 128:(ra + 1) * 128], start=True, stop=False)),
                (lambda E, c=c, rb=rb:
                 E.matmul(pM[0:16, 0:128],
                          eyer[:, 16 * (2 * c + 1):16 * (2 * c + 1) + 16],
                          wnorm[:, rb * 128:(rb + 1) * 128], start=False, stop=True)),
            ]
            p.mm_group(emits, reads=(s_eyer, s_wnorm), write=s_pM)
            p.op("vector",
                 lambda E, c=c, dst=dst: E.tensor_copy(
                     dst[:, c * 128:(c + 1) * 128], pM[0:16, 0:128]),
                 reads=(s_pM,), writes=(sl,))
    for dst, sl, ra, rb in ((sqpat, s_sqpat, 8, 9), (wqsq, s_wqsq, 2, 3),
                            (wksq, s_wksq, 6, 7)):
        for c in range(DC):
            emits = [
                (lambda E, c=c, ra=ra:
                 E.matmul(pM[:, 0:16], wnorm[:, ra * 128:(ra + 1) * 128],
                          eyer[:, 16 * (2 * c):16 * (2 * c) + 16],
                          start=True, stop=False)),
                (lambda E, c=c, rb=rb:
                 E.matmul(pM[:, 0:16], wnorm[:, rb * 128:(rb + 1) * 128],
                          eyer[:, 16 * (2 * c + 1):16 * (2 * c + 1) + 16],
                          start=False, stop=True)),
            ]
            p.mm_group(emits, reads=(s_eyer, s_wnorm), write=s_pM)
            p.op("vector",
                 lambda E, c=c, dst=dst: E.tensor_copy(
                     dst[:, c * 16:(c + 1) * 16], pM[:, 0:16]),
                 reads=(s_pM,), writes=(sl,))

    pMD_ctx.__exit__(None, None, None)
    pM_ctx.__exit__(None, None, None)
    p.barrier()
    ar.free("wnorm", slot_toks(s_wnorm))
    ar.free("wn_q", slot_toks(s_wnq))
    ar.free("wn_k", slot_toks(s_wnk))
    ar.free("c_sb", slot_toks(s_c))
    ar.free("csil", slot_toks(s_csil))
    ar.free("pb_sb", slot_toks(s_pb))

    h1 = ar.alloc("h1", [128, DC * T], BF16)

    def h1c(dc):
        return h1[:, dc * T:(dc + 1) * T]
    s_h1 = [Slot(f"h1_{dc}", ar.take_ghosts()) for dc in range(DC)]

    def layernorm(xslots, xchunk, lrow_fn, lrow_slot, v_sa, hslots, hchunk, tag):
        ghosts = ar.take_ghosts()
        xsq = ar.alloc(f"xsq_{tag}", [128, T], BF16)
        xb = ar.alloc(f"xb_{tag}", [128, T], BF16)
        statr = ar.alloc(f"statr_{tag}", [1, 4 * T], F32)
        statb = ar.alloc(f"statb_{tag}", [1, 2 * T], BF16)
        htmp = ar.alloc(f"htmp_{tag}", [128, T], F32)
        s_xsq = Slot("xsq", ghosts)
        s_xb = Slot("xb", ghosts)
        s_stat = Slot("statrows", ghosts)
        s_statb = Slot("statb", ghosts)
        s_htmp = Slot("htmp", ghosts)
        pS_ctx = [PSUM(f"pS{tag}{i}", 1, T) for i in range(2)]
        pS = [c.__enter__() for c in pS_ctx]
        s_pS = [Slot(f"pS{i}") for i in range(2)]
        for dc in range(DC):
            p.op("vector",
                 lambda E, dc=dc: E.tensor_copy(xb[:], xchunk(dc)),
                 reads=(xslots[dc],), writes=(s_xb,))
            p.op("scalar",
                 lambda E, dc=dc: E.activation(xsq[:], xchunk(dc), ACT.Square),
                 reads=(xslots[dc],), writes=(s_xsq,))
            for which in (0, 1):
                emits = []
                for f0, fl in fslices(T):
                    emits.append(
                        lambda E, which=which, dc=dc, f0=f0, fl=fl:
                        E.matmul(pS[which][:, f0:f0 + fl], onescol_b2[:],
                                 (xb if which == 0 else xsq)[:, f0:f0 + fl],
                                 start=(dc == 0), stop=(dc == DC - 1)))
                rd = (s_ones, s_xb) if which == 0 else (s_ones, s_xsq)
                if dc == DC - 1:
                    p.mm_group(emits, reads=rd, write=s_pS[which])
                else:
                    for e in emits[:-1]:
                        p.op("tensor", e, reads=rd,
                             war=(s_pS[which],) if dc == 0 else (), inc=False)
                    p.op("tensor", emits[-1], reads=rd,
                         war=(s_pS[which],) if dc == 0 else (), inc=True)
        r_rstd = statr[:, 0:T]
        r_mrstd = statr[:, T:2 * T]
        tmp0 = statr[:, 2 * T:3 * T]
        tmp1 = statr[:, 3 * T:4 * T]
        p.op("vector", lambda E: E.tensor_scalar_mul(tmp0, pS[0][:, 0:T], 1.0 / D),
             reads=(s_pS[0],), writes=(s_stat,))
        p.op("vector", lambda E: E.tensor_scalar_mul(tmp1, pS[1][:, 0:T], 1.0 / D),
             reads=(s_pS[1],), writes=(s_stat,))
        p.op("vector", lambda E: E.tensor_tensor(r_mrstd, tmp0, tmp0, ALU.mult),
             reads=(), writes=(s_stat,))
        p.op("vector", lambda E: E.scalar_tensor_tensor(tmp1, tmp1, LN_EPS, r_mrstd,
                                                        ALU.add, ALU.subtract),
             reads=(), writes=(s_stat,))
        p.op("vector", lambda E: E.reciprocal(tmp1, tmp1), reads=(), writes=(s_stat,))
        p.op("scalar", lambda E: E.activation(r_rstd, tmp1, ACT.Sqrt),
             reads=(s_stat,), writes=(s_stat,))
        p.op("vector", lambda E: E.tensor_tensor(r_mrstd, tmp0, r_rstd, ALU.mult),
             reads=(), writes=(s_stat,))
        p.op("vector", lambda E: E.tensor_copy(statb[:, 0:T], r_rstd),
             reads=(s_stat,), writes=(s_statb,))
        p.op("vector", lambda E: E.tensor_copy(statb[:, T:2 * T], r_mrstd),
             reads=(s_stat,), writes=(s_statb,))
        for c in reversed(pS_ctx):
            c.__exit__(None, None, None)
        p.barrier()
        pA_ctx = PSUM(f"pA{tag}", 128, T)
        pB_ctx = PSUM(f"pB{tag}", 128, T)
        pA_, pB_ = pA_ctx.__enter__(), pB_ctx.__enter__()
        s_pA, s_pB = Slot("pA"), Slot("pB")
        for dc in range(DC):
            lrow = lrow_fn(dc)
            emA, emB = [], []
            for f0, fl in fslices(T):
                emA.append(lambda E, lrow=lrow, f0=f0, fl=fl:
                           E.matmul(pA_[:, f0:f0 + fl], lrow,
                                    statb[0:1, f0:f0 + fl], start=True, stop=True))
                emB.append(lambda E, lrow=lrow, f0=f0, fl=fl:
                           E.matmul(pB_[:, f0:f0 + fl], lrow,
                                    statb[0:1, T + f0:T + f0 + fl],
                                    start=True, stop=True))
            p.mm_group(emA, reads=(lrow_slot(dc), s_statb), write=s_pA)
            p.mm_group(emB, reads=(lrow_slot(dc), s_statb), write=s_pB)
            p.op("vector",
                 lambda E, dc=dc: E.tensor_tensor(htmp[:], xchunk(dc), pA_[:, 0:T],
                                                  ALU.mult),
                 reads=(xslots[dc], s_pA), writes=(s_htmp,))
            p.op("vector",
                 lambda E, dc=dc:
                 E.scalar_tensor_tensor(hchunk(dc), htmp[:], modc(v_sa, dc),
                                        pB_[:, 0:T], ALU.add, ALU.subtract),
                 reads=(s_htmp, s_modfm, s_pB), writes=(hslots[dc],))
        pB_ctx.__exit__(None, None, None)
        pA_ctx.__exit__(None, None, None)
        p.barrier()
        ar.free(f"xsq_{tag}", slot_toks(s_xsq))
        ar.free(f"xb_{tag}", slot_toks(s_xb))
        ar.free(f"statr_{tag}", slot_toks(s_stat))
        ar.free(f"statb_{tag}", slot_toks(s_statb))
        ar.free(f"htmp_{tag}", slot_toks(s_htmp))

    layernorm(s_xfm, xfm, caplus_row, lambda dc: s_rows[dc], 0, s_h1, h1c, "ln1")
    dbg["h1"] = (s_h1, h1c)
    if BUILD_UPTO < 3:
        return _finish(nc, p, ar, dbg, out, identb, identr, s_identb, s_identr)

    tok_xoff = []
    for dc in range(DC):
        tok_xoff.append(
            p.dma("sync",
                  lambda E, dc=dc: E.dma_start(out=xfm_d[:, dc * T:(dc + 1) * T],
                                               in_=xfm(dc)),
                  reads=(s_xfm[dc],)))
    ar.free("x_fm", slot_toks(*s_xfm))
    ar.free("xtm_st0", [])
    ar.free("xtm_st1", [])
    ar.free("xtm_st2", [])

    ghosts = ar.take_ghosts()
    wst = [ar.alloc(f"wst{i}", [128, D], BF16) for i in range(3)]
    s_wst = [Slot(f"wst{i}", ghosts) for i in range(3)]
    v_tm = ar.alloc("v_tm", [128, TC * H * 65], BF16)
    s_vtm = [Slot(f"v_tm{tc}", ghosts) for tc in range(TC)]

    def vtm(tc):
        return v_tm[:, tc * H * 65:(tc + 1) * H * 65]
    s_vones = Slot("vones", ghosts)
    p.op("vector", lambda E: E.memset(
        v_tm[:].rearrange("p (a e) -> p a e", e=65)[:, :, 64:65], 1.0),
        writes=(s_vones,))

    pW_ctx = PSUM("pW", 128, 512)
    pW = pW_ctx.__enter__()
    s_pW = Slot("pW")

    def prep_wT(mat_off, dstap, dst_slots):
        for pan in range(8):
            i = pan % 3
            p.dma("gpsimd",
                  lambda E, pan=pan, i=i, mat_off=mat_off: E.dma_start(
                      out=wst[i][:],
                      in_=qkv_w[mat_off + pan * 128:mat_off + (pan + 1) * 128, :]),
                  writes=(s_wst[i],))
            for g in range(2):
                emits = []
                for k in range(4):
                    dc = g * 4 + k
                    emits.append(lambda E, i=i, dc=dc, k=k:
                                 E.matmul(pW[:, k * 128:(k + 1) * 128],
                                          wst[i][:, dc * 128:(dc + 1) * 128],
                                          identb[:], start=True, stop=True))
                p.mm_group(emits, reads=(s_wst[i], s_identb), write=s_pW)

                def ev(E, g=g, pan=pan, dstap=dstap):
                    return E.tensor_copy(
                        dstap.rearrange("p (c j) -> p c j", c=DC)[
                            :, g * 4:(g + 1) * 4, pan * 128:(pan + 1) * 128],
                        pW[:].rearrange("p (c j) -> p c j", c=4))
                p.op("vector", ev, reads=(s_pW,),
                     writes=tuple(dst_slots[g * 4 + k] for k in range(4)))

    wvT = ar.alloc("wvT", [128, DC * D], BF16)
    s_wvT = [Slot(f"wvT{dc}", ghosts) for dc in range(DC)]
    prep_wT(2 * D, wvT, s_wvT)
    pV_ctx = [PSUM(f"pV{i}", 128, 1024) for i in range(2)]
    pV = [c.__enter__() for c in pV_ctx]
    s_pV = [Slot(f"pV{i}") for i in range(2)]
    for tc in range(TC):
        i = tc % 2
        emits = []
        for dc in range(DC):
            for j0, jl in fslices(D):
                emits.append(lambda E, tc=tc, dc=dc, j0=j0, jl=jl, i=i:
                             E.matmul(pV[i][:, j0:j0 + jl],
                                      h1c(dc)[:, tc * 128:(tc + 1) * 128],
                                      wvT[:, dc * D + j0:dc * D + j0 + jl],
                                      start=(dc == 0), stop=(dc == DC - 1)))
        p.mm_group(emits, reads=tuple(s_h1) + tuple(s_wvT), write=s_pV[i])

        def ev(E, tc=tc, i=i):
            return E.tensor_copy(
                vtm(tc).rearrange("p (h e) -> p h e", e=65)[:, :, 0:64],
                pV[i][:].rearrange("p (h e) -> p h e", e=64))
        p.op("vector", ev, reads=(s_pV[i],), writes=(s_vtm[tc],))
    for c in reversed(pV_ctx):
        c.__exit__(None, None, None)
    p.barrier()
    ar.free("wvT", slot_toks(*s_wvT))

    qpk = ar.alloc("qpk", [128, DC * T], BF16)
    kpk = ar.alloc("kpk", [128, DC * T], BF16)

    def qpkc(c):
        return qpk[:, c * T:(c + 1) * T]

    def kpkc(c):
        return kpk[:, c * T:(c + 1) * T]
    ghosts = ar.take_ghosts()
    s_qpk = [Slot(f"qpk{c}", ghosts) for c in range(DC)]
    s_kpk = [Slot(f"kpk{c}", ghosts) for c in range(DC)]
    wqT = ar.alloc("wqT", [128, DC * D], BF16)
    s_wqT = [Slot(f"wqT{dc}", ghosts) for dc in range(DC)]
    prep_wT(0, wqT, s_wqT)

    pQ_ctx = [PSUM(f"pQ{i}", 128, 512) for i in range(2)]
    pQ = [c.__enter__() for c in pQ_ctx]
    s_pQ = [Slot(f"pQ{i}") for i in range(2)]
    qkn = [0]

    def qk_prod(wT, s_wT, dst, s_dst):
        for c in range(DC):
            for f0, fl in fslices(T):
                i = qkn[0] % 2
                qkn[0] += 1
                emits = []
                for dc in range(DC):
                    emits.append(lambda E, c=c, dc=dc, f0=f0, fl=fl, i=i, wT=wT:
                                 E.matmul(pQ[i][:, 0:fl],
                                          wT[:, dc * D + c * 128:dc * D + (c + 1) * 128],
                                          h1c(dc)[:, f0:f0 + fl],
                                          start=(dc == 0), stop=(dc == DC - 1)))
                p.mm_group(emits, reads=tuple(s_h1) + tuple(s_wT), write=s_pQ[i])
                p.op("vector",
                     lambda E, c=c, f0=f0, fl=fl, i=i, dst=dst:
                     E.tensor_copy(dst(c)[:, f0:f0 + fl], pQ[i][:, 0:fl]),
                     reads=(s_pQ[i],), writes=(s_dst[c],))

    qk_prod(wqT, s_wqT, qpkc, s_qpk)
    ar.free("wqT", slot_toks(*s_wqT))
    wkT = ar.alloc("wkT", [128, DC * D], BF16)
    s_wkT = [Slot(f"wkT{dc}", ar.take_ghosts()) for dc in range(DC)]
    prep_wT(D, wkT, s_wkT)
    qk_prod(wkT, s_wkT, kpkc, s_kpk)
    for c in reversed(pQ_ctx):
        c.__exit__(None, None, None)
    pW_ctx.__exit__(None, None, None)
    p.barrier()
    ar.free("wkT", slot_toks(*s_wkT))
    ar.free("wst0", slot_toks(s_wst[0]))
    ar.free("wst1", slot_toks(s_wst[1]))
    ar.free("wst2", slot_toks(s_wst[2]))
    ar.free("h1", slot_toks(*s_h1))
    dbg["qpk"] = (s_qpk, qpkc)
    dbg["kpk"] = (s_kpk, kpkc)
    if BUILD_UPTO < 4:
        return _finish(nc, p, ar, dbg, out, identb, identr, s_identb, s_identr)

    ghosts = ar.take_ghosts()
    qsq = ar.alloc("qsq", [128, T], BF16)
    stat = ar.alloc("stat", [16, 4 * T], F32)
    statbh = ar.alloc("statbh", [16, T], BF16)
    s_qsq = Slot("qsq", ghosts)
    s_stat = Slot("stat4", ghosts)
    s_statbh = Slot("statbh", ghosts)
    r_rstd_q = stat[:, 0 * T:1 * T]
    r_rstd_k = stat[:, 1 * T:2 * T]
    r_tmp = stat[:, 2 * T:3 * T]
    r_tmp2 = stat[:, 3 * T:4 * T]
    s_bias = Slot("bias_t")
    s_cvec = Slot("cvec")
    s_crow = Slot("crow")

    for side, (pkc, s_pk, wpat, s_wpat, wsqp, s_wsqp, r_rstd, cvcol) in enumerate((
            (qpkc, s_qpk, wqpat, s_wqpat, wqsq, s_wqsq, r_rstd_q, 0),
            (kpkc, s_kpk, wkpat, s_wkpat, wksq, s_wksq, r_rstd_k, 1))):
        pSS_ctx = [PSUM(f"pSS{side}_{i}", 16, T) for i in range(2)]
        pSS = [c.__enter__() for c in pSS_ctx]
        s_pSS = [Slot(f"pSS{i}") for i in range(2)]
        for c in range(DC):
            p.op("vector",
                 lambda E, c=c, pkc=pkc: E.tensor_tensor(qsq[:], pkc(c), pkc(c),
                                                         ALU.mult),
                 reads=(s_pk[c],), writes=(s_qsq,))
            for which, pat, s_pat in ((0, sqpat, s_sqpat), (1, wsqp, s_wsqp)):
                emits = []
                for f0, fl in fslices(T):
                    emits.append(lambda E, c=c, f0=f0, fl=fl, which=which, pat=pat:
                                 E.matmul(pSS[which][:, f0:f0 + fl],
                                          pat[:, c * 16:(c + 1) * 16],
                                          qsq[:, f0:f0 + fl],
                                          start=(c == 0), stop=(c == DC - 1)))
                rd = (s_qsq, s_pat)
                if c == DC - 1:
                    p.mm_group(emits, reads=rd, write=s_pSS[which])
                else:
                    for e in emits[:-1]:
                        p.op("tensor", e, reads=rd,
                             war=(s_pSS[which],) if c == 0 else (), inc=False)
                    p.op("tensor", emits[-1], reads=rd,
                         war=(s_pSS[which],) if c == 0 else (), inc=True)
        p.op("vector", lambda E, r_rstd=r_rstd:
             E.tensor_scalar(r_rstd, pSS[0][:, 0:T], 1.0 / DH, RMS_EPS,
                             ALU.mult, ALU.add),
             reads=(s_pSS[0],), writes=(s_stat,))
        p.op("vector", lambda E, r_rstd=r_rstd: E.reciprocal(r_rstd, r_rstd),
             reads=(), writes=(s_stat,))
        p.op("scalar", lambda E, r_rstd=r_rstd: E.activation(r_rstd, r_rstd,
                                                             ACT.Sqrt),
             reads=(s_stat,), writes=(s_stat,))
        p.op("vector", lambda E, r_rstd=r_rstd:
             E.tensor_tensor(r_tmp, r_rstd, r_rstd, ALU.mult),
             reads=(), writes=(s_stat,))
        p.op("vector", lambda E:
             E.tensor_tensor(r_tmp2, pSS[1][:, 0:T], r_tmp, ALU.mult),
             reads=(s_pSS[1],), writes=(s_stat,))
        p.op("vector", lambda E, cvcol=cvcol:
             E.reduce_max(cvec[:, cvcol:cvcol + 1], r_tmp2, axis=AX),
             reads=(s_stat,), writes=(s_cvec,))
        for c_ in reversed(pSS_ctx):
            c_.__exit__(None, None, None)
        p.barrier()
        pRB_ctx = PSUM(f"pRB{side}", 128, T)
        pRB = pRB_ctx.__enter__()
        s_pRB = Slot("pRB")
        p.op("vector", lambda E, r_rstd=r_rstd: E.tensor_copy(statbh[:, 0:T], r_rstd),
             reads=(s_stat,), writes=(s_statbh,))
        for c in range(DC):
            emits = []
            for f0, fl in fslices(T):
                emits.append(lambda E, c=c, f0=f0, fl=fl, wpat=wpat:
                             E.matmul(pRB[:, f0:f0 + fl],
                                      wpat[:, c * 128:(c + 1) * 128],
                                      statbh[0:16, f0:f0 + fl],
                                      start=True, stop=True))
            p.mm_group(emits, reads=(s_wpat, s_statbh), write=s_pRB)
            p.op("vector",
                 lambda E, c=c, pkc=pkc: E.tensor_tensor(pkc(c), pkc(c),
                                                         pRB[:, 0:T], ALU.mult),
                 reads=(s_pRB,), writes=(s_pk[c],))
        pRB_ctx.__exit__(None, None, None)
        p.barrier()
    p.op("vector", lambda E: E.tensor_tensor(cvec[:, 2:3], cvec[:, 0:1],
                                             cvec[:, 1:2], ALU.mult),
         reads=(s_cvec,), writes=(s_cvec,))
    p.op("scalar", lambda E: E.activation(cvec[:, 3:4], cvec[:, 2:3], ACT.Sqrt),
         reads=(s_cvec,), writes=(s_cvec,))
    pC_ctx = PSUM("pC", 128, 32)
    pC = pC_ctx.__enter__()
    s_pC = Slot("pC")
    cvr = ar.alloc("cvr", [16, 4], F32)
    s_cvr = Slot("cvr", ghosts)
    p.op("vector", lambda E: E.tensor_copy(cvr[:], cvec[:]),
         reads=(s_cvec,), writes=(s_cvr,))
    p.mm_group([lambda E: E.matmul(pC[0:1, 0:16], cvr[:, 3:4],
                                   identr[0:16, 0:16], start=True, stop=True)],
               reads=(s_cvr, s_identr), write=s_pC)
    p.op("vector", lambda E: E.tensor_scalar_mul(crow[:], pC[0:1, 0:16], -1.0),
         reads=(s_pC,), writes=(s_crow,))
    crowr = ar.alloc("crowr", [1, 16], F32)
    s_crowr = Slot("crowr", ghosts)
    p.op("vector", lambda E: E.tensor_copy(crowr[:], crow[:]),
         reads=(s_crow,), writes=(s_crowr,))
    p.mm_group([lambda E: E.matmul(pC[:, 128:144], onesrow_r[:], crowr[:, 0:16],
                                   start=True, stop=True)],
               reads=(s_crowr, s_ones), write=s_pC)
    p.op("vector", lambda E: E.tensor_copy(bias_t[:], pC[:, 128:144]),
         reads=(s_pC,), writes=(s_bias,))
    pC_ctx.__exit__(None, None, None)
    p.barrier()
    ar.free("qsq", slot_toks(s_qsq))
    ar.free("stat", slot_toks(s_stat))
    ar.free("statbh", slot_toks(s_statbh))
    ar.free("cvr", slot_toks(s_cvr))
    ar.free("crowr", slot_toks(s_crowr))
    dbg["qn"] = (s_qpk, qpkc)
    dbg["kn"] = (s_kpk, kpkc)
    if BUILD_UPTO < 5:
        return _finish(nc, p, ar, dbg, out, identb, identr, s_identb, s_identr)

    s_w1bf = [Slot(f"w1bf{g}") for g in range(24)]
    s_w2bf = [Slot(f"w2bf{q}") for q in range(24)]

    def stage_chunk(k):
        if k < 24:
            p.dma("gpsimd",
                  lambda E, g=k: E.dma_start(
                      out=w1bf[:, g * 4096:(g + 1) * 4096],
                      in_=w1s[:, g * 4096:(g + 1) * 4096]),
                  writes=(s_w1bf[k],))
        elif k < 48:
            p.dma("gpsimd",
                  lambda E, q=k - 24: E.dma_start(
                      out=w2bf[:, q * 4096:(q + 1) * 4096],
                      in_=w2s[:, q * 4096:(q + 1) * 4096]),
                  writes=(s_w2bf[k - 24],))
    ghosts = ar.take_ghosts()
    attn = ar.alloc("attn", [128, DC * T], BF16, end=True)

    def attnc(c):
        return attn[:, c * T:(c + 1) * T]
    s_attn = [Slot(f"attn{c}", ghosts) for c in range(DC)]
    qx = [ar.alloc(f"qx{par}", [128, T], BF16) for par in range(2)]
    s_qx = [Slot(f"qx{par}", ghosts) for par in range(2)]
    p.op("vector", lambda E: E.memset(qx[0][64:128, :], 0.0), writes=(s_qx[0],))
    p.op("vector", lambda E: E.memset(qx[1][0:64, :], 0.0), writes=(s_qx[1],))
    pt = [ar.alloc(f"pt{i}", [128, 1024], BF16) for i in range(3)]
    s_pt = [Slot(f"pt{i}", ghosts) for i in range(3)]
    ast = [ar.alloc(f"ast{i}", [64, T], BF16) for i in range(2)]
    s_ast = [Slot(f"ast{i}", ghosts) for i in range(2)]
    dtmp = [ar.alloc(f"dtmp{i}", [1, T], BF16) for i in range(2)]
    s_dtmp = [Slot(f"dtmp{i}", ghosts) for i in range(2)]
    dnm = ar.alloc("dnm", [16, T], BF16)
    s_dnm = [Slot(f"dnm{h}", ghosts) for h in range(H)]
    dnr = ar.alloc("dnr", [16, T], BF16)
    s_dnr = Slot("dnr", ghosts)

    psp_ctx = [PSUM(f"psp{i}", 128, 1024) for i in range(2)]
    psp = [c.__enter__() for c in psp_ctx]
    s_psp = [Slot(f"psp{i}") for i in range(2)]
    ppv_ctx = PSUM("ppv", 65, T)
    ppv = ppv_ctx.__enter__()

    SPW = min(SPW_CAP, T)
    NHH = T // SPW
    s_ppv = [Slot(f"ppv{nh}") for nh in range(NHH)]

    def qcopy(h):
        c, par = h // 2, h % 2
        p.dma("sync",
              lambda E, c=c, par=par: E.dma_start(
                  out=qx[par][par * 64:(par + 1) * 64, 0:T],
                  in_=qpkc(c)[par * 64:(par + 1) * 64, 0:T]),
              reads=(s_qpk[c],), writes=(s_qx[par],))

    def flush_pv(h, mc, nh, j):
        emits = []
        for f0, fl in fslices(SPW):
            emits.append(
                lambda E, mc=mc, nh=nh, j=j, h=h, f0=f0, fl=fl:
                E.matmul(ppv[0:65, nh * SPW + f0:nh * SPW + f0 + fl],
                         vtm(mc)[:, h * 65:(h + 1) * 65],
                         pt[j][:, f0:f0 + fl],
                         start=(mc == 0), stop=(mc == TC - 1)))
        rd = (s_pt[j], s_vtm[mc])
        if mc == TC - 1:
            p.mm_group(emits, reads=rd, write=s_ppv[nh])
        else:
            for e in emits[:-1]:
                p.op("tensor", e, reads=rd,
                     war=(s_ppv[nh],) if mc == 0 else (), inc=False)
            p.op("tensor", emits[-1], reads=rd,
                 war=(s_ppv[nh],) if mc == 0 else (), inc=True)

    qcopy(0)
    sub = 0
    for h in range(H):
        c, par = h // 2, h % 2
        if h + 1 < H:
            qcopy(h + 1)
        for k in range(3 * h, 3 * h + 3):
            stage_chunk(k)
        pend = None
        for mc in range(TC):
            for nh in range(NHH):
                i = sub % 2
                j = sub % 3
                sub += 1
                emits = []
                for f0, fl in fslices(SPW):
                    emits.append(
                        lambda E, c=c, mc=mc, nh=nh, f0=f0, fl=fl, i=i, par=par:
                        E.matmul(psp[i][:, f0:f0 + fl],
                                 kpkc(c)[:, mc * 128:(mc + 1) * 128],
                                 qx[par][:, nh * SPW + f0:nh * SPW + f0 + fl],
                                 start=True, stop=True))
                p.mm_group(emits, reads=(s_kpk[c], s_qx[par]), write=s_psp[i])
                p.op("scalar",
                     lambda E, i=i, j=j, h=h: E.activation(
                         pt[j][:, 0:SPW], psp[i][:, 0:SPW], ACT.Exp,
                         bias=bias_t[:, h:h + 1], scale=1.0),
                     reads=(s_psp[i], s_bias), writes=(s_pt[j],))
                if pend is not None:
                    flush_pv(*pend)
                pend = (h, mc, nh, j)
        flush_pv(*pend)
        di = h % 2
        p.op("scalar", lambda E, di=di: E.activation(dtmp[di][:, 0:T],
                                                     ppv[64:65, 0:T], ACT.Copy),
             reads=tuple(s_ppv), writes=(s_dtmp[di],))
        p.dma("sync",
              lambda E, h=h, di=di: E.dma_start(out=dnm[h:h + 1, 0:T],
                                                in_=dtmp[di][:, 0:T]),
              reads=(s_dtmp[di],), writes=(s_dnm[h],))
        if par == 0:
            for nh in range(NHH):
                p.op("vector",
                     lambda E, nh=nh, c=c: E.tensor_copy(
                         attnc(c)[0:64, nh * SPW:(nh + 1) * SPW],
                         ppv[0:64, nh * SPW:(nh + 1) * SPW]),
                     reads=(s_ppv[nh],), writes=(s_attn[c],))
        else:
            ai = (h // 2) % 2
            for nh in range(NHH):
                p.op("vector",
                     lambda E, nh=nh, ai=ai: E.tensor_copy(
                         ast[ai][0:64, nh * SPW:(nh + 1) * SPW],
                         ppv[0:64, nh * SPW:(nh + 1) * SPW]),
                     reads=(s_ppv[nh],), writes=(s_ast[ai],))
            p.dma("sync",
                  lambda E, c=c, ai=ai: E.dma_start(out=attnc(c)[64:128, :],
                                                    in_=ast[ai][0:64, :]),
                  reads=(s_ast[ai],), writes=(s_attn[c],))
    ppv_ctx.__exit__(None, None, None)
    for c in reversed(psp_ctx):
        c.__exit__(None, None, None)
    p.op("vector", lambda E: E.reciprocal(dnr[0:16, 0:T], dnm[0:16, 0:T]),
         reads=tuple(s_dnm), writes=(s_dnr,))
    pbc_ctx = [PSUM(f"pbc{i}", 128, SPW) for i in range(2)]
    pbc = [c.__enter__() for c in pbc_ctx]
    s_pbc = [Slot(f"pbc{i}") for i in range(2)]
    nn = 0
    for c in range(DC):
        for nh in range(NHH):
            i = nn % 2
            nn += 1
            emits = []
            for f0, fl in fslices(SPW):
                emits.append(lambda E, c=c, nh=nh, f0=f0, fl=fl, i=i:
                             E.matmul(pbc[i][:, f0:f0 + fl],
                                      selp[0:16, c * 128:(c + 1) * 128],
                                      dnr[0:16, nh * SPW + f0:nh * SPW + f0 + fl],
                                      start=True, stop=True))
            p.mm_group(emits, reads=(s_dnr, s_selp), write=s_pbc[i])
            p.op("vector",
                 lambda E, c=c, nh=nh, i=i: E.tensor_tensor(
                     attnc(c)[:, nh * SPW:(nh + 1) * SPW],
                     attnc(c)[:, nh * SPW:(nh + 1) * SPW],
                     pbc[i][:, 0:SPW], ALU.mult),
                 reads=(s_pbc[i],), writes=(s_attn[c],))
    for ctx in reversed(pbc_ctx):
        ctx.__exit__(None, None, None)
    p.barrier()
    ar.free("qpk", slot_toks(*s_qpk))
    ar.free("kpk", slot_toks(*s_kpk))
    ar.free("v_tm", slot_toks(*s_vtm) + slot_toks(s_vones))
    ar.free("qx0", slot_toks(s_qx[0]))
    ar.free("qx1", slot_toks(s_qx[1]))
    ar.free("pt0", slot_toks(s_pt[0]))
    ar.free("pt1", slot_toks(s_pt[1]))
    ar.free("pt2", slot_toks(s_pt[2]))
    ar.free("ast0", slot_toks(s_ast[0]))
    ar.free("ast1", slot_toks(s_ast[1]))
    ar.free("dtmp0", slot_toks(s_dtmp[0]))
    ar.free("dtmp1", slot_toks(s_dtmp[1]))
    ar.free("dnm", slot_toks(*s_dnm))
    ar.free("dnr", slot_toks(s_dnr))
    dbg["attn"] = (s_attn, attnc)
    if BUILD_UPTO < 6:
        return _finish(nc, p, ar, dbg, out, identb, identr, s_identb, s_identr)

    ghosts = ar.take_ghosts()
    pwT = ar.alloc("pwT", [128, DC * D], BF16, end=True)
    s_pwT = [Slot(f"pwT{dc}", ghosts) for dc in range(DC)]
    wst2 = [ar.alloc(f"wst2_{i}", [128, D], BF16, end=True)
            for i in range(3)]
    s_wst2 = [Slot(f"wst2_{i}", ghosts) for i in range(3)]
    xb2 = ar.alloc("x_fm2", [128, DC * T], F32)

    def xfm2(dc, xb2=xb2):
        return xb2[:, dc * T:(dc + 1) * T]
    s_xfm2 = [Slot(f"x2_{dc}", ghosts + tok_xoff) for dc in range(DC)]
    for dc in range(DC):
        p.dma("sync",
              lambda E, dc=dc: E.dma_start(out=xfm2(dc),
                                           in_=xfm_d[:, dc * T:(dc + 1) * T]),
              writes=(s_xfm2[dc],))

    pW2_ctx = PSUM("pW2", 128, 512)
    pW2 = pW2_ctx.__enter__()
    s_pW2 = Slot("pW2")
    for pan in range(8):
        i = pan % 3
        p.dma("gpsimd",
              lambda E, pan=pan, i=i: E.dma_start(
                  out=wst2[i][:], in_=proj_w[pan * 128:(pan + 1) * 128, :]),
              writes=(s_wst2[i],))
        for g in range(2):
            emits = []
            for k in range(4):
                dc = g * 4 + k
                emits.append(lambda E, i=i, dc=dc, k=k:
                             E.matmul(pW2[:, k * 128:(k + 1) * 128],
                                      wst2[i][:, dc * 128:(dc + 1) * 128],
                                      identb[:], start=True, stop=True))
            p.mm_group(emits, reads=(s_wst2[i], s_identb), write=s_pW2)

            def ev(E, g=g, pan=pan):
                return E.tensor_copy(
                    pwT[:].rearrange("p (c j) -> p c j", c=DC)[
                        :, g * 4:(g + 1) * 4, pan * 128:(pan + 1) * 128],
                    pW2[:].rearrange("p (c j) -> p c j", c=4))
            p.op("vector", ev, reads=(s_pW2,),
                 writes=tuple(s_pwT[g * 4 + k] for k in range(4)))
    pW2_ctx.__exit__(None, None, None)
    p.barrier()

    ptmp = ar.alloc("ptmp", [128, 512], F32, end=True)
    s_ptmp = Slot("ptmp", ghosts)
    pP_ctx = [PSUM(f"pP{i}", 128, 512) for i in range(2)]
    pP = [c.__enter__() for c in pP_ctx]
    s_pP = [Slot(f"pP{i}") for i in range(2)]
    n2 = 0
    for jc in range(DC):
        for f0, fl in fslices(T):
            i = n2 % 2
            n2 += 1
            emits = []
            for dc in range(DC):
                emits.append(lambda E, jc=jc, dc=dc, f0=f0, fl=fl, i=i:
                             E.matmul(pP[i][:, 0:fl],
                                      pwT[:, dc * D + jc * 128:dc * D + (jc + 1) * 128],
                                      attnc(dc)[:, f0:f0 + fl],
                                      start=(dc == 0), stop=(dc == DC - 1)))
            p.mm_group(emits, reads=tuple(s_attn) + tuple(s_pwT), write=s_pP[i])
            p.op("vector",
                 lambda E, jc=jc, f0=f0, fl=fl, i=i:
                 E.tensor_scalar(ptmp[:, 0:fl], pP[i][:, 0:fl],
                                 modc(6, jc), modc(2, jc), ALU.add, ALU.mult),
                 reads=(s_pP[i], s_modfm), writes=(s_ptmp,))
            p.op("vector",
                 lambda E, jc=jc, f0=f0, fl=fl:
                 E.tensor_tensor(xfm2(jc)[:, f0:f0 + fl], xfm2(jc)[:, f0:f0 + fl],
                                 ptmp[:, 0:fl], ALU.add),
                 reads=(s_ptmp,), writes=(s_xfm2[jc],))
    for c in reversed(pP_ctx):
        c.__exit__(None, None, None)
    p.barrier()
    ar.free("attn", slot_toks(*s_attn))
    ar.free("pwT", slot_toks(*s_pwT))
    ar.free("wst2_0", slot_toks(s_wst2[0]))
    ar.free("wst2_1", slot_toks(s_wst2[1]))
    ar.free("wst2_2", slot_toks(s_wst2[2]))
    ar.free("ptmp", slot_toks(s_ptmp))
    dbg["x1"] = (s_xfm2, xfm2)
    if BUILD_UPTO < 7:
        return _finish(nc, p, ar, dbg, out, identb, identr, s_identb, s_identr)

    h2 = ar.alloc("h2", [128, DC * T], BF16, end=True)

    def h2c(dc):
        return h2[:, dc * T:(dc + 1) * T]
    s_h2 = [Slot(f"h2_{dc}", ar.take_ghosts()) for dc in range(DC)]
    layernorm(s_xfm2, xfm2, cbplus_row, lambda dc: s_rows[8 + dc], 3, s_h2, h2c,
              "ln2")
    dbg["h2"] = (s_h2, h2c)

    ghosts = ar.take_ghosts()
    x1st = [ar.alloc(f"x1st{i}", [128, D], F32) for i in range(2)]
    s_x1st = [Slot(f"x1st{i}", ghosts) for i in range(2)]
    pX_ctx = PSUM("pX", 128, 1024)
    pX = pX_ctx.__enter__()
    s_pX = Slot("pX")
    tok_x1 = []
    for tc in range(TC):
        i = tc % 2
        emits = []
        for dc in range(DC):
            emits.append(lambda E, dc=dc, tc=tc:
                         E.matmul(pX[:, dc * 128:(dc + 1) * 128],
                                  xfm2(dc)[:, tc * 128:(tc + 1) * 128],
                                  identr[:], start=True, stop=True))
        p.mm_group(emits, reads=tuple(s_xfm2) + (s_identr,), write=s_pX)
        p.op("vector", lambda E, i=i: E.tensor_copy(x1st[i][:], pX[:]),
             reads=(s_pX,), writes=(s_x1st[i],))
        tok_x1.append(
            p.dma("sync",
                  lambda E, tc=tc, i=i: E.dma_start(
                      out=x1tm_d[tc * 128:(tc + 1) * 128, :], in_=x1st[i][:]),
                  reads=(s_x1st[i],)))
    pX_ctx.__exit__(None, None, None)
    p.barrier()
    ar.free("x_fm2", slot_toks(*s_xfm2))
    ar.free("x1st0", slot_toks(s_x1st[0]))
    ar.free("x1st1", slot_toks(s_x1st[1]))
    if BUILD_UPTO < 8:
        return _finish(nc, p, ar, dbg, out, identb, identr, s_identb, s_identr)

    ghosts = ar.take_ghosts()
    s_rwT = Slot("rwT")
    rw_b = ar.alloc("rw_b", [128, D], BF16)
    s_rwb = Slot("rw_b", ghosts)
    p.op("vector", lambda E: E.memset(rw_b[:], 0.0), writes=(s_rwb,))
    p.dma("gpsimd", lambda E: E.dma_start(out=rw_b[0:8, :], in_=router_w),
          writes=(s_rwb,))
    GT = FG // 128
    NTI = T // FG
    sqct = [ar.alloc(f"sqct{i}", [8, FG], BF16) for i in range(2)]
    s_sqct = [Slot(f"sqct{i}", ghosts) for i in range(2)]
    s_sqc = [Slot(f"sqc{ti}") for ti in range(NTI)]
    s_probs = Slot("probs")
    s_comb = Slot("comb")
    s_m8 = Slot("m8")
    s_tt = Slot("tt")

    def router_tc(ti, tj, pRx, sl2):
        if True:
            tc = ti * GT + tj
            which = tj % 2
            sl = sl2[which]
            lo = 128 + which * 128
            emits = []
            for dc in range(DC):
                emits.append(lambda E, tc=tc, dc=dc, lo=lo, pRx=pRx:
                             E.matmul(pRx[:, lo:lo + 8],
                                      h2c(dc)[:, tc * 128:(tc + 1) * 128],
                                      rwT[:, dc * 8:(dc + 1) * 8],
                                      start=(dc == 0), stop=(dc == DC - 1)))
            p.mm_group(emits, reads=tuple(s_h2) + (s_rwT,), write=sl)
            pr = probs[:, tc * 8:(tc + 1) * 8]
            cb = comb[:, tc * 8:(tc + 1) * 8]
            p.op("scalar",
                 lambda E, pr=pr, lo=lo, pRx=pRx: E.activation(
                     pr, pRx[:, lo:lo + 8], ACT.Exp, scale=-1.0),
                 reads=(sl,), writes=(s_probs,))
            p.op("vector", lambda E, pr=pr: E.tensor_scalar_add(pr, pr, 1.0),
                 reads=(), writes=(s_probs,))
            p.op("vector", lambda E, pr=pr: E.reciprocal(pr, pr),
                 reads=(), writes=(s_probs,))
            p.op("vector", lambda E, pr=pr: E.max(out=m8[:], in_=pr),
                 reads=(s_probs,), writes=(s_m8,))
            p.op("vector", lambda E: E.tensor_tensor(tt1[:, 0:1], m8[:, 0:1],
                                                     m8[:, 1:2], ALU.add),
                 reads=(s_m8,), writes=(s_tt,))
            p.op("vector", lambda E: E.tensor_scalar_add(tt1[:, 0:1],
                                                         tt1[:, 0:1], 1e-20),
                 reads=(), writes=(s_tt,))
            p.op("vector", lambda E: E.reciprocal(tt1[:, 1:2], tt1[:, 0:1]),
                 reads=(), writes=(s_tt,))
            p.op("vector", lambda E: E.tensor_tensor(tt1[:, 2:3], m8[:, 0:1],
                                                     tt1[:, 1:2], ALU.mult),
                 reads=(s_m8,), writes=(s_tt,))
            p.op("vector", lambda E: E.tensor_tensor(tt1[:, 3:4], m8[:, 1:2],
                                                     tt1[:, 1:2], ALU.mult),
                 reads=(s_m8,), writes=(s_tt,))
            p.op("vector", lambda E, pr=pr: E.tensor_scalar(
                tt2[:], pr, m8[:, 0:1], None, ALU.is_equal),
                 reads=(s_probs, s_m8), writes=(s_tt,))
            p.op("vector", lambda E, cb=cb: E.tensor_scalar(
                cb, tt2[:], tt1[:, 2:3], None, ALU.mult),
                 reads=(s_tt,), writes=(s_comb,))
            p.op("vector", lambda E: E.tensor_scalar(tt3[:], tt2[:], -1.0, 1.0,
                                                     ALU.mult, ALU.add),
                 reads=(s_tt,), writes=(s_tt,))
            p.op("vector", lambda E, pr=pr: E.tensor_scalar(
                tt2[:], pr, m8[:, 1:2], None, ALU.is_equal),
                 reads=(s_probs, s_m8), writes=(s_tt,))
            p.op("vector", lambda E: E.tensor_scalar(tt2[:], tt2[:],
                                                     tt1[:, 3:4], None,
                                                     ALU.mult),
                 reads=(), writes=(s_tt,))
            p.op("vector", lambda E: E.tensor_tensor(tt2[:], tt2[:], tt3[:],
                                                     ALU.mult),
                 reads=(), writes=(s_tt,))
            p.op("vector", lambda E, cb=cb: E.tensor_tensor(cb, cb, tt2[:],
                                                            ALU.add),
                 reads=(s_tt,), writes=(s_comb,))

    def router_fin(ti, pqx, s_pqx):
        p.op("scalar", lambda E, ti=ti: E.activation(
            comb[:, ti * GT * 8:(ti + 1) * GT * 8],
            comb[:, ti * GT * 8:(ti + 1) * GT * 8], ACT.Sqrt),
             reads=(s_comb,), writes=(s_comb,))
        emits = []
        for tj in range(GT):
            tc = ti * GT + tj
            emits.append(lambda E, tc=tc, tj=tj, pqx=pqx:
                         E.matmul(pqx[0:8, tj * 128:(tj + 1) * 128],
                                  comb[:, tc * 8:(tc + 1) * 8], identr[:],
                                  start=True, stop=True))
        p.mm_group(emits, reads=(s_comb, s_identr), write=s_pqx)
        si = ti % 2
        p.op("vector", lambda E, si=si, pqx=pqx: E.tensor_copy(
            sqct[si][:, 0:FG], pqx[0:8, 0:FG]),
             reads=(s_pqx,), writes=(s_sqct[si],))
        p.dma("gpsimd", lambda E, ti=ti, si=si: E.dma_start(
            out=sqc_d[:, ti * FG:(ti + 1) * FG], in_=sqct[si][:, 0:FG]),
              reads=(s_sqct[si],), writes=(s_sqc[ti],))

    def router_ti(ti, pRx, sl2, pqx, s_pqx):
        for tj in range(GT):
            router_tc(ti, tj, pRx, sl2)
        router_fin(ti, pqx, s_pqx)

    pR_ctx = PSUM("pR", 128, 512)
    pR = pR_ctx.__enter__()
    s_pR = Slot("pR")
    s_pR2 = Slot("pR2")
    if BUILD_UPTO < 8.02:
        p.barrier()
        return _finish(nc, p, ar, dbg, out, identb, identr, s_identb, s_identr)
    for dc in range(DC):
        p.mm_group([lambda E, dc=dc: E.matmul(pR[:, 0:128],
                                              rw_b[:, dc * 128:(dc + 1) * 128],
                                              identb[:], start=True,
                                              stop=True)],
                   reads=(s_rwb, s_identb), write=s_pR)
        p.op("vector", lambda E, dc=dc: E.tensor_copy(rwT[:, dc * 8:(dc + 1) * 8],
                                                      pR[:, 0:8]),
             reads=(s_pR,), writes=(s_rwT,))
    if BUILD_UPTO < 8.1:
        p.barrier()
        return _finish(nc, p, ar, dbg, out, identb, identr, s_identb, s_identr)
    pq0_ctx = PSUM("pq0", 8, 512)
    pq0 = pq0_ctx.__enter__()
    s_pq0 = Slot("pq0")
    router_ti(0, pR, (s_pR, s_pR2), pq0, s_pq0)
    pG_ctx = PSUM("pG", 128, 1024)
    pG = pG_ctx.__enter__()
    s_pG = Slot("pG")
    emits = []
    for dc in range(DC):
        emits.append(lambda E, dc=dc:
                     E.matmul(pG[:, dc * 128:(dc + 1) * 128], onesrow_b[:],
                              gb_row(dc), start=True, stop=True))
    p.mm_group(emits, reads=tuple(s_rows[16:24]) + (s_ones,), write=s_pG)
    s_gbb = Slot("gbb")
    p.op("vector", lambda E: E.tensor_copy(gbb[:], pG[:]),
         reads=(s_pG,), writes=(s_gbb,))
    pG_ctx.__exit__(None, None, None)
    pq0_ctx.__exit__(None, None, None)
    pR_ctx.__exit__(None, None, None)
    p.barrier()
    ar.free("rw_b", slot_toks(s_rwb))
    if BUILD_UPTO < 9:
        return _finish(nc, p, ar, dbg, out, identb, identr, s_identb, s_identr)

    ghosts = ar.take_ghosts()
    h_all = ar.alloc("h_all", [128, NWC * FG], BF16)

    def hallw(wc):
        return h_all[:, wc * FG:(wc + 1) * FG]
    s_hall = [Slot(f"hall{wc}", ghosts) for wc in range(NWC)]
    cbt = [ar.alloc(f"cbt{i}", [128, FG], F32) for i in range(2)]
    s_cbt = [Slot(f"cbt{i}", ghosts) for i in range(2)]
    WG = 4
    w1t = [ar.alloc(f"w1t{i}", [128, DC * WG * 128], BF16) for i in range(2)]
    s_w1t = [Slot(f"w1t{i}", ghosts) for i in range(2)]
    htmp8 = [ar.alloc(f"htmp8{i}", [128, FG], BF16) for i in range(2)]
    s_htmp8 = [Slot(f"htmp8{i}", ghosts) for i in range(2)]
    WQ = 2
    w2g = [ar.alloc(f"w2g{i}", [128, WQ * D], BF16) for i in range(2)]
    s_w2g = [Slot(f"w2g{i}", ghosts) for i in range(2)]
    x1t = [ar.alloc(f"x1t{i}", [128, D], F32) for i in range(2)]
    s_x1t = [Slot(f"x1t{i}", ghosts + tok_x1) for i in range(2)]
    ost = [ar.alloc(f"ost{i}", [128, D], F32) for i in range(2)]
    s_ost = [Slot(f"ost{i}", ghosts) for i in range(2)]
    final = []
    NWG = NWC // WG
    for ti in range(NTI):
        t0 = ti * FG
        pH_ctx = [PSUM(f"pH{ti}_{i}", 128, FG) for i in range(2)]
        pH = [c.__enter__() for c in pH_ctx]
        s_pH = [Slot(f"pH{i}") for i in range(2)]
        pRt_ctx = PSUM(f"pRt{ti}", 128, 512)
        pRt = pRt_ctx.__enter__()
        s_pRa, s_pRb = Slot("pRa"), Slot("pRb")
        pqt_ctx = PSUM(f"pqt{ti}", 8, 512)
        pqt = pqt_ctx.__enter__()
        s_pqt = Slot("pqt")
        for e in range(2):
            p.dma("gpsimd",
                  lambda E, e=e, t0=t0: E.dma_start(
                      out=cbt[e][:].unsqueeze(1),
                      in_=sqc_d[e:e + 1, t0:t0 + FG].partition_broadcast(128)),
                  reads=(s_sqc[ti],), writes=(s_cbt[e],))
        WGW = WG * 128
        if ti == 0:
            for g in range(min(2, NWG)):
                p.dma("gpsimd",
                      lambda E, g=g: E.dma_start(
                          out=w1t[g % 2][:, 0:DC * WGW],
                          in_=w1bf[:, g * 4096:(g + 1) * 4096]),
                      reads=(s_w1bf[g],), writes=(s_w1t[g % 2],))
        for q2 in range(min(2, NWC // WQ)):
            p.dma("scalar",
                  lambda E, q2=q2: E.dma_start(
                      out=w2g[q2 % 2][:],
                      in_=w2bf[:, q2 * WQ * D:(q2 + 1) * WQ * D]),
                  reads=(s_w2bf[(q2 * WQ) // 4],), writes=(s_w2g[q2 % 2],))
        for wc in range(NWC):
            e = WC_EXPERT[wc]
            g, wi = wc // WG, wc % WG
            i = wc % 2
            emits = []
            for dc in range(DC):
                emits.append(
                    lambda E, dc=dc, i=i, g=g, wi=wi, t0=t0, pH=pH, WGW=WGW:
                    E.matmul(pH[i][:, 0:FG],
                             w1t[g % 2][:, dc * WGW + wi * 128:dc * WGW + (wi + 1) * 128],
                             h2c(dc)[:, t0:t0 + FG],
                             start=(dc == 0), stop=(dc == DC - 1)))
            p.mm_group(emits, reads=tuple(s_h2) + (s_w1t[g % 2],), write=s_pH[i])
            p.op("vector",
                 lambda E, i=i, e=e, pH=pH: E.scalar_tensor_tensor(
                     htmp8[i][:, 0:FG], pH[i][:, 0:FG], 0.0, cbt[e % 2][:, 0:FG],
                     ALU.max, ALU.mult),
                 reads=(s_pH[i], s_cbt[e % 2]), writes=(s_htmp8[i],))
            p.op("scalar",
                 lambda E, i=i, wc=wc: E.activation(hallw(wc), htmp8[i][:, 0:FG],
                                                    ACT.Square),
                 reads=(s_htmp8[i],), writes=(s_hall[wc],))
            if wi == WG - 1 and g + 2 < NWG:
                p.dma("gpsimd",
                      lambda E, g=g: E.dma_start(
                          out=w1t[g % 2][:, 0:DC * WGW],
                          in_=w1bf[:, (g + 2) * 4096:(g + 3) * 4096]),
                      reads=(s_w1bf[g + 2],), writes=(s_w1t[g % 2],))
            if wc + 1 < NWC and WC_EXPERT[wc + 1] == e + 1 and e + 2 < 8:
                p.dma("gpsimd",
                      lambda E, e=e, t0=t0: E.dma_start(
                          out=cbt[e % 2][:].unsqueeze(1),
                          in_=sqc_d[e + 2:e + 3, t0:t0 + FG].partition_broadcast(128)),
                      reads=(s_sqc[ti],), writes=(s_cbt[e % 2],))
            if ti + 1 < NTI:
                if wc >= 11 and wc < 11 + 8 * GT and (wc - 11) % 8 == 0:
                    router_tc(ti + 1, (wc - 11) // 8, pRt, (s_pRa, s_pRb))
                elif wc == 11 + 8 * GT:
                    router_fin(ti + 1, pqt, s_pqt)
        pqt_ctx.__exit__(None, None, None)
        pRt_ctx.__exit__(None, None, None)
        for c in reversed(pH_ctx):
            c.__exit__(None, None, None)
        pY_ctx = [PSUM(f"pY{ti}_{i}", 128, 1024)
                  for i in range(GT)]
        pY = [c.__enter__() for c in pY_ctx]
        s_pY = [Slot(f"pY{i}") for i in range(GT)]
        pyw = {0: (s_pH[0], s_pH[1]), 1: (s_pRa, s_pRb, s_pqt)}
        for tj in range(min(2, GT)):
            tc = ti * GT + tj
            p.dma("sync",
                  lambda E, tc=tc, tj=tj: E.dma_start(
                      out=x1t[tj % 2][:], in_=x1tm_d[tc * 128:(tc + 1) * 128, :]),
                  writes=(s_x1t[tj % 2],))
        for wc in range(NWC):
            q2, wj = wc // WQ, wc % WQ
            for tj in range(GT):
                emits = []
                for j0, jl in fslices(D):
                    emits.append(
                        lambda E, wc=wc, tj=tj, j0=j0, jl=jl, q2=q2, wj=wj, pY=pY:
                        E.matmul(pY[tj][:, j0:j0 + jl],
                                 hallw(wc)[:, tj * 128:(tj + 1) * 128],
                                 w2g[q2 % 2][:, wj * D + j0:wj * D + j0 + jl],
                                 start=(wc == 0), stop=(wc == NWC - 1)))
                rd = (s_hall[wc], s_w2g[q2 % 2])
                w0 = (s_pY[tj],) + pyw.get(tj, ()) if wc == 0 else ()
                if wc == NWC - 1:
                    p.mm_group(emits, reads=rd, write=s_pY[tj])
                else:
                    for e_ in emits[:-1]:
                        p.op("tensor", e_, reads=rd, war=w0, inc=False)
                    p.op("tensor", emits[-1], reads=rd, war=w0, inc=True)
            if wj == WQ - 1 and (q2 + 2) * WQ < NWC:
                p.dma("scalar",
                      lambda E, q2=q2: E.dma_start(
                          out=w2g[q2 % 2][:],
                          in_=w2bf[:, (q2 + 2) * WQ * D:(q2 + 3) * WQ * D]),
                      reads=(s_w2bf[((q2 + 2) * WQ) // 4],),
                      writes=(s_w2g[q2 % 2],))
            if wc == 48 and ti + 1 < NTI:
                for g in range(min(2, NWG)):
                    p.dma("gpsimd",
                          lambda E, g=g: E.dma_start(
                              out=w1t[g % 2][:, 0:DC * WGW],
                              in_=w1bf[:, g * 4096:(g + 1) * 4096]),
                          reads=(s_w1bf[g],), writes=(s_w1t[g % 2],))
        for tj in range(GT):
            tc = ti * GT + tj
            oi = tj % 2
            p.op("vector",
                 lambda E, tj=tj, oi=oi, pY=pY: E.tensor_tensor(ost[oi][:], pY[tj][:],
                                                                gbb[:], ALU.mult),
                 reads=(s_pY[tj], s_gbb), writes=(s_ost[oi],))
            p.op("vector",
                 lambda E, oi=oi: E.tensor_tensor(ost[oi][:], ost[oi][:],
                                                  x1t[oi][:], ALU.add),
                 reads=(s_x1t[oi],), writes=(s_ost[oi],))
            final.append(p.dma("sync",
                               lambda E, tc=tc, oi=oi: E.dma_start(
                                   out=out[tc * 128:(tc + 1) * 128, :],
                                   in_=ost[oi][:]),
                               reads=(s_ost[oi],)))
            if tj + 2 < GT:
                tc2 = ti * GT + tj + 2
                p.dma("sync",
                      lambda E, tc2=tc2, oi=oi: E.dma_start(
                          out=x1t[oi][:],
                          in_=x1tm_d[tc2 * 128:(tc2 + 1) * 128, :]),
                      writes=(s_x1t[oi],))
        for c in reversed(pY_ctx):
            c.__exit__(None, None, None)
        p.barrier()
    p.finalize(final)
    global LAST_AR
    LAST_AR = ar
    with nc.Block() as block:
        with nc.allow_low_precision(reason="f32r/bf16 by design"):
            p.run_sections(block)
    return nc


LAST_AR = None


def _finish(nc, p, ar, dbg, out, identb, identr, s_identb, s_identr):
    slots, chunkf = dbg[DEV_TAP]
    ghosts = ar.take_ghosts()
    outst = [ar.alloc(f"outst{i}", [128, D], F32) for i in range(2)]
    s_outst = [Slot(f"outst{i}", ghosts) for i in range(2)]
    final = []
    pO_ctx = nc.psum_tensor("pO", [128, 1024], F32)
    pO = pO_ctx.__enter__()
    s_pO = Slot("pO")
    use_b = chunkf(0).dtype == BF16
    iden, s_iden = (identb, s_identb) if use_b else (identr, s_identr)
    for tc in range(TC):
        i = tc % 2
        emits = []
        for dc in range(DC):
            emits.append(lambda E, dc=dc, tc=tc, iden=iden:
                         E.matmul(pO[:, dc * 128:(dc + 1) * 128],
                                  chunkf(dc)[:, tc * 128:(tc + 1) * 128],
                                  iden[:], start=True, stop=True))
        p.mm_group(emits, reads=tuple(slots) + (s_iden,), write=s_pO)
        p.op("vector", lambda E, i=i: E.tensor_copy(outst[i][:], pO[:]),
             reads=(s_pO,), writes=(s_outst[i],))
        final.append(
            p.dma("sync",
                  lambda E, tc=tc, i=i: E.dma_start(
                      out=out[tc * 128:(tc + 1) * 128, :], in_=outst[i][:]),
                  reads=(s_outst[i],)))
    pO_ctx.__exit__(None, None, None)
    p.finalize(final)
    with nc.Block() as block:
        with nc.allow_low_precision(reason="f32r/bf16 by design"):
            p.run_sections(block)
    return nc


def _consts():
    return {
        "ident_b": np.eye(128, dtype=ml_dtypes.bfloat16),
        "ident_r": np.eye(128, dtype=np.float32),
        "eyerows": np.eye(16, dtype=np.float32).reshape(1, 256),
        "selpat": _selpat(),
    }


def _selpat():
    s = np.zeros((16, DC * 128), dtype=ml_dtypes.bfloat16)
    for h in range(16):
        s[h, (h // 2) * 128 + (h % 2) * 64:(h // 2) * 128 + (h % 2) * 64 + 64] = 1
    return s


def make_in_maps(L0_attn_x, L0_attn_qkv_w, L0_attn_proj_w, L0_attn_proj_b,
                 L0_attn_q_norm_w, L0_attn_k_norm_w, L0_ffn_router_w, L0_ffn_w1,
                 L0_ffn_w2, L0_c, L0_wmod):
    x = np.ascontiguousarray(np.asarray(L0_attn_x, dtype=np.float32))
    shared = {
        "qkv_w": np.ascontiguousarray(np.asarray(L0_attn_qkv_w, np.float32)),
        "proj_w": np.ascontiguousarray(np.asarray(L0_attn_proj_w, np.float32)),
        "proj_b": np.asarray(L0_attn_proj_b, np.float32).reshape(1, D),
        "q_norm_w": np.asarray(L0_attn_q_norm_w, np.float32).reshape(1, DH),
        "k_norm_w": np.asarray(L0_attn_k_norm_w, np.float32).reshape(1, DH),
        "router_w": np.ascontiguousarray(np.asarray(L0_ffn_router_w, np.float32)),
        "w1s": np.ascontiguousarray(
            np.asarray(L0_ffn_w1, np.float32).reshape(8, 128, 24, 4, 128)
            .transpose(1, 2, 0, 3, 4).reshape(128, 24 * 4096)),
        "w2s": np.ascontiguousarray(
            np.asarray(L0_ffn_w2, np.float32).reshape(96, 128, 1024)
            .transpose(1, 0, 2).reshape(128, 96 * 1024)),
        "wmod": np.ascontiguousarray(np.asarray(L0_wmod, np.float32)),
    }
    shared.update(_consts())
    carr = np.asarray(L0_c, np.float32)
    in_maps = []
    for b in range(x.shape[0]):
        m = dict(shared)
        m["x"] = np.ascontiguousarray(x[b][:T])
        m["c_row"] = np.ascontiguousarray(carr[b:b + 1])
        in_maps.append(m)
    return in_maps


def kernel(**inputs):
    in_maps = make_in_maps(**inputs)
    nc = build_program()
    res = run_bass_kernel_spmd(nc, in_maps, core_ids=list(range(len(in_maps))))
    return np.stack([np.asarray(r["out"], np.float32) for r in res.results], axis=0)


# version 18
# speedup vs baseline: 1.0196x; 1.0196x over previous
import numpy as np
import ml_dtypes
import concourse.bass as bass
import concourse.mybir as mybir
from concourse.bass_utils import run_bass_kernel_spmd

F32 = mybir.dt.float32
F32R = mybir.dt.float32r
BF16 = mybir.dt.bfloat16
ALU = mybir.AluOpType
ACT = mybir.ActivationFunctionType
AX = mybir.AxisListType.X

NCORES = 8
D = 1024
H = 16
DH = 64
DC = D // 128
WIDTHS = [1024] * 4 + [2048] * 4
TOTW = sum(WIDTHS)
NWC = TOTW // 128
WC_EXPERT = [e for e in range(8) for _ in range(WIDTHS[e] // 128)]
LN_EPS = 1e-5
RMS_EPS = 1e-6

T = 2048
TC = T // 128
FG = 512

BUILD_UPTO = 99
DEV_TAP = None
SPW_CAP = 1024


def set_T(t):
    global T, TC, FG
    T = t
    TC = T // 128
    FG = min(512, T)


class Slot:
    __slots__ = ("w", "rs", "name")

    def __init__(self, name="", ghosts=()):
        self.w = None
        self.rs = list(ghosts)
        self.name = name


class Prog:
    CENG = ("tensor", "vector", "scalar", "gpsimd", "sync")

    def __init__(self, nc, dma_pool=14):
        self.nc = nc
        self.instrs = {e: [] for e in self.CENG}
        self._sem_ctxs = []
        self.sems = {}
        for e in self.CENG:
            self._mksem("e_" + e)
        self.dma_sems = {}
        for q in ("sync", "gpsimd", "scalar"):
            names = []
            for i in range(dma_pool):
                n = f"d_{q}{i}"
                self._mksem(n)
                names.append(n)
            self.dma_sems[q] = names
        self.cnt = {e: 0 for e in self.CENG}
        self.wm = {e: {} for e in self.CENG}
        self.dcnt = {n: 0 for q in self.dma_sems for n in self.dma_sems[q]}
        self.dwaited = {n: 0 for n in self.dcnt}
        self.dlive = {n: [] for n in self.dcnt}
        self.dorder = {n: [] for n in self.dcnt}
        self.drr = {q: 0 for q in self.dma_sems}
        self.erun = {e: [] for e in self.CENG}
        self.econs = {e: 0 for e in self.CENG}

    def _mksem(self, name):
        ctx = self.nc.semaphore(name)
        h = ctx.__enter__()
        self._sem_ctxs.append(ctx)
        self.sems[name] = h

    def _wait(self, eng, tok, waited_dma=None):
        sem, val = tok
        if self.wm[eng].get(sem, 0) >= val:
            return
        if sem in self.dcnt:
            assert val == self.dcnt[sem], (
                f"dma wait on {sem} for {val} but counter at {self.dcnt[sem]}")
            self.dwaited[sem] = max(self.dwaited[sem], val)
            if waited_dma is not None:
                waited_dma.append(sem)
        else:
            eng2 = sem[2:]
            if eng2 in self.econs:
                self.econs[eng2] = max(self.econs[eng2], val)
        self.wm[eng][sem] = val
        self.erun[eng] = []
        h = self.sems[sem]
        self.instrs[eng].append(lambda E, h=h, v=val: E.wait_ge(h, v))

    def _pre_inc(self, eng):
        run = self.erun[eng]
        if run and self.econs[eng] >= run[0][1]:
            h = self.sems["e_" + eng]
            v = self.cnt[eng]
            self.instrs[eng].append(lambda E, h=h, v=v: E.wait_ge(h, v))
            self.wm[eng]["e_" + eng] = v
            self.erun[eng] = []

    def _deps(self, reads, writes, war=()):
        deps = []
        for s in reads:
            if s.w is not None:
                deps.append(s.w)
        for s in tuple(writes) + tuple(war):
            deps.extend(s.rs)
            if s.w is not None:
                deps.append(s.w)
        best = {}
        for t in deps:
            if t[0] not in best or best[t[0]][1] < t[1]:
                best[t[0]] = t
        return list(best.values())

    def op(self, eng, emit, reads=(), writes=(), war=(), inc=True):
        wd = []
        for tok in self._deps(reads, writes, war):
            self._wait(eng, tok, waited_dma=wd)
        sem = "e_" + eng
        if inc:
            self._pre_inc(eng)
            self.cnt[eng] += 1
            for t in self.erun[eng]:
                t[1] = self.cnt[eng]
            tok = [sem, self.cnt[eng]]
            self.erun[eng].append(tok)
            h = self.sems[sem]
            self.instrs[eng].append(lambda E, f=emit, h=h: f(E).then_inc(h, 1))
        else:
            tok = [sem, self.cnt[eng]]
            self.instrs[eng].append(lambda E, f=emit: f(E))
        for s in reads:
            s.rs.append(tok)
        for s in writes:
            s.w = tok
            s.rs = []
        for dsem in wd:
            self.dorder[dsem].append(tok)
        return tok

    def mm_group(self, emits, reads=(), write=None, war=()):
        writes = (write,) if write is not None else ()
        wd = []
        for tok in self._deps(reads, writes, war):
            self._wait("tensor", tok, waited_dma=wd)
        sem = "e_tensor"
        h = self.sems[sem]
        for f in emits[:-1]:
            self.instrs["tensor"].append(lambda E, f=f: f(E))
        self._pre_inc("tensor")
        self.cnt["tensor"] += 1
        for t in self.erun["tensor"]:
            t[1] = self.cnt["tensor"]
        tok = [sem, self.cnt["tensor"]]
        self.erun["tensor"].append(tok)
        f = emits[-1]
        self.instrs["tensor"].append(lambda E, f=f, h=h: f(E).then_inc(h, 1))
        for s in reads:
            s.rs.append(tok)
        if write is not None:
            write.w = tok
            write.rs = []
        for dsem in wd:
            self.dorder[dsem].append(tok)
        return tok

    def dma(self, queue, emit, reads=(), writes=(), war=()):
        wd = []
        for tok in self._deps(reads, writes, war):
            self._wait(queue, tok, waited_dma=wd)
        pool = self.dma_sems[queue]
        for _ in range(len(pool)):
            n = pool[self.drr[queue] % len(pool)]
            self.drr[queue] += 1
            if self.dwaited[n] >= self.dcnt[n]:
                break
        else:
            n = pool[self.drr[queue] % len(pool)]
            self.drr[queue] += 1
        if self.dcnt[n] > 0:
            for t in list(self.dorder[n]):
                self._wait(queue, t, waited_dma=wd)
            self.dorder[n] = []
        self.dcnt[n] += 16
        for t in self.dlive[n]:
            t[1] = self.dcnt[n]
        tok = [n, self.dcnt[n]]
        self.dlive[n].append(tok)
        h = self.sems[n]
        self.instrs[queue].append(lambda E, f=emit, h=h: f(E).then_inc(h, 16))
        for s in reads:
            s.rs.append(tok)
        for s in writes:
            s.w = tok
            s.rs = []
        for dsem in wd:
            self.dorder[dsem].append(tok)
        return tok

    def barrier(self, engines=("tensor", "vector", "scalar")):
        toks = []
        for e in engines:
            self._pre_inc(e)
            self.cnt[e] += 1
            for t in self.erun[e]:
                t[1] = self.cnt[e]
            h = self.sems["e_" + e]

            def mk(E, h=h):
                ins = E.engine_nop() if hasattr(E, "engine_nop") else E.nop()
                return ins.then_inc(h, 1)
            self.instrs[e].append(mk)
            tok = ["e_" + e, self.cnt[e]]
            self.erun[e].append(tok)
            toks.append(tok)
        for e in engines:
            for tok in toks:
                self._wait(e, tok)

    def finalize(self, final_tokens):
        for tok in final_tokens:
            self._wait("sync", tok)

    def run_sections(self, block):
        for e in self.CENG:
            lst = self.instrs[e]

            def runner(E, lst=lst):
                for f in lst:
                    f(E)
            getattr(block, e)(runner)


def fslices(total, step=512):
    return [(i, min(step, total - i)) for i in range(0, total, step)]


class Arena:

    def __init__(self, nc, kib):
        self.t = nc.alloc_sbuf_tensor("arena", [128, kib * 1024], mybir.dt.uint8)
        self.free_list = [[0, kib * 1024]]
        self.regions = {}
        self.ghosts = []

    def alloc(self, name, shape, dtype, end=False):
        esz = {F32: 4, F32R: 4, BF16: 2}[dtype]
        size = (shape[1] * esz + 31) // 32 * 32
        frs = self.free_list if not end else list(reversed(self.free_list))
        for fr in frs:
            if fr[1] >= size:
                if end:
                    off = fr[0] + fr[1] - size
                    fr[1] -= size
                else:
                    off = fr[0]
                    fr[0] += size
                    fr[1] -= size
                break
        else:
            raise MemoryError(f"arena: no room for {name} ({size}B) "
                              f"free={self.free_list} regions={self.regions}")
        self.free_list = [f for f in self.free_list if f[1] > 0]
        self.regions[name] = (off, size)
        return self.t[:][0:shape[0], off:off + shape[1] * esz].bitcast(dtype)

    def free(self, name, tokens=()):
        off, size = self.regions.pop(name)
        self.free_list.append([off, size])
        self.free_list.sort()
        merged = []
        for fr in self.free_list:
            if merged and merged[-1][0] + merged[-1][1] == fr[0]:
                merged[-1][1] += fr[1]
            else:
                merged.append(fr)
        self.free_list = merged
        self.ghosts.extend(t for t in tokens if t is not None)

    def take_ghosts(self):
        return list(self.ghosts)


def slot_toks(*slots):
    out = []
    for s in slots:
        if s.w is not None:
            out.append(s.w)
        out.extend(s.rs)
    return out


def build_program():
    nc = bass.Bass("TRN2", target_bir_lowering=False)

    def PSUM(name, rows, cols):
        return nc.psum_tensor(name, [rows, (cols + 511) // 512 * 512], F32)
    p = Prog(nc)
    A = nc.alloc_sbuf_tensor
    ar = Arena(nc, 176)
    dbg = {}

    x_in = nc.dram_tensor("x", [T, D], F32, kind="ExternalInput").ap()
    qkv_w = nc.dram_tensor("qkv_w", [3 * D, D], F32, kind="ExternalInput").ap()
    proj_w = nc.dram_tensor("proj_w", [D, D], F32, kind="ExternalInput").ap()
    proj_b = nc.dram_tensor("proj_b", [1, D], F32, kind="ExternalInput").ap()
    q_norm_w = nc.dram_tensor("q_norm_w", [1, DH], F32, kind="ExternalInput").ap()
    k_norm_w = nc.dram_tensor("k_norm_w", [1, DH], F32, kind="ExternalInput").ap()
    router_w = nc.dram_tensor("router_w", [8, D], F32, kind="ExternalInput").ap()
    w1s = nc.dram_tensor("w1s", [128, 24 * 4096], F32, kind="ExternalInput").ap()
    w2s = nc.dram_tensor("w2s", [128, 96 * 1024], F32, kind="ExternalInput").ap()
    w1bf = nc.dram_tensor("w1bf", [128, 24 * 4096], BF16, kind="Internal").ap()
    w2bf = nc.dram_tensor("w2bf", [128, 96 * 1024], BF16, kind="Internal").ap()
    c_row = nc.dram_tensor("c_row", [1, D], F32, kind="ExternalInput").ap()
    wmod = nc.dram_tensor("wmod", [D, 6 * D], F32, kind="ExternalInput").ap()
    ident_b = nc.dram_tensor("ident_b", [128, 128], BF16, kind="ExternalInput").ap()
    ident_r = nc.dram_tensor("ident_r", [128, 128], F32, kind="ExternalInput").ap()
    eyerows = nc.dram_tensor("eyerows", [1, 256], F32, kind="ExternalInput").ap()
    selpat = nc.dram_tensor("selpat", [16, DC * 128], BF16,
                            kind="ExternalInput").ap()
    out = nc.dram_tensor("out", [T, D], F32, kind="ExternalOutput").ap()
    xfm_d = nc.dram_tensor("xfm_d", [128, DC * T], F32, kind="Internal").ap()
    x1tm_d = nc.dram_tensor("x1tm_d", [T, D], F32, kind="Internal").ap()
    sqc_d = nc.dram_tensor("sqc_d", [8, T], F32, kind="Internal").ap()

    identb = A("identb", [128, 128], BF16)
    identr = A("identr", [128, 128], F32)
    eyer = A("eyer", [1, 256], F32)
    selp = A("selp", [16, DC * 128], BF16)
    onesrow_r = A("onesrow_r", [1, 128], F32)
    onesrow_b = A("onesrow_b", [1, 128], BF16)
    onescol_b2 = A("onescol_b2", [128, 1], BF16)
    onescol_r2 = A("onescol_r2", [128, 1], F32)
    mod_fm = A("mod_fm", [128, 64], F32)
    mod_acc = A("mod_acc", [128, 48], F32)
    modfm_r = A("modfm_r", [128, 64], F32)
    modT = A("modT", [64, 128], BF16)
    rows_r = A("rows_r", [1, 24 * 128], BF16)
    sgn = A("sgn", [128, 16], F32)
    silc = A("silc", [128, DC], F32)
    wqpat = A("wqpat", [16, 1024], BF16)
    wkpat = A("wkpat", [16, 1024], BF16)
    sqpat = A("sqpat", [128, 16 * DC], BF16)
    wqsq = A("wqsq", [128, 16 * DC], BF16)
    wksq = A("wksq", [128, 16 * DC], BF16)
    bias_t = A("bias_t", [128, 16], F32)
    cvec = A("cvec", [16, 4], F32)
    crow = A("crow", [1, 16], F32)
    rwT = A("rwT", [128, 64], BF16)
    probs = A("probs", [128, 8 * TC], F32)
    comb = A("comb", [128, 8 * TC], F32)
    m8 = A("m8", [128, 8], F32)
    tt1 = A("tt1", [128, 8], F32)
    tt2 = A("tt2", [128, 8], F32)
    tt3 = A("tt3", [128, 8], F32)
    gbb = A("gbb", [128, D], F32)

    def modc(v, dc):
        return mod_fm[:, dc * 8 + v: dc * 8 + v + 1]

    def caplus_row(dc):
        return rows_r[:, (0 * 8 + dc) * 128:(0 * 8 + dc + 1) * 128]

    def cbplus_row(dc):
        return rows_r[:, (1 * 8 + dc) * 128:(1 * 8 + dc + 1) * 128]

    def gb_row(dc):
        return rows_r[:, (2 * 8 + dc) * 128:(2 * 8 + dc + 1) * 128]

    s_identb, s_identr, s_eyer = Slot("identb"), Slot("identr"), Slot("eyer")
    s_ones = Slot("ones")
    p.dma("sync", lambda E: E.dma_start(out=identb[:], in_=ident_b), writes=(s_identb,))
    p.dma("sync", lambda E: E.dma_start(out=identr[:], in_=ident_r), writes=(s_identr,))
    p.dma("sync", lambda E: E.dma_start(out=eyer[:], in_=eyerows), writes=(s_eyer,))
    s_selp = Slot("selp")
    p.dma("sync", lambda E: E.dma_start(out=selp[:], in_=selpat), writes=(s_selp,))
    p.op("vector", lambda E: E.memset(onesrow_r[:], 1.0), writes=(s_ones,))
    p.op("vector", lambda E: E.memset(onesrow_b[:], 1.0), writes=(s_ones,))
    p.op("vector", lambda E: E.memset(onescol_b2[:], 1.0), writes=(s_ones,))
    p.op("vector", lambda E: E.memset(onescol_r2[:], 1.0), writes=(s_ones,))

    x_fm = ar.alloc("x_fm", [128, DC * T], F32)

    def xfm(dc):
        return x_fm[:, dc * T:(dc + 1) * T]
    s_xfm = [Slot(f"x_fm{dc}") for dc in range(DC)]

    xtm_st = [ar.alloc(f"xtm_st{i}", [128, D], F32) for i in range(3)]
    s_xtm = [Slot(f"xtm{i}") for i in range(3)]
    pT_ctx = [PSUM(f"pT{i}", 128, 512) for i in range(2)]
    pT = [c.__enter__() for c in pT_ctx]
    s_pT = [Slot(f"pT{i}") for i in range(2)]

    for tc in range(TC):
        i = tc % 3
        p.dma("sync",
              lambda E, tc=tc, i=i: E.dma_start(
                  out=xtm_st[i][:], in_=x_in[tc * 128:(tc + 1) * 128, :]),
              writes=(s_xtm[i],))
        for g in range(2):
            pj = (tc * 2 + g) % 2
            emits = []
            for k in range(4):
                dc = g * 4 + k
                emits.append(lambda E, i=i, dc=dc, pj=pj, k=k:
                             E.matmul(pT[pj][:, k * 128:(k + 1) * 128],
                                      xtm_st[i][:, dc * 128:(dc + 1) * 128],
                                      identr[:], start=True, stop=True))
            p.mm_group(emits, reads=(s_xtm[i], s_identr), write=s_pT[pj])

            def ev(E, pj=pj, g=g, tc=tc, xf=x_fm):
                return E.tensor_copy(
                    xf[:].rearrange("p (c t) -> p c t", c=DC)[
                        :, g * 4:(g + 1) * 4, tc * 128:(tc + 1) * 128],
                    pT[pj][:].rearrange("p (c t) -> p c t", c=4))
            p.op("vector", ev, reads=(s_pT[pj],),
                 writes=tuple(s_xfm[g * 4 + k] for k in range(4)))
    for c in reversed(pT_ctx):
        c.__exit__(None, None, None)
    p.barrier()
    dbg["x_fm"] = (s_xfm, xfm)

    s_modfm = Slot("mod_fm")
    p.op("vector", lambda E: E.memset(mod_fm[:], 0.0), writes=(s_modfm,))
    c_sb = ar.alloc("c_sb", [1, D], F32)
    csil = ar.alloc("csil", [1, D], F32)
    pb_sb = ar.alloc("pb_sb", [1, D], F32)
    s_c = Slot("c", ar.take_ghosts())
    p.dma("sync", lambda E: E.dma_start(out=c_sb[:], in_=c_row), writes=(s_c,))
    s_csil = Slot("csil", ar.take_ghosts())
    p.op("scalar", lambda E: E.activation(csil[:], c_sb[:], ACT.Silu),
         reads=(s_c,), writes=(s_csil,))
    s_silc = Slot("silc")
    pM_ctx = PSUM("pM", 128, 128)
    pM = pM_ctx.__enter__()
    s_pM = Slot("pM")
    emits = [
        (lambda E, dc=dc: E.matmul(pM[:, dc:dc + 1],
                                   csil[:, dc * 128:(dc + 1) * 128],
                                   onesrow_r[:, 0:1], start=True, stop=True))
        for dc in range(DC)]
    p.mm_group(emits, reads=(s_csil, s_ones), write=s_pM)
    p.op("vector", lambda E: E.tensor_copy(silc[:], pM[:, 0:DC]),
         reads=(s_pM,), writes=(s_silc,))

    wmod_st = [ar.alloc(f"wmod_st{i}", [128, 6 * D], F32) for i in range(2)]
    s_wmod = [Slot(f"wmod{i}") for i in range(2)]
    s_macc = Slot("mod_acc")
    pMD_ctx = PSUM("pMD", 128, 64)
    pMD = pMD_ctx.__enter__()
    s_pMD = Slot("pMD")
    for dci in range(DC):
        i = dci % 2
        p.dma("sync",
              lambda E, dci=dci, i=i: E.dma_start(
                  out=wmod_st[i][:], in_=wmod[dci * 128:(dci + 1) * 128, :]),
              writes=(s_wmod[i],))
        emits = []
        for v in range(6):
            for dc in range(DC):
                emits.append(lambda E, i=i, dci=dci, v=v, dc=dc:
                             E.matmul(pMD[:, dc * 8 + v:dc * 8 + v + 1],
                                      wmod_st[i][:, v * D + dc * 128:v * D + (dc + 1) * 128],
                                      silc[:, dci:dci + 1],
                                      start=True, stop=True))
        p.mm_group(emits, reads=(s_silc, s_wmod[i]), write=s_pMD)
        idx = pMD[:, 0:64].rearrange("p (c v) -> p c v", c=DC)[:, :, 0:6]
        dst = mod_acc[:].rearrange("p (c v) -> p c v", c=DC, v=6)
        if dci == 0:
            p.op("vector", lambda E, idx=idx, dst=dst: E.tensor_copy(dst, idx),
                 reads=(s_pMD,), writes=(s_macc,))
        else:
            p.op("vector", lambda E, idx=idx, dst=dst:
                 E.tensor_tensor(dst, dst, idx, ALU.add),
                 reads=(s_pMD,), writes=(s_macc,))
    ar.free("wmod_st0", slot_toks(s_wmod[0]))
    ar.free("wmod_st1", slot_toks(s_wmod[1]))

    s_pb = Slot("pb", ar.take_ghosts())
    p.dma("sync", lambda E: E.dma_start(out=pb_sb[:], in_=proj_b), writes=(s_pb,))
    p.op("vector",
         lambda E: E.tensor_copy(
             mod_fm[:].rearrange("p (c v) -> p c v", c=DC)[:, :, 0:6],
             mod_acc[:].rearrange("p (c v) -> p c v", c=DC, v=6)),
         reads=(s_macc,), writes=(s_modfm,))
    emits = [(lambda E, dc=dc: E.matmul(pMD[:, dc * 8 + 6:dc * 8 + 7],
                                        pb_sb[:, dc * 128:(dc + 1) * 128],
                                        onesrow_r[:, 0:1], start=True, stop=True))
             for dc in range(DC)]
    p.mm_group(emits, reads=(s_pb, s_ones), write=s_pMD)
    p.op("vector",
         lambda E: E.tensor_copy(
             mod_fm[:].rearrange("p (c v) -> p c v", c=DC)[:, :, 6:7],
             pMD[:, 0:64].rearrange("p (c v) -> p c v", c=DC)[:, :, 6:7]),
         reads=(s_pMD,), writes=(s_modfm,))

    s_sgn = Slot("sgn")

    def sgnv(j):
        return sgn[:, j * 8:(j + 1) * 8].rearrange("p (a b) -> p a b", b=1)
    for j, v in enumerate((2, 5)):
        view = mod_fm[:].rearrange("p (c v) -> p c v", c=DC)[:, :, v:v + 1]
        p.op("vector",
             lambda E, j=j, view=view:
             E.tensor_scalar(sgnv(j), view, 0.0, 2.0, ALU.is_ge, ALU.mult),
             reads=(s_modfm,), writes=(s_sgn,))
        p.op("vector",
             lambda E, j=j: E.tensor_scalar_add(sgnv(j), sgnv(j), -1.0),
             reads=(), writes=(s_sgn,))
        p.op("vector",
             lambda E, j=j, view=view:
             E.tensor_tensor(view, view, sgnv(j), ALU.mult),
             reads=(s_sgn,), writes=(s_modfm,))
        p.op("vector",
             lambda E, view=view:
             E.tensor_scalar_max(view, view, 0.25),
             reads=(), writes=(s_modfm,))
        p.op("vector",
             lambda E, j=j, view=view:
             E.tensor_tensor(view, view, sgnv(j), ALU.mult),
             reads=(s_sgn,), writes=(s_modfm,))
    for v in (1, 4):
        view = mod_fm[:].rearrange("p (c v) -> p c v", c=DC)[:, :, v:v + 1]
        p.op("vector",
             lambda E, view=view: E.tensor_scalar_add(view, view, 1.0),
             reads=(), writes=(s_modfm,))

    s_modT = Slot("modT")
    s_modfm_r = Slot("modfm_r")
    p.op("vector", lambda E: E.tensor_copy(modfm_r[:], mod_fm[:]),
         reads=(s_modfm,), writes=(s_modfm_r,))
    p.mm_group([lambda E: E.matmul(pM[0:64, 0:128], modfm_r[:], identr[:],
                                   start=True, stop=True)],
               reads=(s_modfm_r, s_identr), write=s_pM)
    p.op("vector", lambda E: E.tensor_copy(modT[:], pM[0:64, 0:128]),
         reads=(s_pM,), writes=(s_modT,))
    s_rows = [Slot(f"rows{i}") for i in range(24)]
    for si, v in enumerate((1, 4, 5)):
        for dc in range(DC):
            seg = si * 8 + dc
            p.dma("gpsimd",
                  lambda E, seg=seg, v=v, dc=dc: E.dma_start(
                      out=rows_r[:, seg * 128:(seg + 1) * 128],
                      in_=modT[dc * 8 + v:dc * 8 + v + 1, :]),
                  reads=(s_modT,), writes=(s_rows[seg],))

    wnorm = ar.alloc("wnorm", [1, 10 * 128], F32)
    wn_q = ar.alloc("wn_q", [1, DH], F32)
    wn_k = ar.alloc("wn_k", [1, DH], F32)
    s_wnq, s_wnk = Slot("wn_q", ar.take_ghosts()), Slot("wn_k", ar.take_ghosts())
    p.dma("sync", lambda E: E.dma_start(out=wn_q[:], in_=q_norm_w), writes=(s_wnq,))
    p.dma("sync", lambda E: E.dma_start(out=wn_k[:], in_=k_norm_w), writes=(s_wnk,))
    s_wnorm = Slot("wnorm", ar.take_ghosts())
    p.op("vector", lambda E: E.memset(wnorm[:], 0.0), writes=(s_wnorm,))
    sc = DH ** -0.5

    def wseg(i, hi):
        off = i * 128 + (DH if hi else 0)
        return wnorm[:, off:off + DH]
    p.op("vector", lambda E: E.tensor_scalar_mul(wseg(0, 0), wn_q[:], sc),
         reads=(s_wnq,), writes=(s_wnorm,))
    p.op("vector", lambda E: E.tensor_scalar_mul(wseg(1, 1), wn_q[:], sc),
         reads=(s_wnq,), writes=(s_wnorm,))
    p.op("vector", lambda E: E.tensor_tensor(wseg(2, 0), wseg(0, 0), wseg(0, 0),
                                             ALU.mult),
         reads=(), writes=(s_wnorm,))
    p.op("vector", lambda E: E.tensor_tensor(wseg(3, 1), wseg(0, 0), wseg(0, 0),
                                             ALU.mult),
         reads=(), writes=(s_wnorm,))
    p.op("vector", lambda E: E.tensor_copy(wseg(4, 0), wn_k[:]),
         reads=(s_wnk,), writes=(s_wnorm,))
    p.op("vector", lambda E: E.tensor_copy(wseg(5, 1), wn_k[:]),
         reads=(s_wnk,), writes=(s_wnorm,))
    p.op("vector", lambda E: E.tensor_tensor(wseg(6, 0), wseg(4, 0), wseg(4, 0),
                                             ALU.mult),
         reads=(), writes=(s_wnorm,))
    p.op("vector", lambda E: E.tensor_tensor(wseg(7, 1), wseg(4, 0), wseg(4, 0),
                                             ALU.mult),
         reads=(), writes=(s_wnorm,))
    p.op("vector", lambda E: E.memset(wseg(8, 0), 1.0), reads=(), writes=(s_wnorm,))
    p.op("vector", lambda E: E.memset(wseg(9, 1), 1.0), reads=(), writes=(s_wnorm,))

    s_wqpat, s_wkpat = Slot("wqpat"), Slot("wkpat")
    s_sqpat, s_wqsq, s_wksq = Slot("sqpat"), Slot("wqsq"), Slot("wksq")
    for dst, sl, ra, rb in ((wqpat, s_wqpat, 0, 1), (wkpat, s_wkpat, 4, 5)):
        for c in range(DC):
            emits = [
                (lambda E, c=c, ra=ra:
                 E.matmul(pM[0:16, 0:128], eyer[:, 16 * (2 * c):16 * (2 * c) + 16],
                          wnorm[:, ra * 128:(ra + 1) * 128], start=True, stop=False)),
                (lambda E, c=c, rb=rb:
                 E.matmul(pM[0:16, 0:128],
                          eyer[:, 16 * (2 * c + 1):16 * (2 * c + 1) + 16],
                          wnorm[:, rb * 128:(rb + 1) * 128], start=False, stop=True)),
            ]
            p.mm_group(emits, reads=(s_eyer, s_wnorm), write=s_pM)
            p.op("vector",
                 lambda E, c=c, dst=dst: E.tensor_copy(
                     dst[:, c * 128:(c + 1) * 128], pM[0:16, 0:128]),
                 reads=(s_pM,), writes=(sl,))
    for dst, sl, ra, rb in ((sqpat, s_sqpat, 8, 9), (wqsq, s_wqsq, 2, 3),
                            (wksq, s_wksq, 6, 7)):
        for c in range(DC):
            emits = [
                (lambda E, c=c, ra=ra:
                 E.matmul(pM[:, 0:16], wnorm[:, ra * 128:(ra + 1) * 128],
                          eyer[:, 16 * (2 * c):16 * (2 * c) + 16],
                          start=True, stop=False)),
                (lambda E, c=c, rb=rb:
                 E.matmul(pM[:, 0:16], wnorm[:, rb * 128:(rb + 1) * 128],
                          eyer[:, 16 * (2 * c + 1):16 * (2 * c + 1) + 16],
                          start=False, stop=True)),
            ]
            p.mm_group(emits, reads=(s_eyer, s_wnorm), write=s_pM)
            p.op("vector",
                 lambda E, c=c, dst=dst: E.tensor_copy(
                     dst[:, c * 16:(c + 1) * 16], pM[:, 0:16]),
                 reads=(s_pM,), writes=(sl,))

    pMD_ctx.__exit__(None, None, None)
    pM_ctx.__exit__(None, None, None)
    p.barrier()
    ar.free("wnorm", slot_toks(s_wnorm))
    ar.free("wn_q", slot_toks(s_wnq))
    ar.free("wn_k", slot_toks(s_wnk))
    ar.free("c_sb", slot_toks(s_c))
    ar.free("csil", slot_toks(s_csil))
    ar.free("pb_sb", slot_toks(s_pb))

    h1 = ar.alloc("h1", [128, DC * T], BF16)

    def h1c(dc):
        return h1[:, dc * T:(dc + 1) * T]
    s_h1 = [Slot(f"h1_{dc}", ar.take_ghosts()) for dc in range(DC)]

    def layernorm(xslots, xchunk, lrow_fn, lrow_slot, v_sa, hslots, hchunk, tag):
        ghosts = ar.take_ghosts()
        xsq = ar.alloc(f"xsq_{tag}", [128, T], BF16)
        statr = ar.alloc(f"statr_{tag}", [1, 4 * T], F32)
        statb = ar.alloc(f"statb_{tag}", [1, 2 * T], BF16)
        htmp = ar.alloc(f"htmp_{tag}", [128, T], F32)
        s_xsq = Slot("xsq", ghosts)
        s_stat = Slot("statrows", ghosts)
        s_statb = Slot("statb", ghosts)
        s_htmp = Slot("htmp", ghosts)
        pS_ctx = [PSUM(f"pS{tag}{i}", 1, T) for i in range(2)]
        pS = [c.__enter__() for c in pS_ctx]
        s_pS = [Slot(f"pS{i}") for i in range(2)]
        for dc in range(DC):
            p.op("scalar",
                 lambda E, dc=dc: E.activation(xsq[:], xchunk(dc), ACT.Square),
                 reads=(xslots[dc],), writes=(s_xsq,))
            for which in (0, 1):
                emits = []
                for f0, fl in fslices(T):
                    emits.append(
                        lambda E, which=which, dc=dc, f0=f0, fl=fl:
                        E.matmul(pS[which][:, f0:f0 + fl],
                                 onescol_r2[:] if which == 0 else onescol_b2[:],
                                 (xchunk(dc) if which == 0 else xsq[:])[
                                     0:128, f0:f0 + fl],
                                 start=(dc == 0), stop=(dc == DC - 1)))
                rd = ((s_ones, xslots[dc]) if which == 0
                      else (s_ones, s_xsq))
                if dc == DC - 1:
                    p.mm_group(emits, reads=rd, write=s_pS[which])
                else:
                    for e in emits[:-1]:
                        p.op("tensor", e, reads=rd,
                             war=(s_pS[which],) if dc == 0 else (), inc=False)
                    p.op("tensor", emits[-1], reads=rd,
                         war=(s_pS[which],) if dc == 0 else (), inc=True)
        r_rstd = statr[:, 0:T]
        r_mrstd = statr[:, T:2 * T]
        tmp0 = statr[:, 2 * T:3 * T]
        tmp1 = statr[:, 3 * T:4 * T]
        p.op("vector", lambda E: E.tensor_scalar_mul(tmp0, pS[0][:, 0:T], 1.0 / D),
             reads=(s_pS[0],), writes=(s_stat,))
        p.op("vector", lambda E: E.tensor_scalar_mul(tmp1, pS[1][:, 0:T], 1.0 / D),
             reads=(s_pS[1],), writes=(s_stat,))
        p.op("vector", lambda E: E.tensor_tensor(r_mrstd, tmp0, tmp0, ALU.mult),
             reads=(), writes=(s_stat,))
        p.op("vector", lambda E: E.scalar_tensor_tensor(tmp1, tmp1, LN_EPS, r_mrstd,
                                                        ALU.add, ALU.subtract),
             reads=(), writes=(s_stat,))
        p.op("vector", lambda E: E.reciprocal(tmp1, tmp1), reads=(), writes=(s_stat,))
        p.op("scalar", lambda E: E.activation(r_rstd, tmp1, ACT.Sqrt),
             reads=(s_stat,), writes=(s_stat,))
        p.op("vector", lambda E: E.tensor_tensor(r_mrstd, tmp0, r_rstd, ALU.mult),
             reads=(), writes=(s_stat,))
        p.op("vector", lambda E: E.tensor_copy(statb[:, 0:T], r_rstd),
             reads=(s_stat,), writes=(s_statb,))
        p.op("vector", lambda E: E.tensor_copy(statb[:, T:2 * T], r_mrstd),
             reads=(s_stat,), writes=(s_statb,))
        for c in reversed(pS_ctx):
            c.__exit__(None, None, None)
        p.barrier()
        pA_ctx = PSUM(f"pA{tag}", 128, T)
        pB_ctx = PSUM(f"pB{tag}", 128, T)
        pA_, pB_ = pA_ctx.__enter__(), pB_ctx.__enter__()
        s_pA, s_pB = Slot("pA"), Slot("pB")
        for dc in range(DC):
            lrow = lrow_fn(dc)
            emA, emB = [], []
            for f0, fl in fslices(T):
                emA.append(lambda E, lrow=lrow, f0=f0, fl=fl:
                           E.matmul(pA_[:, f0:f0 + fl], lrow,
                                    statb[0:1, f0:f0 + fl], start=True, stop=True))
                emB.append(lambda E, lrow=lrow, f0=f0, fl=fl:
                           E.matmul(pB_[:, f0:f0 + fl], lrow,
                                    statb[0:1, T + f0:T + f0 + fl],
                                    start=True, stop=True))
            p.mm_group(emA, reads=(lrow_slot(dc), s_statb), write=s_pA)
            p.mm_group(emB, reads=(lrow_slot(dc), s_statb), write=s_pB)
            p.op("vector",
                 lambda E, dc=dc: E.tensor_tensor(htmp[:], xchunk(dc), pA_[:, 0:T],
                                                  ALU.mult),
                 reads=(xslots[dc], s_pA), writes=(s_htmp,))
            p.op("vector",
                 lambda E, dc=dc:
                 E.scalar_tensor_tensor(hchunk(dc), htmp[:], modc(v_sa, dc),
                                        pB_[:, 0:T], ALU.add, ALU.subtract),
                 reads=(s_htmp, s_modfm, s_pB), writes=(hslots[dc],))
        pB_ctx.__exit__(None, None, None)
        pA_ctx.__exit__(None, None, None)
        p.barrier()
        ar.free(f"xsq_{tag}", slot_toks(s_xsq))
        ar.free(f"statr_{tag}", slot_toks(s_stat))
        ar.free(f"statb_{tag}", slot_toks(s_statb))
        ar.free(f"htmp_{tag}", slot_toks(s_htmp))

    layernorm(s_xfm, xfm, caplus_row, lambda dc: s_rows[dc], 0, s_h1, h1c, "ln1")
    dbg["h1"] = (s_h1, h1c)
    if BUILD_UPTO < 3:
        return _finish(nc, p, ar, dbg, out, identb, identr, s_identb, s_identr)

    tok_xoff = []
    for dc in range(DC):
        tok_xoff.append(
            p.dma("sync",
                  lambda E, dc=dc: E.dma_start(out=xfm_d[:, dc * T:(dc + 1) * T],
                                               in_=xfm(dc)),
                  reads=(s_xfm[dc],)))
    ar.free("x_fm", slot_toks(*s_xfm))
    ar.free("xtm_st0", [])
    ar.free("xtm_st1", [])
    ar.free("xtm_st2", [])

    ghosts = ar.take_ghosts()
    wst = [ar.alloc(f"wst{i}", [128, D], BF16) for i in range(3)]
    s_wst = [Slot(f"wst{i}", ghosts) for i in range(3)]
    v_tm = ar.alloc("v_tm", [128, TC * H * 65], BF16)
    s_vtm = [Slot(f"v_tm{tc}", ghosts) for tc in range(TC)]

    def vtm(tc):
        return v_tm[:, tc * H * 65:(tc + 1) * H * 65]
    s_vones = Slot("vones", ghosts)
    p.op("vector", lambda E: E.memset(
        v_tm[:].rearrange("p (a e) -> p a e", e=65)[:, :, 64:65], 1.0),
        writes=(s_vones,))

    pW_ctx = PSUM("pW", 128, 512)
    pW = pW_ctx.__enter__()
    s_pW = Slot("pW")

    def prep_wT(mat_off, dstap, dst_slots):
        for pan in range(8):
            i = pan % 3
            p.dma("gpsimd",
                  lambda E, pan=pan, i=i, mat_off=mat_off: E.dma_start(
                      out=wst[i][:],
                      in_=qkv_w[mat_off + pan * 128:mat_off + (pan + 1) * 128, :]),
                  writes=(s_wst[i],))
            for g in range(2):
                emits = []
                for k in range(4):
                    dc = g * 4 + k
                    emits.append(lambda E, i=i, dc=dc, k=k:
                                 E.matmul(pW[:, k * 128:(k + 1) * 128],
                                          wst[i][:, dc * 128:(dc + 1) * 128],
                                          identb[:], start=True, stop=True))
                p.mm_group(emits, reads=(s_wst[i], s_identb), write=s_pW)

                def ev(E, g=g, pan=pan, dstap=dstap):
                    return E.tensor_copy(
                        dstap.rearrange("p (c j) -> p c j", c=DC)[
                            :, g * 4:(g + 1) * 4, pan * 128:(pan + 1) * 128],
                        pW[:].rearrange("p (c j) -> p c j", c=4))
                p.op("vector", ev, reads=(s_pW,),
                     writes=tuple(dst_slots[g * 4 + k] for k in range(4)))

    wvT = ar.alloc("wvT", [128, DC * D], BF16)
    s_wvT = [Slot(f"wvT{dc}", ghosts) for dc in range(DC)]
    prep_wT(2 * D, wvT, s_wvT)
    pV_ctx = [PSUM(f"pV{i}", 128, 1024) for i in range(2)]
    pV = [c.__enter__() for c in pV_ctx]
    s_pV = [Slot(f"pV{i}") for i in range(2)]
    for tc in range(TC):
        i = tc % 2
        emits = []
        for dc in range(DC):
            for j0, jl in fslices(D):
                emits.append(lambda E, tc=tc, dc=dc, j0=j0, jl=jl, i=i:
                             E.matmul(pV[i][:, j0:j0 + jl],
                                      h1c(dc)[:, tc * 128:(tc + 1) * 128],
                                      wvT[:, dc * D + j0:dc * D + j0 + jl],
                                      start=(dc == 0), stop=(dc == DC - 1)))
        p.mm_group(emits, reads=tuple(s_h1) + tuple(s_wvT), write=s_pV[i])

        def ev(E, tc=tc, i=i):
            return E.tensor_copy(
                vtm(tc).rearrange("p (h e) -> p h e", e=65)[:, :, 0:64],
                pV[i][:].rearrange("p (h e) -> p h e", e=64))
        p.op("vector", ev, reads=(s_pV[i],), writes=(s_vtm[tc],))
    for c in reversed(pV_ctx):
        c.__exit__(None, None, None)
    p.barrier()
    ar.free("wvT", slot_toks(*s_wvT))

    qpk = ar.alloc("qpk", [128, DC * T], BF16)
    kpk = ar.alloc("kpk", [128, DC * T], BF16)

    def qpkc(c):
        return qpk[:, c * T:(c + 1) * T]

    def kpkc(c):
        return kpk[:, c * T:(c + 1) * T]
    ghosts = ar.take_ghosts()
    s_qpk = [Slot(f"qpk{c}", ghosts) for c in range(DC)]
    s_kpk = [Slot(f"kpk{c}", ghosts) for c in range(DC)]
    wqT = ar.alloc("wqT", [128, DC * D], BF16)
    s_wqT = [Slot(f"wqT{dc}", ghosts) for dc in range(DC)]
    prep_wT(0, wqT, s_wqT)

    pQ_ctx = [PSUM(f"pQ{i}", 128, 512) for i in range(2)]
    pQ = [c.__enter__() for c in pQ_ctx]
    s_pQ = [Slot(f"pQ{i}") for i in range(2)]
    qkn = [0]

    def qk_prod(wT, s_wT, dst, s_dst):
        for c in range(DC):
            for f0, fl in fslices(T):
                i = qkn[0] % 2
                qkn[0] += 1
                emits = []
                for dc in range(DC):
                    emits.append(lambda E, c=c, dc=dc, f0=f0, fl=fl, i=i, wT=wT:
                                 E.matmul(pQ[i][:, 0:fl],
                                          wT[:, dc * D + c * 128:dc * D + (c + 1) * 128],
                                          h1c(dc)[:, f0:f0 + fl],
                                          start=(dc == 0), stop=(dc == DC - 1)))
                p.mm_group(emits, reads=tuple(s_h1) + tuple(s_wT), write=s_pQ[i])
                p.op("vector",
                     lambda E, c=c, f0=f0, fl=fl, i=i, dst=dst:
                     E.tensor_copy(dst(c)[:, f0:f0 + fl], pQ[i][:, 0:fl]),
                     reads=(s_pQ[i],), writes=(s_dst[c],))

    qk_prod(wqT, s_wqT, qpkc, s_qpk)
    ar.free("wqT", slot_toks(*s_wqT))
    wkT = ar.alloc("wkT", [128, DC * D], BF16)
    s_wkT = [Slot(f"wkT{dc}", ar.take_ghosts()) for dc in range(DC)]
    prep_wT(D, wkT, s_wkT)
    qk_prod(wkT, s_wkT, kpkc, s_kpk)
    for c in reversed(pQ_ctx):
        c.__exit__(None, None, None)
    pW_ctx.__exit__(None, None, None)
    p.barrier()
    ar.free("wkT", slot_toks(*s_wkT))
    ar.free("wst0", slot_toks(s_wst[0]))
    ar.free("wst1", slot_toks(s_wst[1]))
    ar.free("wst2", slot_toks(s_wst[2]))
    ar.free("h1", slot_toks(*s_h1))
    dbg["qpk"] = (s_qpk, qpkc)
    dbg["kpk"] = (s_kpk, kpkc)
    if BUILD_UPTO < 4:
        return _finish(nc, p, ar, dbg, out, identb, identr, s_identb, s_identr)

    ghosts = ar.take_ghosts()
    qsq = ar.alloc("qsq", [128, T], BF16)
    stat = ar.alloc("stat", [16, 4 * T], F32)
    statbh = ar.alloc("statbh", [16, T], BF16)
    s_qsq = Slot("qsq", ghosts)
    s_stat = Slot("stat4", ghosts)
    s_statbh = Slot("statbh", ghosts)
    r_rstd_q = stat[:, 0 * T:1 * T]
    r_rstd_k = stat[:, 1 * T:2 * T]
    r_tmp = stat[:, 2 * T:3 * T]
    r_tmp2 = stat[:, 3 * T:4 * T]
    s_bias = Slot("bias_t")
    s_cvec = Slot("cvec")
    s_crow = Slot("crow")

    for side, (pkc, s_pk, wpat, s_wpat, wsqp, s_wsqp, r_rstd, cvcol) in enumerate((
            (qpkc, s_qpk, wqpat, s_wqpat, wqsq, s_wqsq, r_rstd_q, 0),
            (kpkc, s_kpk, wkpat, s_wkpat, wksq, s_wksq, r_rstd_k, 1))):
        pSS_ctx = [PSUM(f"pSS{side}_{i}", 16, T) for i in range(2)]
        pSS = [c.__enter__() for c in pSS_ctx]
        s_pSS = [Slot(f"pSS{i}") for i in range(2)]
        for c in range(DC):
            p.op("vector",
                 lambda E, c=c, pkc=pkc: E.tensor_tensor(qsq[:], pkc(c), pkc(c),
                                                         ALU.mult),
                 reads=(s_pk[c],), writes=(s_qsq,))
            for which, pat, s_pat in ((0, sqpat, s_sqpat), (1, wsqp, s_wsqp)):
                emits = []
                for f0, fl in fslices(T):
                    emits.append(lambda E, c=c, f0=f0, fl=fl, which=which, pat=pat:
                                 E.matmul(pSS[which][:, f0:f0 + fl],
                                          pat[:, c * 16:(c + 1) * 16],
                                          qsq[:, f0:f0 + fl],
                                          start=(c == 0), stop=(c == DC - 1)))
                rd = (s_qsq, s_pat)
                if c == DC - 1:
                    p.mm_group(emits, reads=rd, write=s_pSS[which])
                else:
                    for e in emits[:-1]:
                        p.op("tensor", e, reads=rd,
                             war=(s_pSS[which],) if c == 0 else (), inc=False)
                    p.op("tensor", emits[-1], reads=rd,
                         war=(s_pSS[which],) if c == 0 else (), inc=True)
        p.op("vector", lambda E, r_rstd=r_rstd:
             E.tensor_scalar(r_rstd, pSS[0][:, 0:T], 1.0 / DH, RMS_EPS,
                             ALU.mult, ALU.add),
             reads=(s_pSS[0],), writes=(s_stat,))
        p.op("vector", lambda E, r_rstd=r_rstd: E.reciprocal(r_rstd, r_rstd),
             reads=(), writes=(s_stat,))
        p.op("scalar", lambda E, r_rstd=r_rstd: E.activation(r_rstd, r_rstd,
                                                             ACT.Sqrt),
             reads=(s_stat,), writes=(s_stat,))
        p.op("vector", lambda E, r_rstd=r_rstd:
             E.tensor_tensor(r_tmp, r_rstd, r_rstd, ALU.mult),
             reads=(), writes=(s_stat,))
        p.op("vector", lambda E:
             E.tensor_tensor(r_tmp2, pSS[1][:, 0:T], r_tmp, ALU.mult),
             reads=(s_pSS[1],), writes=(s_stat,))
        p.op("vector", lambda E, cvcol=cvcol:
             E.reduce_max(cvec[:, cvcol:cvcol + 1], r_tmp2, axis=AX),
             reads=(s_stat,), writes=(s_cvec,))
        for c_ in reversed(pSS_ctx):
            c_.__exit__(None, None, None)
        p.barrier()
        pRB_ctx = PSUM(f"pRB{side}", 128, T)
        pRB = pRB_ctx.__enter__()
        s_pRB = Slot("pRB")
        p.op("vector", lambda E, r_rstd=r_rstd: E.tensor_copy(statbh[:, 0:T], r_rstd),
             reads=(s_stat,), writes=(s_statbh,))
        for c in range(DC):
            emits = []
            for f0, fl in fslices(T):
                emits.append(lambda E, c=c, f0=f0, fl=fl, wpat=wpat:
                             E.matmul(pRB[:, f0:f0 + fl],
                                      wpat[:, c * 128:(c + 1) * 128],
                                      statbh[0:16, f0:f0 + fl],
                                      start=True, stop=True))
            p.mm_group(emits, reads=(s_wpat, s_statbh), write=s_pRB)
            p.op("vector",
                 lambda E, c=c, pkc=pkc: E.tensor_tensor(pkc(c), pkc(c),
                                                         pRB[:, 0:T], ALU.mult),
                 reads=(s_pRB,), writes=(s_pk[c],))
        pRB_ctx.__exit__(None, None, None)
        p.barrier()
    p.op("vector", lambda E: E.tensor_tensor(cvec[:, 2:3], cvec[:, 0:1],
                                             cvec[:, 1:2], ALU.mult),
         reads=(s_cvec,), writes=(s_cvec,))
    p.op("scalar", lambda E: E.activation(cvec[:, 3:4], cvec[:, 2:3], ACT.Sqrt),
         reads=(s_cvec,), writes=(s_cvec,))
    pC_ctx = PSUM("pC", 128, 32)
    pC = pC_ctx.__enter__()
    s_pC = Slot("pC")
    cvr = ar.alloc("cvr", [16, 4], F32)
    s_cvr = Slot("cvr", ghosts)
    p.op("vector", lambda E: E.tensor_copy(cvr[:], cvec[:]),
         reads=(s_cvec,), writes=(s_cvr,))
    p.mm_group([lambda E: E.matmul(pC[0:1, 0:16], cvr[:, 3:4],
                                   identr[0:16, 0:16], start=True, stop=True)],
               reads=(s_cvr, s_identr), write=s_pC)
    p.op("vector", lambda E: E.tensor_scalar_mul(crow[:], pC[0:1, 0:16], -1.0),
         reads=(s_pC,), writes=(s_crow,))
    crowr = ar.alloc("crowr", [1, 16], F32)
    s_crowr = Slot("crowr", ghosts)
    p.op("vector", lambda E: E.tensor_copy(crowr[:], crow[:]),
         reads=(s_crow,), writes=(s_crowr,))
    p.mm_group([lambda E: E.matmul(pC[:, 128:144], onesrow_r[:], crowr[:, 0:16],
                                   start=True, stop=True)],
               reads=(s_crowr, s_ones), write=s_pC)
    p.op("vector", lambda E: E.tensor_copy(bias_t[:], pC[:, 128:144]),
         reads=(s_pC,), writes=(s_bias,))
    pC_ctx.__exit__(None, None, None)
    p.barrier()
    ar.free("qsq", slot_toks(s_qsq))
    ar.free("stat", slot_toks(s_stat))
    ar.free("statbh", slot_toks(s_statbh))
    ar.free("cvr", slot_toks(s_cvr))
    ar.free("crowr", slot_toks(s_crowr))
    dbg["qn"] = (s_qpk, qpkc)
    dbg["kn"] = (s_kpk, kpkc)
    if BUILD_UPTO < 5:
        return _finish(nc, p, ar, dbg, out, identb, identr, s_identb, s_identr)

    s_w1bf = [Slot(f"w1bf{g}") for g in range(24)]
    s_w2bf = [Slot(f"w2bf{q}") for q in range(24)]

    def stage_chunk(k):
        if k < 24:
            p.dma("gpsimd",
                  lambda E, g=k: E.dma_start(
                      out=w1bf[:, g * 4096:(g + 1) * 4096],
                      in_=w1s[:, g * 4096:(g + 1) * 4096]),
                  writes=(s_w1bf[k],))
        elif k < 48:
            p.dma("gpsimd",
                  lambda E, q=k - 24: E.dma_start(
                      out=w2bf[:, q * 4096:(q + 1) * 4096],
                      in_=w2s[:, q * 4096:(q + 1) * 4096]),
                  writes=(s_w2bf[k - 24],))
    ghosts = ar.take_ghosts()
    attn = ar.alloc("attn", [128, DC * T], BF16, end=True)

    def attnc(c):
        return attn[:, c * T:(c + 1) * T]
    s_attn = [Slot(f"attn{c}", ghosts) for c in range(DC)]
    qx = [ar.alloc(f"qx{par}", [128, T], BF16) for par in range(2)]
    s_qx = [Slot(f"qx{par}", ghosts) for par in range(2)]
    p.op("vector", lambda E: E.memset(qx[0][64:128, :], 0.0), writes=(s_qx[0],))
    p.op("vector", lambda E: E.memset(qx[1][0:64, :], 0.0), writes=(s_qx[1],))
    pt = [ar.alloc(f"pt{i}", [128, 1024], BF16) for i in range(3)]
    s_pt = [Slot(f"pt{i}", ghosts) for i in range(3)]
    ast = [ar.alloc(f"ast{i}", [64, T], BF16) for i in range(2)]
    s_ast = [Slot(f"ast{i}", ghosts) for i in range(2)]
    dtmp = [ar.alloc(f"dtmp{i}", [1, T], BF16) for i in range(2)]
    s_dtmp = [Slot(f"dtmp{i}", ghosts) for i in range(2)]
    dnm = ar.alloc("dnm", [16, T], BF16)
    s_dnm = [Slot(f"dnm{h}", ghosts) for h in range(H)]
    dnr = ar.alloc("dnr", [16, T], BF16)
    s_dnr = Slot("dnr", ghosts)

    psp_ctx = [PSUM(f"psp{i}", 128, 1024) for i in range(2)]
    psp = [c.__enter__() for c in psp_ctx]
    s_psp = [Slot(f"psp{i}") for i in range(2)]
    ppv_ctx = PSUM("ppv", 65, T)
    ppv = ppv_ctx.__enter__()

    SPW = min(SPW_CAP, T)
    NHH = T // SPW
    s_ppv = [Slot(f"ppv{nh}") for nh in range(NHH)]

    def qcopy(h):
        c, par = h // 2, h % 2
        p.dma("sync",
              lambda E, c=c, par=par: E.dma_start(
                  out=qx[par][par * 64:(par + 1) * 64, 0:T],
                  in_=qpkc(c)[par * 64:(par + 1) * 64, 0:T]),
              reads=(s_qpk[c],), writes=(s_qx[par],))

    def flush_pv(h, mc, nh, j):
        emits = []
        for f0, fl in fslices(SPW):
            emits.append(
                lambda E, mc=mc, nh=nh, j=j, h=h, f0=f0, fl=fl:
                E.matmul(ppv[0:65, nh * SPW + f0:nh * SPW + f0 + fl],
                         vtm(mc)[:, h * 65:(h + 1) * 65],
                         pt[j][:, f0:f0 + fl],
                         start=(mc == 0), stop=(mc == TC - 1)))
        rd = (s_pt[j], s_vtm[mc])
        if mc == TC - 1:
            p.mm_group(emits, reads=rd, write=s_ppv[nh])
        else:
            for e in emits[:-1]:
                p.op("tensor", e, reads=rd,
                     war=(s_ppv[nh],) if mc == 0 else (), inc=False)
            p.op("tensor", emits[-1], reads=rd,
                 war=(s_ppv[nh],) if mc == 0 else (), inc=True)

    qcopy(0)
    sub = 0
    for h in range(H):
        c, par = h // 2, h % 2
        if h + 1 < H:
            qcopy(h + 1)
        for k in range(3 * h, 3 * h + 3):
            stage_chunk(k)
        pend = None
        for mc in range(TC):
            for nh in range(NHH):
                i = sub % 2
                j = sub % 3
                sub += 1
                emits = []
                for f0, fl in fslices(SPW):
                    emits.append(
                        lambda E, c=c, mc=mc, nh=nh, f0=f0, fl=fl, i=i, par=par:
                        E.matmul(psp[i][:, f0:f0 + fl],
                                 kpkc(c)[:, mc * 128:(mc + 1) * 128],
                                 qx[par][:, nh * SPW + f0:nh * SPW + f0 + fl],
                                 start=True, stop=True))
                p.mm_group(emits, reads=(s_kpk[c], s_qx[par]), write=s_psp[i])
                p.op("scalar",
                     lambda E, i=i, j=j, h=h: E.activation(
                         pt[j][:, 0:SPW], psp[i][:, 0:SPW], ACT.Exp,
                         bias=bias_t[:, h:h + 1], scale=1.0),
                     reads=(s_psp[i], s_bias), writes=(s_pt[j],))
                if pend is not None:
                    flush_pv(*pend)
                pend = (h, mc, nh, j)
        flush_pv(*pend)
        di = h % 2
        p.op("scalar", lambda E, di=di: E.activation(dtmp[di][:, 0:T],
                                                     ppv[64:65, 0:T], ACT.Copy),
             reads=tuple(s_ppv), writes=(s_dtmp[di],))
        p.dma("sync",
              lambda E, h=h, di=di: E.dma_start(out=dnm[h:h + 1, 0:T],
                                                in_=dtmp[di][:, 0:T]),
              reads=(s_dtmp[di],), writes=(s_dnm[h],))
        if par == 0:
            for nh in range(NHH):
                p.op("vector",
                     lambda E, nh=nh, c=c: E.tensor_copy(
                         attnc(c)[0:64, nh * SPW:(nh + 1) * SPW],
                         ppv[0:64, nh * SPW:(nh + 1) * SPW]),
                     reads=(s_ppv[nh],), writes=(s_attn[c],))
        else:
            ai = (h // 2) % 2
            for nh in range(NHH):
                p.op("vector",
                     lambda E, nh=nh, ai=ai: E.tensor_copy(
                         ast[ai][0:64, nh * SPW:(nh + 1) * SPW],
                         ppv[0:64, nh * SPW:(nh + 1) * SPW]),
                     reads=(s_ppv[nh],), writes=(s_ast[ai],))
            p.dma("sync",
                  lambda E, c=c, ai=ai: E.dma_start(out=attnc(c)[64:128, :],
                                                    in_=ast[ai][0:64, :]),
                  reads=(s_ast[ai],), writes=(s_attn[c],))
    ppv_ctx.__exit__(None, None, None)
    for c in reversed(psp_ctx):
        c.__exit__(None, None, None)
    p.op("vector", lambda E: E.reciprocal(dnr[0:16, 0:T], dnm[0:16, 0:T]),
         reads=tuple(s_dnm), writes=(s_dnr,))
    pbc_ctx = [PSUM(f"pbc{i}", 128, SPW) for i in range(2)]
    pbc = [c.__enter__() for c in pbc_ctx]
    s_pbc = [Slot(f"pbc{i}") for i in range(2)]
    nn = 0
    for c in range(DC):
        for nh in range(NHH):
            i = nn % 2
            nn += 1
            emits = []
            for f0, fl in fslices(SPW):
                emits.append(lambda E, c=c, nh=nh, f0=f0, fl=fl, i=i:
                             E.matmul(pbc[i][:, f0:f0 + fl],
                                      selp[0:16, c * 128:(c + 1) * 128],
                                      dnr[0:16, nh * SPW + f0:nh * SPW + f0 + fl],
                                      start=True, stop=True))
            p.mm_group(emits, reads=(s_dnr, s_selp), write=s_pbc[i])
            p.op("vector",
                 lambda E, c=c, nh=nh, i=i: E.tensor_tensor(
                     attnc(c)[:, nh * SPW:(nh + 1) * SPW],
                     attnc(c)[:, nh * SPW:(nh + 1) * SPW],
                     pbc[i][:, 0:SPW], ALU.mult),
                 reads=(s_pbc[i],), writes=(s_attn[c],))
    for ctx in reversed(pbc_ctx):
        ctx.__exit__(None, None, None)
    p.barrier()
    ar.free("qpk", slot_toks(*s_qpk))
    ar.free("kpk", slot_toks(*s_kpk))
    ar.free("v_tm", slot_toks(*s_vtm) + slot_toks(s_vones))
    ar.free("qx0", slot_toks(s_qx[0]))
    ar.free("qx1", slot_toks(s_qx[1]))
    ar.free("pt0", slot_toks(s_pt[0]))
    ar.free("pt1", slot_toks(s_pt[1]))
    ar.free("pt2", slot_toks(s_pt[2]))
    ar.free("ast0", slot_toks(s_ast[0]))
    ar.free("ast1", slot_toks(s_ast[1]))
    ar.free("dtmp0", slot_toks(s_dtmp[0]))
    ar.free("dtmp1", slot_toks(s_dtmp[1]))
    ar.free("dnm", slot_toks(*s_dnm))
    ar.free("dnr", slot_toks(s_dnr))
    dbg["attn"] = (s_attn, attnc)
    if BUILD_UPTO < 6:
        return _finish(nc, p, ar, dbg, out, identb, identr, s_identb, s_identr)

    ghosts = ar.take_ghosts()
    pwT = ar.alloc("pwT", [128, DC * D], BF16, end=True)
    s_pwT = [Slot(f"pwT{dc}", ghosts) for dc in range(DC)]
    wst2 = [ar.alloc(f"wst2_{i}", [128, D], BF16, end=True)
            for i in range(3)]
    s_wst2 = [Slot(f"wst2_{i}", ghosts) for i in range(3)]
    xb2 = ar.alloc("x_fm2", [128, DC * T], F32)

    def xfm2(dc, xb2=xb2):
        return xb2[:, dc * T:(dc + 1) * T]
    s_xfm2 = [Slot(f"x2_{dc}", ghosts + tok_xoff) for dc in range(DC)]
    for dc in range(DC):
        p.dma("sync",
              lambda E, dc=dc: E.dma_start(out=xfm2(dc),
                                           in_=xfm_d[:, dc * T:(dc + 1) * T]),
              writes=(s_xfm2[dc],))

    pW2_ctx = PSUM("pW2", 128, 512)
    pW2 = pW2_ctx.__enter__()
    s_pW2 = Slot("pW2")
    for pan in range(8):
        i = pan % 3
        p.dma("gpsimd",
              lambda E, pan=pan, i=i: E.dma_start(
                  out=wst2[i][:], in_=proj_w[pan * 128:(pan + 1) * 128, :]),
              writes=(s_wst2[i],))
        for g in range(2):
            emits = []
            for k in range(4):
                dc = g * 4 + k
                emits.append(lambda E, i=i, dc=dc, k=k:
                             E.matmul(pW2[:, k * 128:(k + 1) * 128],
                                      wst2[i][:, dc * 128:(dc + 1) * 128],
                                      identb[:], start=True, stop=True))
            p.mm_group(emits, reads=(s_wst2[i], s_identb), write=s_pW2)

            def ev(E, g=g, pan=pan):
                return E.tensor_copy(
                    pwT[:].rearrange("p (c j) -> p c j", c=DC)[
                        :, g * 4:(g + 1) * 4, pan * 128:(pan + 1) * 128],
                    pW2[:].rearrange("p (c j) -> p c j", c=4))
            p.op("vector", ev, reads=(s_pW2,),
                 writes=tuple(s_pwT[g * 4 + k] for k in range(4)))
    pW2_ctx.__exit__(None, None, None)
    p.barrier()

    ptmp = ar.alloc("ptmp", [128, 512], F32, end=True)
    s_ptmp = Slot("ptmp", ghosts)
    pP_ctx = [PSUM(f"pP{i}", 128, 512) for i in range(2)]
    pP = [c.__enter__() for c in pP_ctx]
    s_pP = [Slot(f"pP{i}") for i in range(2)]
    n2 = 0
    for jc in range(DC):
        for f0, fl in fslices(T):
            i = n2 % 2
            n2 += 1
            emits = []
            for dc in range(DC):
                emits.append(lambda E, jc=jc, dc=dc, f0=f0, fl=fl, i=i:
                             E.matmul(pP[i][:, 0:fl],
                                      pwT[:, dc * D + jc * 128:dc * D + (jc + 1) * 128],
                                      attnc(dc)[:, f0:f0 + fl],
                                      start=(dc == 0), stop=(dc == DC - 1)))
            p.mm_group(emits, reads=tuple(s_attn) + tuple(s_pwT), write=s_pP[i])
            p.op("vector",
                 lambda E, jc=jc, f0=f0, fl=fl, i=i:
                 E.tensor_scalar(ptmp[:, 0:fl], pP[i][:, 0:fl],
                                 modc(6, jc), modc(2, jc), ALU.add, ALU.mult),
                 reads=(s_pP[i], s_modfm), writes=(s_ptmp,))
            p.op("vector",
                 lambda E, jc=jc, f0=f0, fl=fl:
                 E.tensor_tensor(xfm2(jc)[:, f0:f0 + fl], xfm2(jc)[:, f0:f0 + fl],
                                 ptmp[:, 0:fl], ALU.add),
                 reads=(s_ptmp,), writes=(s_xfm2[jc],))
    for c in reversed(pP_ctx):
        c.__exit__(None, None, None)
    p.barrier()
    ar.free("attn", slot_toks(*s_attn))
    ar.free("pwT", slot_toks(*s_pwT))
    ar.free("wst2_0", slot_toks(s_wst2[0]))
    ar.free("wst2_1", slot_toks(s_wst2[1]))
    ar.free("wst2_2", slot_toks(s_wst2[2]))
    ar.free("ptmp", slot_toks(s_ptmp))
    dbg["x1"] = (s_xfm2, xfm2)
    if BUILD_UPTO < 7:
        return _finish(nc, p, ar, dbg, out, identb, identr, s_identb, s_identr)

    h2 = ar.alloc("h2", [128, DC * T], BF16, end=True)

    def h2c(dc):
        return h2[:, dc * T:(dc + 1) * T]
    s_h2 = [Slot(f"h2_{dc}", ar.take_ghosts()) for dc in range(DC)]
    layernorm(s_xfm2, xfm2, cbplus_row, lambda dc: s_rows[8 + dc], 3, s_h2, h2c,
              "ln2")
    dbg["h2"] = (s_h2, h2c)

    ghosts = ar.take_ghosts()
    x1st = [ar.alloc(f"x1st{i}", [128, D], F32) for i in range(2)]
    s_x1st = [Slot(f"x1st{i}", ghosts) for i in range(2)]
    pX_ctx = PSUM("pX", 128, 1024)
    pX = pX_ctx.__enter__()
    s_pX = Slot("pX")
    tok_x1 = []
    for tc in range(TC):
        i = tc % 2
        emits = []
        for dc in range(DC):
            emits.append(lambda E, dc=dc, tc=tc:
                         E.matmul(pX[:, dc * 128:(dc + 1) * 128],
                                  xfm2(dc)[:, tc * 128:(tc + 1) * 128],
                                  identr[:], start=True, stop=True))
        p.mm_group(emits, reads=tuple(s_xfm2) + (s_identr,), write=s_pX)
        p.op("vector", lambda E, i=i: E.tensor_copy(x1st[i][:], pX[:]),
             reads=(s_pX,), writes=(s_x1st[i],))
        tok_x1.append(
            p.dma("sync",
                  lambda E, tc=tc, i=i: E.dma_start(
                      out=x1tm_d[tc * 128:(tc + 1) * 128, :], in_=x1st[i][:]),
                  reads=(s_x1st[i],)))
    pX_ctx.__exit__(None, None, None)
    p.barrier()
    ar.free("x_fm2", slot_toks(*s_xfm2))
    ar.free("x1st0", slot_toks(s_x1st[0]))
    ar.free("x1st1", slot_toks(s_x1st[1]))
    if BUILD_UPTO < 8:
        return _finish(nc, p, ar, dbg, out, identb, identr, s_identb, s_identr)

    ghosts = ar.take_ghosts()
    s_rwT = Slot("rwT")
    rw_b = ar.alloc("rw_b", [128, D], BF16)
    s_rwb = Slot("rw_b", ghosts)
    p.op("vector", lambda E: E.memset(rw_b[:], 0.0), writes=(s_rwb,))
    p.dma("gpsimd", lambda E: E.dma_start(out=rw_b[0:8, :], in_=router_w),
          writes=(s_rwb,))
    pR_ctx = PSUM("pR", 128, 160)
    pR = pR_ctx.__enter__()
    s_pR = Slot("pR")
    s_pR2 = Slot("pR2")
    if BUILD_UPTO < 8.02:
        p.barrier()
        return _finish(nc, p, ar, dbg, out, identb, identr, s_identb, s_identr)
    for dc in range(DC):
        p.mm_group([lambda E, dc=dc: E.matmul(pR[:, 0:128],
                                              rw_b[:, dc * 128:(dc + 1) * 128],
                                              identb[:], start=True,
                                              stop=True)],
                   reads=(s_rwb, s_identb), write=s_pR)
        p.op("vector", lambda E, dc=dc: E.tensor_copy(rwT[:, dc * 8:(dc + 1) * 8],
                                                      pR[:, 0:8]),
             reads=(s_pR,), writes=(s_rwT,))
    if BUILD_UPTO < 8.1:
        p.barrier()
        return _finish(nc, p, ar, dbg, out, identb, identr, s_identb, s_identr)
    s_probs = Slot("probs")
    for tc in range(TC):
        which = tc % 2
        sl = s_pR if which == 0 else s_pR2
        lo = 128 + which * 128
        emits = []
        for dc in range(DC):
            emits.append(lambda E, tc=tc, dc=dc, lo=lo:
                         E.matmul(pR[:, lo:lo + 8],
                                  h2c(dc)[:, tc * 128:(tc + 1) * 128],
                                  rwT[:, dc * 8:(dc + 1) * 8],
                                  start=(dc == 0), stop=(dc == DC - 1)))
        p.mm_group(emits, reads=tuple(s_h2) + (s_rwT,), write=sl)
        p.op("scalar",
             lambda E, tc=tc, lo=lo: E.activation(probs[:, tc * 8:(tc + 1) * 8],
                                                  pR[:, lo:lo + 8], ACT.Exp,
                                                  scale=-1.0),
             reads=(sl,), writes=(s_probs,))
        p.op("vector",
             lambda E, tc=tc: E.tensor_scalar_add(probs[:, tc * 8:(tc + 1) * 8],
                                                  probs[:, tc * 8:(tc + 1) * 8],
                                                  1.0),
             reads=(), writes=(s_probs,))
        p.op("vector",
             lambda E, tc=tc: E.reciprocal(probs[:, tc * 8:(tc + 1) * 8],
                                           probs[:, tc * 8:(tc + 1) * 8]),
             reads=(), writes=(s_probs,))
    if BUILD_UPTO < 8.2:
        p.barrier()
        return _finish(nc, p, ar, dbg, out, identb, identr, s_identb, s_identr)
    s_comb = Slot("comb")
    s_m8 = Slot("m8")
    s_tt = Slot("tt")
    for tc in range(TC):
        pr = probs[:, tc * 8:(tc + 1) * 8]
        cb = comb[:, tc * 8:(tc + 1) * 8]
        p.op("vector", lambda E, pr=pr: E.max(out=m8[:], in_=pr),
             reads=(s_probs,), writes=(s_m8,))
        p.op("vector", lambda E: E.tensor_tensor(tt1[:, 0:1], m8[:, 0:1],
                                                 m8[:, 1:2], ALU.add),
             reads=(s_m8,), writes=(s_tt,))
        p.op("vector", lambda E: E.tensor_scalar_add(tt1[:, 0:1], tt1[:, 0:1],
                                                     1e-20),
             reads=(), writes=(s_tt,))
        p.op("vector", lambda E: E.reciprocal(tt1[:, 1:2], tt1[:, 0:1]),
             reads=(), writes=(s_tt,))
        p.op("vector", lambda E: E.tensor_tensor(tt1[:, 2:3], m8[:, 0:1],
                                                 tt1[:, 1:2], ALU.mult),
             reads=(s_m8,), writes=(s_tt,))
        p.op("vector", lambda E: E.tensor_tensor(tt1[:, 3:4], m8[:, 1:2],
                                                 tt1[:, 1:2], ALU.mult),
             reads=(s_m8,), writes=(s_tt,))
        p.op("vector", lambda E, pr=pr: E.tensor_scalar(tt2[:], pr, m8[:, 0:1],
                                                        None, ALU.is_equal),
             reads=(s_probs, s_m8), writes=(s_tt,))
        p.op("vector", lambda E, cb=cb: E.tensor_scalar(cb, tt2[:], tt1[:, 2:3],
                                                        None, ALU.mult),
             reads=(s_tt,), writes=(s_comb,))
        p.op("vector", lambda E: E.tensor_scalar(tt3[:], tt2[:], -1.0, 1.0,
                                                 ALU.mult, ALU.add),
             reads=(s_tt,), writes=(s_tt,))
        p.op("vector", lambda E, pr=pr: E.tensor_scalar(tt2[:], pr, m8[:, 1:2],
                                                        None, ALU.is_equal),
             reads=(s_probs, s_m8), writes=(s_tt,))
        p.op("vector", lambda E: E.tensor_scalar(tt2[:], tt2[:], tt1[:, 3:4],
                                                 None, ALU.mult),
             reads=(), writes=(s_tt,))
        p.op("vector", lambda E: E.tensor_tensor(tt2[:], tt2[:], tt3[:], ALU.mult),
             reads=(), writes=(s_tt,))
        p.op("vector", lambda E, cb=cb: E.tensor_tensor(cb, cb, tt2[:], ALU.add),
             reads=(s_tt,), writes=(s_comb,))
    if BUILD_UPTO < 8.4:
        p.barrier()
        return _finish(nc, p, ar, dbg, out, identb, identr, s_identb, s_identr)
    p.op("scalar", lambda E: E.activation(comb[:], comb[:], ACT.Sqrt),
         reads=(s_comb,), writes=(s_comb,))
    pC2_ctx = PSUM("pC2", 8, T)
    pC2 = pC2_ctx.__enter__()
    s_pC2 = Slot("pC2")
    emits = []
    for tc in range(TC):
        emits.append(lambda E, tc=tc:
                     E.matmul(pC2[0:8, tc * 128:(tc + 1) * 128],
                              comb[:, tc * 8:(tc + 1) * 8], identr[:],
                              start=True, stop=True))
    p.mm_group(emits, reads=(s_comb, s_identr), write=s_pC2)
    sqckm = ar.alloc("sqckm", [8, T], F32)
    s_sqckm = Slot("sqckm", ghosts)
    p.op("vector", lambda E: E.tensor_copy(sqckm[:, 0:T], pC2[0:8, 0:T]),
         reads=(s_pC2,), writes=(s_sqckm,))
    pC2_ctx.__exit__(None, None, None)
    tok_sqc = p.dma("sync", lambda E: E.dma_start(out=sqc_d[:, 0:T],
                                                  in_=sqckm[:, 0:T]),
                    reads=(s_sqckm,))
    p.barrier()
    pG_ctx = PSUM("pG", 128, 1024)
    pG = pG_ctx.__enter__()
    s_pG = Slot("pG")
    emits = []
    for dc in range(DC):
        emits.append(lambda E, dc=dc:
                     E.matmul(pG[:, dc * 128:(dc + 1) * 128], onesrow_b[:],
                              gb_row(dc), start=True, stop=True))
    p.mm_group(emits, reads=tuple(s_rows[16:24]) + (s_ones,), write=s_pG)
    s_gbb = Slot("gbb")
    p.op("vector", lambda E: E.tensor_copy(gbb[:], pG[:]),
         reads=(s_pG,), writes=(s_gbb,))
    pG_ctx.__exit__(None, None, None)
    pR_ctx.__exit__(None, None, None)
    p.barrier()
    ar.free("rw_b", slot_toks(s_rwb))
    ar.free("sqckm", slot_toks(s_sqckm))
    if BUILD_UPTO < 9:
        return _finish(nc, p, ar, dbg, out, identb, identr, s_identb, s_identr)

    ghosts = ar.take_ghosts() + [tok_sqc]
    h_all = ar.alloc("h_all", [128, NWC * FG], BF16)

    def hallw(wc):
        return h_all[:, wc * FG:(wc + 1) * FG]
    s_hall = [Slot(f"hall{wc}", ghosts) for wc in range(NWC)]
    cbt = [ar.alloc(f"cbt{i}", [128, FG], F32) for i in range(2)]
    s_cbt = [Slot(f"cbt{i}", ghosts) for i in range(2)]
    WG = 4
    wsh = ar.alloc("wsh", [128, 2 * DC * WG * 128], BF16)
    w1t = [wsh[:, i * 4096:(i + 1) * 4096] for i in range(2)]
    s_w1t = [Slot(f"w1t{i}", ghosts) for i in range(2)]
    htmp8 = [ar.alloc(f"htmp8{i}", [128, FG], BF16) for i in range(2)]
    s_htmp8 = [Slot(f"htmp8{i}", ghosts) for i in range(2)]
    w2g = [wsh[:, i * 4096:(i + 1) * 4096] for i in range(2)]
    s_w2g = [Slot(f"w2g{i}", ghosts) for i in range(2)]
    x1t = [ar.alloc(f"x1t{i}", [128, D], F32) for i in range(2)]
    s_x1t = [Slot(f"x1t{i}", ghosts + tok_x1) for i in range(2)]
    ost = [ar.alloc(f"ost{i}", [128, D], F32) for i in range(2)]
    s_ost = [Slot(f"ost{i}", ghosts) for i in range(2)]
    final = []
    NWG = NWC // WG
    GT = FG // 128
    for ti in range(T // FG):
        t0 = ti * FG
        pH_ctx = [PSUM(f"pH{ti}_{i}", 128, FG) for i in range(2)]
        pH = [c.__enter__() for c in pH_ctx]
        s_pH = [Slot(f"pH{i}") for i in range(2)]
        for e in range(2):
            p.dma("gpsimd",
                  lambda E, e=e, t0=t0: E.dma_start(
                      out=cbt[e][:].unsqueeze(1),
                      in_=sqc_d[e:e + 1, t0:t0 + FG].partition_broadcast(128)),
                  reads=(), writes=(s_cbt[e],))
        WGW = WG * 128
        for g in range(min(2, NWG)):
            p.dma("gpsimd",
                  lambda E, g=g: E.dma_start(
                      out=w1t[g % 2][:, 0:DC * WGW],
                      in_=w1bf[:, g * 4096:(g + 1) * 4096]),
                  reads=(s_w1bf[g],), writes=(s_w1t[g % 2],),
                  war=(s_w2g[g % 2],))
        for wc in range(NWC):
            e = WC_EXPERT[wc]
            g, wi = wc // WG, wc % WG
            i = wc % 2
            emits = []
            for dc in range(DC):
                emits.append(
                    lambda E, dc=dc, i=i, g=g, wi=wi, t0=t0, pH=pH, WGW=WGW:
                    E.matmul(pH[i][:, 0:FG],
                             w1t[g % 2][:, dc * WGW + wi * 128:dc * WGW + (wi + 1) * 128],
                             h2c(dc)[:, t0:t0 + FG],
                             start=(dc == 0), stop=(dc == DC - 1)))
            p.mm_group(emits, reads=tuple(s_h2) + (s_w1t[g % 2],), write=s_pH[i])
            p.op("vector",
                 lambda E, i=i, e=e, pH=pH: E.scalar_tensor_tensor(
                     htmp8[i][:, 0:FG], pH[i][:, 0:FG], 0.0, cbt[e % 2][:, 0:FG],
                     ALU.max, ALU.mult),
                 reads=(s_pH[i], s_cbt[e % 2]), writes=(s_htmp8[i],))
            p.op("scalar",
                 lambda E, i=i, wc=wc: E.activation(hallw(wc), htmp8[i][:, 0:FG],
                                                    ACT.Square),
                 reads=(s_htmp8[i],), writes=(s_hall[wc],))
            if wi == WG - 1 and g + 2 < NWG:
                p.dma("gpsimd",
                      lambda E, g=g: E.dma_start(
                          out=w1t[g % 2][:, 0:DC * WGW],
                          in_=w1bf[:, (g + 2) * 4096:(g + 3) * 4096]),
                      reads=(s_w1bf[g + 2],), writes=(s_w1t[g % 2],))
            if wc + 1 < NWC and WC_EXPERT[wc + 1] == e + 1 and e + 2 < 8:
                p.dma("gpsimd",
                      lambda E, e=e, t0=t0: E.dma_start(
                          out=cbt[e % 2][:].unsqueeze(1),
                          in_=sqc_d[e + 2:e + 3, t0:t0 + FG].partition_broadcast(128)),
                      reads=(), writes=(s_cbt[e % 2],))
        for c in reversed(pH_ctx):
            c.__exit__(None, None, None)
        p.barrier()
        pY_ctx = [PSUM(f"pY{ti}_{i}", 128, 1024)
                  for i in range(GT)]
        pY = [c.__enter__() for c in pY_ctx]
        s_pY = [Slot(f"pY{i}") for i in range(GT)]
        for tj in range(min(2, GT)):
            tc = ti * GT + tj
            p.dma("sync",
                  lambda E, tc=tc, tj=tj: E.dma_start(
                      out=x1t[tj % 2][:], in_=x1tm_d[tc * 128:(tc + 1) * 128, :]),
                  writes=(s_x1t[tj % 2],))
        for q in range(min(2, NWC // 4)):
            p.dma("scalar",
                  lambda E, q=q: E.dma_start(
                      out=w2g[q % 2][:], in_=w2bf[:, q * 4096:(q + 1) * 4096]),
                  reads=(s_w2bf[q],), writes=(s_w2g[q % 2],),
                  war=(s_w1t[q % 2],))
        for wc in range(NWC):
            q, wj = wc // 4, wc % 4
            for tj in range(GT):
                emits = []
                for j0, jl in fslices(D):
                    emits.append(
                        lambda E, wc=wc, tj=tj, j0=j0, jl=jl, q=q, wj=wj, pY=pY:
                        E.matmul(pY[tj][:, j0:j0 + jl],
                                 hallw(wc)[:, tj * 128:(tj + 1) * 128],
                                 w2g[q % 2][:, wj * D + j0:wj * D + j0 + jl],
                                 start=(wc == 0), stop=(wc == NWC - 1)))
                rd = (s_hall[wc], s_w2g[q % 2])
                if wc == NWC - 1:
                    p.mm_group(emits, reads=rd, write=s_pY[tj])
                else:
                    for e_ in emits[:-1]:
                        p.op("tensor", e_, reads=rd,
                             war=(s_pY[tj],) if wc == 0 else (), inc=False)
                    p.op("tensor", emits[-1], reads=rd,
                         war=(s_pY[tj],) if wc == 0 else (), inc=True)
            if wj == 3 and (q + 2) * 4 < NWC:
                p.dma("scalar",
                      lambda E, q=q: E.dma_start(
                          out=w2g[q % 2][:],
                          in_=w2bf[:, (q + 2) * 4096:(q + 3) * 4096]),
                      reads=(s_w2bf[q + 2],), writes=(s_w2g[q % 2],))
        for tj in range(GT):
            tc = ti * GT + tj
            oi = tj % 2
            p.op("vector",
                 lambda E, tj=tj, oi=oi, pY=pY: E.tensor_tensor(ost[oi][:], pY[tj][:],
                                                                gbb[:], ALU.mult),
                 reads=(s_pY[tj], s_gbb), writes=(s_ost[oi],))
            p.op("vector",
                 lambda E, oi=oi: E.tensor_tensor(ost[oi][:], ost[oi][:],
                                                  x1t[oi][:], ALU.add),
                 reads=(s_x1t[oi],), writes=(s_ost[oi],))
            final.append(p.dma("sync",
                               lambda E, tc=tc, oi=oi: E.dma_start(
                                   out=out[tc * 128:(tc + 1) * 128, :],
                                   in_=ost[oi][:]),
                               reads=(s_ost[oi],)))
            if tj + 2 < GT:
                tc2 = ti * GT + tj + 2
                p.dma("sync",
                      lambda E, tc2=tc2, oi=oi: E.dma_start(
                          out=x1t[oi][:],
                          in_=x1tm_d[tc2 * 128:(tc2 + 1) * 128, :]),
                      writes=(s_x1t[oi],))
        for c in reversed(pY_ctx):
            c.__exit__(None, None, None)
        p.barrier()
    p.finalize(final)
    global LAST_AR
    LAST_AR = ar
    with nc.Block() as block:
        with nc.allow_low_precision(reason="f32r/bf16 by design"):
            p.run_sections(block)
    return nc


LAST_AR = None


def _finish(nc, p, ar, dbg, out, identb, identr, s_identb, s_identr):
    slots, chunkf = dbg[DEV_TAP]
    ghosts = ar.take_ghosts()
    outst = [ar.alloc(f"outst{i}", [128, D], F32) for i in range(2)]
    s_outst = [Slot(f"outst{i}", ghosts) for i in range(2)]
    final = []
    pO_ctx = nc.psum_tensor("pO", [128, 1024], F32)
    pO = pO_ctx.__enter__()
    s_pO = Slot("pO")
    use_b = chunkf(0).dtype == BF16
    iden, s_iden = (identb, s_identb) if use_b else (identr, s_identr)
    for tc in range(TC):
        i = tc % 2
        emits = []
        for dc in range(DC):
            emits.append(lambda E, dc=dc, tc=tc, iden=iden:
                         E.matmul(pO[:, dc * 128:(dc + 1) * 128],
                                  chunkf(dc)[:, tc * 128:(tc + 1) * 128],
                                  iden[:], start=True, stop=True))
        p.mm_group(emits, reads=tuple(slots) + (s_iden,), write=s_pO)
        p.op("vector", lambda E, i=i: E.tensor_copy(outst[i][:], pO[:]),
             reads=(s_pO,), writes=(s_outst[i],))
        final.append(
            p.dma("sync",
                  lambda E, tc=tc, i=i: E.dma_start(
                      out=out[tc * 128:(tc + 1) * 128, :], in_=outst[i][:]),
                  reads=(s_outst[i],)))
    pO_ctx.__exit__(None, None, None)
    p.finalize(final)
    with nc.Block() as block:
        with nc.allow_low_precision(reason="f32r/bf16 by design"):
            p.run_sections(block)
    return nc


def _consts():
    return {
        "ident_b": np.eye(128, dtype=ml_dtypes.bfloat16),
        "ident_r": np.eye(128, dtype=np.float32),
        "eyerows": np.eye(16, dtype=np.float32).reshape(1, 256),
        "selpat": _selpat(),
    }


def _selpat():
    s = np.zeros((16, DC * 128), dtype=ml_dtypes.bfloat16)
    for h in range(16):
        s[h, (h // 2) * 128 + (h % 2) * 64:(h // 2) * 128 + (h % 2) * 64 + 64] = 1
    return s


def make_in_maps(L0_attn_x, L0_attn_qkv_w, L0_attn_proj_w, L0_attn_proj_b,
                 L0_attn_q_norm_w, L0_attn_k_norm_w, L0_ffn_router_w, L0_ffn_w1,
                 L0_ffn_w2, L0_c, L0_wmod):
    x = np.ascontiguousarray(np.asarray(L0_attn_x, dtype=np.float32))
    shared = {
        "qkv_w": np.ascontiguousarray(np.asarray(L0_attn_qkv_w, np.float32)),
        "proj_w": np.ascontiguousarray(np.asarray(L0_attn_proj_w, np.float32)),
        "proj_b": np.asarray(L0_attn_proj_b, np.float32).reshape(1, D),
        "q_norm_w": np.asarray(L0_attn_q_norm_w, np.float32).reshape(1, DH),
        "k_norm_w": np.asarray(L0_attn_k_norm_w, np.float32).reshape(1, DH),
        "router_w": np.ascontiguousarray(np.asarray(L0_ffn_router_w, np.float32)),
        "w1s": np.ascontiguousarray(
            np.asarray(L0_ffn_w1, np.float32).reshape(8, 128, 24, 4, 128)
            .transpose(1, 2, 0, 3, 4).reshape(128, 24 * 4096)),
        "w2s": np.ascontiguousarray(
            np.asarray(L0_ffn_w2, np.float32).reshape(96, 128, 1024)
            .transpose(1, 0, 2).reshape(128, 96 * 1024)),
        "wmod": np.ascontiguousarray(np.asarray(L0_wmod, np.float32)),
    }
    shared.update(_consts())
    carr = np.asarray(L0_c, np.float32)
    in_maps = []
    for b in range(x.shape[0]):
        m = dict(shared)
        m["x"] = np.ascontiguousarray(x[b][:T])
        m["c_row"] = np.ascontiguousarray(carr[b:b + 1])
        in_maps.append(m)
    return in_maps


def kernel(**inputs):
    in_maps = make_in_maps(**inputs)
    nc = build_program()
    res = run_bass_kernel_spmd(nc, in_maps, core_ids=list(range(len(in_maps))))
    return np.stack([np.asarray(r["out"], np.float32) for r in res.results], axis=0)


# version 19
# speedup vs baseline: 1.0227x; 1.0031x over previous
import numpy as np
import ml_dtypes
import concourse.bass as bass
import concourse.mybir as mybir
from concourse.bass_utils import run_bass_kernel_spmd

F32 = mybir.dt.float32
F32R = mybir.dt.float32r
BF16 = mybir.dt.bfloat16
ALU = mybir.AluOpType
ACT = mybir.ActivationFunctionType
AX = mybir.AxisListType.X

NCORES = 8
D = 1024
H = 16
DH = 64
DC = D // 128
WIDTHS = [1024] * 4 + [2048] * 4
TOTW = sum(WIDTHS)
NWC = TOTW // 128
WC_EXPERT = [e for e in range(8) for _ in range(WIDTHS[e] // 128)]
LN_EPS = 1e-5
RMS_EPS = 1e-6

T = 2048
TC = T // 128
FG = 512

BUILD_UPTO = 99
DEV_TAP = None
SPW_CAP = 1024


def set_T(t):
    global T, TC, FG
    T = t
    TC = T // 128
    FG = min(512, T)


class Slot:
    __slots__ = ("w", "rs", "name")

    def __init__(self, name="", ghosts=()):
        self.w = None
        self.rs = list(ghosts)
        self.name = name


class Prog:
    CENG = ("tensor", "vector", "scalar", "gpsimd", "sync")

    def __init__(self, nc, dma_pool=14):
        self.nc = nc
        self.instrs = {e: [] for e in self.CENG}
        self._sem_ctxs = []
        self.sems = {}
        for e in self.CENG:
            self._mksem("e_" + e)
        self.dma_sems = {}
        for q in ("sync", "gpsimd", "scalar"):
            names = []
            for i in range(dma_pool):
                n = f"d_{q}{i}"
                self._mksem(n)
                names.append(n)
            self.dma_sems[q] = names
        self.cnt = {e: 0 for e in self.CENG}
        self.wm = {e: {} for e in self.CENG}
        self.dcnt = {n: 0 for q in self.dma_sems for n in self.dma_sems[q]}
        self.dwaited = {n: 0 for n in self.dcnt}
        self.dlive = {n: [] for n in self.dcnt}
        self.dorder = {n: [] for n in self.dcnt}
        self.drr = {q: 0 for q in self.dma_sems}
        self.erun = {e: [] for e in self.CENG}
        self.econs = {e: 0 for e in self.CENG}

    def _mksem(self, name):
        ctx = self.nc.semaphore(name)
        h = ctx.__enter__()
        self._sem_ctxs.append(ctx)
        self.sems[name] = h

    def _wait(self, eng, tok, waited_dma=None):
        sem, val = tok
        if self.wm[eng].get(sem, 0) >= val:
            return
        if sem in self.dcnt:
            assert val == self.dcnt[sem], (
                f"dma wait on {sem} for {val} but counter at {self.dcnt[sem]}")
            self.dwaited[sem] = max(self.dwaited[sem], val)
            if waited_dma is not None:
                waited_dma.append(sem)
        else:
            eng2 = sem[2:]
            if eng2 in self.econs:
                self.econs[eng2] = max(self.econs[eng2], val)
        self.wm[eng][sem] = val
        self.erun[eng] = []
        h = self.sems[sem]
        self.instrs[eng].append(lambda E, h=h, v=val: E.wait_ge(h, v))

    def _pre_inc(self, eng):
        run = self.erun[eng]
        if run and self.econs[eng] >= run[0][1]:
            h = self.sems["e_" + eng]
            v = self.cnt[eng]
            self.instrs[eng].append(lambda E, h=h, v=v: E.wait_ge(h, v))
            self.wm[eng]["e_" + eng] = v
            self.erun[eng] = []

    def _deps(self, reads, writes, war=()):
        deps = []
        for s in reads:
            if s.w is not None:
                deps.append(s.w)
        for s in tuple(writes) + tuple(war):
            deps.extend(s.rs)
            if s.w is not None:
                deps.append(s.w)
        best = {}
        for t in deps:
            if t[0] not in best or best[t[0]][1] < t[1]:
                best[t[0]] = t
        return list(best.values())

    def op(self, eng, emit, reads=(), writes=(), war=(), inc=True):
        wd = []
        for tok in self._deps(reads, writes, war):
            self._wait(eng, tok, waited_dma=wd)
        sem = "e_" + eng
        if inc:
            self._pre_inc(eng)
            self.cnt[eng] += 1
            for t in self.erun[eng]:
                t[1] = self.cnt[eng]
            tok = [sem, self.cnt[eng]]
            self.erun[eng].append(tok)
            h = self.sems[sem]
            self.instrs[eng].append(lambda E, f=emit, h=h: f(E).then_inc(h, 1))
        else:
            tok = [sem, self.cnt[eng]]
            self.instrs[eng].append(lambda E, f=emit: f(E))
        for s in reads:
            s.rs.append(tok)
        for s in writes:
            s.w = tok
            s.rs = []
        for dsem in wd:
            self.dorder[dsem].append(tok)
        return tok

    def mm_group(self, emits, reads=(), write=None, war=()):
        writes = (write,) if write is not None else ()
        wd = []
        for tok in self._deps(reads, writes, war):
            self._wait("tensor", tok, waited_dma=wd)
        sem = "e_tensor"
        h = self.sems[sem]
        for f in emits[:-1]:
            self.instrs["tensor"].append(lambda E, f=f: f(E))
        self._pre_inc("tensor")
        self.cnt["tensor"] += 1
        for t in self.erun["tensor"]:
            t[1] = self.cnt["tensor"]
        tok = [sem, self.cnt["tensor"]]
        self.erun["tensor"].append(tok)
        f = emits[-1]
        self.instrs["tensor"].append(lambda E, f=f, h=h: f(E).then_inc(h, 1))
        for s in reads:
            s.rs.append(tok)
        if write is not None:
            write.w = tok
            write.rs = []
        for dsem in wd:
            self.dorder[dsem].append(tok)
        return tok

    def dma(self, queue, emit, reads=(), writes=(), war=()):
        wd = []
        for tok in self._deps(reads, writes, war):
            self._wait(queue, tok, waited_dma=wd)
        pool = self.dma_sems[queue]
        for _ in range(len(pool)):
            n = pool[self.drr[queue] % len(pool)]
            self.drr[queue] += 1
            if self.dwaited[n] >= self.dcnt[n]:
                break
        else:
            n = pool[self.drr[queue] % len(pool)]
            self.drr[queue] += 1
        if self.dcnt[n] > 0:
            for t in list(self.dorder[n]):
                self._wait(queue, t, waited_dma=wd)
            self.dorder[n] = []
        self.dcnt[n] += 16
        for t in self.dlive[n]:
            t[1] = self.dcnt[n]
        tok = [n, self.dcnt[n]]
        self.dlive[n].append(tok)
        h = self.sems[n]
        self.instrs[queue].append(lambda E, f=emit, h=h: f(E).then_inc(h, 16))
        for s in reads:
            s.rs.append(tok)
        for s in writes:
            s.w = tok
            s.rs = []
        for dsem in wd:
            self.dorder[dsem].append(tok)
        return tok

    def barrier(self, engines=("tensor", "vector", "scalar")):
        toks = []
        for e in engines:
            self._pre_inc(e)
            self.cnt[e] += 1
            for t in self.erun[e]:
                t[1] = self.cnt[e]
            h = self.sems["e_" + e]

            def mk(E, h=h):
                ins = E.engine_nop() if hasattr(E, "engine_nop") else E.nop()
                return ins.then_inc(h, 1)
            self.instrs[e].append(mk)
            tok = ["e_" + e, self.cnt[e]]
            self.erun[e].append(tok)
            toks.append(tok)
        for e in engines:
            for tok in toks:
                self._wait(e, tok)

    def finalize(self, final_tokens):
        for tok in final_tokens:
            self._wait("sync", tok)

    def run_sections(self, block):
        for e in self.CENG:
            lst = self.instrs[e]

            def runner(E, lst=lst):
                for f in lst:
                    f(E)
            getattr(block, e)(runner)


def fslices(total, step=512):
    return [(i, min(step, total - i)) for i in range(0, total, step)]


class Arena:

    def __init__(self, nc, kib):
        self.t = nc.alloc_sbuf_tensor("arena", [128, kib * 1024], mybir.dt.uint8)
        self.free_list = [[0, kib * 1024]]
        self.regions = {}
        self.ghosts = []

    def alloc(self, name, shape, dtype, end=False):
        esz = {F32: 4, F32R: 4, BF16: 2}[dtype]
        size = (shape[1] * esz + 31) // 32 * 32
        frs = self.free_list if not end else list(reversed(self.free_list))
        for fr in frs:
            if fr[1] >= size:
                if end:
                    off = fr[0] + fr[1] - size
                    fr[1] -= size
                else:
                    off = fr[0]
                    fr[0] += size
                    fr[1] -= size
                break
        else:
            raise MemoryError(f"arena: no room for {name} ({size}B) "
                              f"free={self.free_list} regions={self.regions}")
        self.free_list = [f for f in self.free_list if f[1] > 0]
        self.regions[name] = (off, size)
        return self.t[:][0:shape[0], off:off + shape[1] * esz].bitcast(dtype)

    def free(self, name, tokens=()):
        off, size = self.regions.pop(name)
        self.free_list.append([off, size])
        self.free_list.sort()
        merged = []
        for fr in self.free_list:
            if merged and merged[-1][0] + merged[-1][1] == fr[0]:
                merged[-1][1] += fr[1]
            else:
                merged.append(fr)
        self.free_list = merged
        self.ghosts.extend(t for t in tokens if t is not None)

    def take_ghosts(self):
        return list(self.ghosts)


def slot_toks(*slots):
    out = []
    for s in slots:
        if s.w is not None:
            out.append(s.w)
        out.extend(s.rs)
    return out


def build_program():
    nc = bass.Bass("TRN2", target_bir_lowering=False)

    def PSUM(name, rows, cols):
        return nc.psum_tensor(name, [rows, (cols + 511) // 512 * 512], F32)
    p = Prog(nc)
    A = nc.alloc_sbuf_tensor
    ar = Arena(nc, 176)
    dbg = {}

    x_in = nc.dram_tensor("x", [T, D], F32, kind="ExternalInput").ap()
    qkv_w = nc.dram_tensor("qkv_w", [3 * D, D], F32, kind="ExternalInput").ap()
    proj_w = nc.dram_tensor("proj_w", [D, D], F32, kind="ExternalInput").ap()
    proj_b = nc.dram_tensor("proj_b", [1, D], F32, kind="ExternalInput").ap()
    q_norm_w = nc.dram_tensor("q_norm_w", [1, DH], F32, kind="ExternalInput").ap()
    k_norm_w = nc.dram_tensor("k_norm_w", [1, DH], F32, kind="ExternalInput").ap()
    router_w = nc.dram_tensor("router_w", [8, D], F32, kind="ExternalInput").ap()
    w1s = nc.dram_tensor("w1s", [128, 24 * 4096], F32, kind="ExternalInput").ap()
    w2s = nc.dram_tensor("w2s", [128, 96 * 1024], F32, kind="ExternalInput").ap()
    w1bf = nc.dram_tensor("w1bf", [128, 24 * 4096], BF16, kind="Internal").ap()
    w2bf = nc.dram_tensor("w2bf", [128, 96 * 1024], BF16, kind="Internal").ap()
    c_row = nc.dram_tensor("c_row", [1, D], F32, kind="ExternalInput").ap()
    wmod = nc.dram_tensor("wmod", [D, 6 * D], F32, kind="ExternalInput").ap()
    ident_b = nc.dram_tensor("ident_b", [128, 128], BF16, kind="ExternalInput").ap()
    ident_r = nc.dram_tensor("ident_r", [128, 128], F32, kind="ExternalInput").ap()
    eyerows = nc.dram_tensor("eyerows", [1, 256], F32, kind="ExternalInput").ap()
    selpat = nc.dram_tensor("selpat", [16, DC * 128], BF16,
                            kind="ExternalInput").ap()
    out = nc.dram_tensor("out", [T, D], F32, kind="ExternalOutput").ap()
    xfm_d = nc.dram_tensor("xfm_d", [128, DC * T], F32, kind="Internal").ap()
    x1tm_d = nc.dram_tensor("x1tm_d", [T, D], F32, kind="Internal").ap()
    sqc_d = nc.dram_tensor("sqc_d", [8, T], F32, kind="Internal").ap()

    identb = A("identb", [128, 128], BF16)
    identr = A("identr", [128, 128], F32)
    eyer = A("eyer", [1, 256], F32)
    selp = A("selp", [16, DC * 128], BF16)
    onesrow_r = A("onesrow_r", [1, 128], F32)
    onesrow_b = A("onesrow_b", [1, 128], BF16)
    onescol_b2 = A("onescol_b2", [128, 1], BF16)
    mod_fm = A("mod_fm", [128, 64], F32)
    mod_acc = A("mod_acc", [128, 48], F32)
    modfm_r = A("modfm_r", [128, 64], F32)
    modT = A("modT", [64, 128], BF16)
    rows_r = A("rows_r", [1, 24 * 128], BF16)
    sgn = A("sgn", [128, 16], F32)
    silc = A("silc", [128, DC], F32)
    wqpat = A("wqpat", [16, 1024], BF16)
    wkpat = A("wkpat", [16, 1024], BF16)
    sqpat = A("sqpat", [128, 16 * DC], BF16)
    wqsq = A("wqsq", [128, 16 * DC], BF16)
    wksq = A("wksq", [128, 16 * DC], BF16)
    bias_t = A("bias_t", [128, 16], F32)
    cvec = A("cvec", [16, 4], F32)
    crow = A("crow", [1, 16], F32)
    rwT = A("rwT", [128, 64], BF16)
    probs = A("probs", [128, 8 * TC], F32)
    comb = A("comb", [128, 8 * TC], F32)
    m8 = A("m8", [128, 8], F32)
    tt1 = A("tt1", [128, 8], F32)
    tt2 = A("tt2", [128, 8], F32)
    tt3 = A("tt3", [128, 8], F32)
    gbb = A("gbb", [128, D], F32)

    def modc(v, dc):
        return mod_fm[:, dc * 8 + v: dc * 8 + v + 1]

    def caplus_row(dc):
        return rows_r[:, (0 * 8 + dc) * 128:(0 * 8 + dc + 1) * 128]

    def cbplus_row(dc):
        return rows_r[:, (1 * 8 + dc) * 128:(1 * 8 + dc + 1) * 128]

    def gb_row(dc):
        return rows_r[:, (2 * 8 + dc) * 128:(2 * 8 + dc + 1) * 128]

    s_identb, s_identr, s_eyer = Slot("identb"), Slot("identr"), Slot("eyer")
    s_ones = Slot("ones")
    p.dma("sync", lambda E: E.dma_start(out=identb[:], in_=ident_b), writes=(s_identb,))
    p.dma("sync", lambda E: E.dma_start(out=identr[:], in_=ident_r), writes=(s_identr,))
    p.dma("sync", lambda E: E.dma_start(out=eyer[:], in_=eyerows), writes=(s_eyer,))
    s_selp = Slot("selp")
    p.dma("sync", lambda E: E.dma_start(out=selp[:], in_=selpat), writes=(s_selp,))
    p.op("vector", lambda E: E.memset(onesrow_r[:], 1.0), writes=(s_ones,))
    p.op("vector", lambda E: E.memset(onesrow_b[:], 1.0), writes=(s_ones,))
    p.op("vector", lambda E: E.memset(onescol_b2[:], 1.0), writes=(s_ones,))

    x_fm = ar.alloc("x_fm", [128, DC * T], F32)

    def xfm(dc):
        return x_fm[:, dc * T:(dc + 1) * T]
    s_xfm = [Slot(f"x_fm{dc}") for dc in range(DC)]

    xtm_st = [ar.alloc(f"xtm_st{i}", [128, D], F32) for i in range(3)]
    s_xtm = [Slot(f"xtm{i}") for i in range(3)]
    pT_ctx = [PSUM(f"pT{i}", 128, 512) for i in range(2)]
    pT = [c.__enter__() for c in pT_ctx]
    s_pT = [Slot(f"pT{i}") for i in range(2)]

    for tc in range(TC):
        i = tc % 3
        p.dma("sync",
              lambda E, tc=tc, i=i: E.dma_start(
                  out=xtm_st[i][:], in_=x_in[tc * 128:(tc + 1) * 128, :]),
              writes=(s_xtm[i],))
        for g in range(2):
            pj = (tc * 2 + g) % 2
            emits = []
            for k in range(4):
                dc = g * 4 + k
                emits.append(lambda E, i=i, dc=dc, pj=pj, k=k:
                             E.matmul(pT[pj][:, k * 128:(k + 1) * 128],
                                      xtm_st[i][:, dc * 128:(dc + 1) * 128],
                                      identr[:], start=True, stop=True))
            p.mm_group(emits, reads=(s_xtm[i], s_identr), write=s_pT[pj])

            def ev(E, pj=pj, g=g, tc=tc, xf=x_fm):
                return E.tensor_copy(
                    xf[:].rearrange("p (c t) -> p c t", c=DC)[
                        :, g * 4:(g + 1) * 4, tc * 128:(tc + 1) * 128],
                    pT[pj][:].rearrange("p (c t) -> p c t", c=4))
            p.op("vector", ev, reads=(s_pT[pj],),
                 writes=tuple(s_xfm[g * 4 + k] for k in range(4)))
    for c in reversed(pT_ctx):
        c.__exit__(None, None, None)
    p.barrier()
    dbg["x_fm"] = (s_xfm, xfm)

    wst = [ar.alloc(f"wst{i}", [128, D], BF16) for i in range(3)]
    s_wst = [Slot(f"wst{i}") for i in range(3)]
    for pan in range(3):
        p.dma("gpsimd",
              lambda E, pan=pan: E.dma_start(
                  out=wst[pan][:],
                  in_=qkv_w[2 * D + pan * 128:2 * D + (pan + 1) * 128, :]),
              writes=(s_wst[pan],))

    s_modfm = Slot("mod_fm")
    p.op("vector", lambda E: E.memset(mod_fm[:], 0.0), writes=(s_modfm,))
    c_sb = ar.alloc("c_sb", [1, D], F32)
    csil = ar.alloc("csil", [1, D], F32)
    pb_sb = ar.alloc("pb_sb", [1, D], F32)
    s_c = Slot("c", ar.take_ghosts())
    p.dma("sync", lambda E: E.dma_start(out=c_sb[:], in_=c_row), writes=(s_c,))
    s_csil = Slot("csil", ar.take_ghosts())
    p.op("scalar", lambda E: E.activation(csil[:], c_sb[:], ACT.Silu),
         reads=(s_c,), writes=(s_csil,))
    s_silc = Slot("silc")
    pM_ctx = PSUM("pM", 128, 128)
    pM = pM_ctx.__enter__()
    s_pM = Slot("pM")
    emits = [
        (lambda E, dc=dc: E.matmul(pM[:, dc:dc + 1],
                                   csil[:, dc * 128:(dc + 1) * 128],
                                   onesrow_r[:, 0:1], start=True, stop=True))
        for dc in range(DC)]
    p.mm_group(emits, reads=(s_csil, s_ones), write=s_pM)
    p.op("vector", lambda E: E.tensor_copy(silc[:], pM[:, 0:DC]),
         reads=(s_pM,), writes=(s_silc,))

    wmod_st = [ar.alloc(f"wmod_st{i}", [128, 6 * D], F32) for i in range(2)]
    s_wmod = [Slot(f"wmod{i}") for i in range(2)]
    s_macc = Slot("mod_acc")
    pMD_ctx = PSUM("pMD", 128, 64)
    pMD = pMD_ctx.__enter__()
    s_pMD = Slot("pMD")
    for dci in range(DC):
        i = dci % 2
        p.dma("sync",
              lambda E, dci=dci, i=i: E.dma_start(
                  out=wmod_st[i][:], in_=wmod[dci * 128:(dci + 1) * 128, :]),
              writes=(s_wmod[i],))
        emits = []
        for v in range(6):
            for dc in range(DC):
                emits.append(lambda E, i=i, dci=dci, v=v, dc=dc:
                             E.matmul(pMD[:, dc * 8 + v:dc * 8 + v + 1],
                                      wmod_st[i][:, v * D + dc * 128:v * D + (dc + 1) * 128],
                                      silc[:, dci:dci + 1],
                                      start=True, stop=True))
        p.mm_group(emits, reads=(s_silc, s_wmod[i]), write=s_pMD)
        idx = pMD[:, 0:64].rearrange("p (c v) -> p c v", c=DC)[:, :, 0:6]
        dst = mod_acc[:].rearrange("p (c v) -> p c v", c=DC, v=6)
        if dci == 0:
            p.op("vector", lambda E, idx=idx, dst=dst: E.tensor_copy(dst, idx),
                 reads=(s_pMD,), writes=(s_macc,))
        else:
            p.op("vector", lambda E, idx=idx, dst=dst:
                 E.tensor_tensor(dst, dst, idx, ALU.add),
                 reads=(s_pMD,), writes=(s_macc,))
    ar.free("wmod_st0", slot_toks(s_wmod[0]))
    ar.free("wmod_st1", slot_toks(s_wmod[1]))

    s_pb = Slot("pb", ar.take_ghosts())
    p.dma("sync", lambda E: E.dma_start(out=pb_sb[:], in_=proj_b), writes=(s_pb,))
    p.op("vector",
         lambda E: E.tensor_copy(
             mod_fm[:].rearrange("p (c v) -> p c v", c=DC)[:, :, 0:6],
             mod_acc[:].rearrange("p (c v) -> p c v", c=DC, v=6)),
         reads=(s_macc,), writes=(s_modfm,))
    emits = [(lambda E, dc=dc: E.matmul(pMD[:, dc * 8 + 6:dc * 8 + 7],
                                        pb_sb[:, dc * 128:(dc + 1) * 128],
                                        onesrow_r[:, 0:1], start=True, stop=True))
             for dc in range(DC)]
    p.mm_group(emits, reads=(s_pb, s_ones), write=s_pMD)
    p.op("vector",
         lambda E: E.tensor_copy(
             mod_fm[:].rearrange("p (c v) -> p c v", c=DC)[:, :, 6:7],
             pMD[:, 0:64].rearrange("p (c v) -> p c v", c=DC)[:, :, 6:7]),
         reads=(s_pMD,), writes=(s_modfm,))

    s_sgn = Slot("sgn")

    def sgnv(j):
        return sgn[:, j * 8:(j + 1) * 8].rearrange("p (a b) -> p a b", b=1)
    for j, v in enumerate((2, 5)):
        view = mod_fm[:].rearrange("p (c v) -> p c v", c=DC)[:, :, v:v + 1]
        p.op("vector",
             lambda E, j=j, view=view:
             E.tensor_scalar(sgnv(j), view, 0.0, 2.0, ALU.is_ge, ALU.mult),
             reads=(s_modfm,), writes=(s_sgn,))
        p.op("vector",
             lambda E, j=j: E.tensor_scalar_add(sgnv(j), sgnv(j), -1.0),
             reads=(), writes=(s_sgn,))
        p.op("vector",
             lambda E, j=j, view=view:
             E.tensor_tensor(view, view, sgnv(j), ALU.mult),
             reads=(s_sgn,), writes=(s_modfm,))
        p.op("vector",
             lambda E, view=view:
             E.tensor_scalar_max(view, view, 0.25),
             reads=(), writes=(s_modfm,))
        p.op("vector",
             lambda E, j=j, view=view:
             E.tensor_tensor(view, view, sgnv(j), ALU.mult),
             reads=(s_sgn,), writes=(s_modfm,))
    for v in (1, 4):
        view = mod_fm[:].rearrange("p (c v) -> p c v", c=DC)[:, :, v:v + 1]
        p.op("vector",
             lambda E, view=view: E.tensor_scalar_add(view, view, 1.0),
             reads=(), writes=(s_modfm,))

    s_modT = Slot("modT")
    s_modfm_r = Slot("modfm_r")
    p.op("vector", lambda E: E.tensor_copy(modfm_r[:], mod_fm[:]),
         reads=(s_modfm,), writes=(s_modfm_r,))
    p.mm_group([lambda E: E.matmul(pM[0:64, 0:128], modfm_r[:], identr[:],
                                   start=True, stop=True)],
               reads=(s_modfm_r, s_identr), write=s_pM)
    p.op("vector", lambda E: E.tensor_copy(modT[:], pM[0:64, 0:128]),
         reads=(s_pM,), writes=(s_modT,))
    s_rows = [Slot(f"rows{i}") for i in range(24)]
    for si, v in enumerate((1, 4, 5)):
        for dc in range(DC):
            seg = si * 8 + dc
            p.dma("gpsimd",
                  lambda E, seg=seg, v=v, dc=dc: E.dma_start(
                      out=rows_r[:, seg * 128:(seg + 1) * 128],
                      in_=modT[dc * 8 + v:dc * 8 + v + 1, :]),
                  reads=(s_modT,), writes=(s_rows[seg],))

    wnorm = ar.alloc("wnorm", [1, 10 * 128], F32)
    wn_q = ar.alloc("wn_q", [1, DH], F32)
    wn_k = ar.alloc("wn_k", [1, DH], F32)
    s_wnq, s_wnk = Slot("wn_q", ar.take_ghosts()), Slot("wn_k", ar.take_ghosts())
    p.dma("sync", lambda E: E.dma_start(out=wn_q[:], in_=q_norm_w), writes=(s_wnq,))
    p.dma("sync", lambda E: E.dma_start(out=wn_k[:], in_=k_norm_w), writes=(s_wnk,))
    s_wnorm = Slot("wnorm", ar.take_ghosts())
    p.op("vector", lambda E: E.memset(wnorm[:], 0.0), writes=(s_wnorm,))
    sc = DH ** -0.5

    def wseg(i, hi):
        off = i * 128 + (DH if hi else 0)
        return wnorm[:, off:off + DH]
    p.op("vector", lambda E: E.tensor_scalar_mul(wseg(0, 0), wn_q[:], sc),
         reads=(s_wnq,), writes=(s_wnorm,))
    p.op("vector", lambda E: E.tensor_scalar_mul(wseg(1, 1), wn_q[:], sc),
         reads=(s_wnq,), writes=(s_wnorm,))
    p.op("vector", lambda E: E.tensor_tensor(wseg(2, 0), wseg(0, 0), wseg(0, 0),
                                             ALU.mult),
         reads=(), writes=(s_wnorm,))
    p.op("vector", lambda E: E.tensor_tensor(wseg(3, 1), wseg(0, 0), wseg(0, 0),
                                             ALU.mult),
         reads=(), writes=(s_wnorm,))
    p.op("vector", lambda E: E.tensor_copy(wseg(4, 0), wn_k[:]),
         reads=(s_wnk,), writes=(s_wnorm,))
    p.op("vector", lambda E: E.tensor_copy(wseg(5, 1), wn_k[:]),
         reads=(s_wnk,), writes=(s_wnorm,))
    p.op("vector", lambda E: E.tensor_tensor(wseg(6, 0), wseg(4, 0), wseg(4, 0),
                                             ALU.mult),
         reads=(), writes=(s_wnorm,))
    p.op("vector", lambda E: E.tensor_tensor(wseg(7, 1), wseg(4, 0), wseg(4, 0),
                                             ALU.mult),
         reads=(), writes=(s_wnorm,))
    p.op("vector", lambda E: E.memset(wseg(8, 0), 1.0), reads=(), writes=(s_wnorm,))
    p.op("vector", lambda E: E.memset(wseg(9, 1), 1.0), reads=(), writes=(s_wnorm,))

    s_wqpat, s_wkpat = Slot("wqpat"), Slot("wkpat")
    s_sqpat, s_wqsq, s_wksq = Slot("sqpat"), Slot("wqsq"), Slot("wksq")
    for dst, sl, ra, rb in ((wqpat, s_wqpat, 0, 1), (wkpat, s_wkpat, 4, 5)):
        for c in range(DC):
            emits = [
                (lambda E, c=c, ra=ra:
                 E.matmul(pM[0:16, 0:128], eyer[:, 16 * (2 * c):16 * (2 * c) + 16],
                          wnorm[:, ra * 128:(ra + 1) * 128], start=True, stop=False)),
                (lambda E, c=c, rb=rb:
                 E.matmul(pM[0:16, 0:128],
                          eyer[:, 16 * (2 * c + 1):16 * (2 * c + 1) + 16],
                          wnorm[:, rb * 128:(rb + 1) * 128], start=False, stop=True)),
            ]
            p.mm_group(emits, reads=(s_eyer, s_wnorm), write=s_pM)
            p.op("vector",
                 lambda E, c=c, dst=dst: E.tensor_copy(
                     dst[:, c * 128:(c + 1) * 128], pM[0:16, 0:128]),
                 reads=(s_pM,), writes=(sl,))
    for dst, sl, ra, rb in ((sqpat, s_sqpat, 8, 9), (wqsq, s_wqsq, 2, 3),
                            (wksq, s_wksq, 6, 7)):
        for c in range(DC):
            emits = [
                (lambda E, c=c, ra=ra:
                 E.matmul(pM[:, 0:16], wnorm[:, ra * 128:(ra + 1) * 128],
                          eyer[:, 16 * (2 * c):16 * (2 * c) + 16],
                          start=True, stop=False)),
                (lambda E, c=c, rb=rb:
                 E.matmul(pM[:, 0:16], wnorm[:, rb * 128:(rb + 1) * 128],
                          eyer[:, 16 * (2 * c + 1):16 * (2 * c + 1) + 16],
                          start=False, stop=True)),
            ]
            p.mm_group(emits, reads=(s_eyer, s_wnorm), write=s_pM)
            p.op("vector",
                 lambda E, c=c, dst=dst: E.tensor_copy(
                     dst[:, c * 16:(c + 1) * 16], pM[:, 0:16]),
                 reads=(s_pM,), writes=(sl,))

    pMD_ctx.__exit__(None, None, None)
    pM_ctx.__exit__(None, None, None)
    p.barrier()
    ar.free("wnorm", slot_toks(s_wnorm))
    ar.free("wn_q", slot_toks(s_wnq))
    ar.free("wn_k", slot_toks(s_wnk))
    ar.free("c_sb", slot_toks(s_c))
    ar.free("csil", slot_toks(s_csil))
    ar.free("pb_sb", slot_toks(s_pb))

    h1 = ar.alloc("h1", [128, DC * T], BF16)

    def h1c(dc):
        return h1[:, dc * T:(dc + 1) * T]
    s_h1 = [Slot(f"h1_{dc}", ar.take_ghosts()) for dc in range(DC)]

    def layernorm(xslots, xchunk, lrow_fn, lrow_slot, v_sa, hslots, hchunk, tag):
        ghosts = ar.take_ghosts()
        xsq = ar.alloc(f"xsq_{tag}", [128, T], BF16)
        xb = ar.alloc(f"xb_{tag}", [128, T], BF16)
        statr = ar.alloc(f"statr_{tag}", [1, 4 * T], F32)
        statb = ar.alloc(f"statb_{tag}", [1, 2 * T], BF16)
        htmp = ar.alloc(f"htmp_{tag}", [128, T], F32)
        s_xsq = Slot("xsq", ghosts)
        s_xb = Slot("xb", ghosts)
        s_stat = Slot("statrows", ghosts)
        s_statb = Slot("statb", ghosts)
        s_htmp = Slot("htmp", ghosts)
        pS_ctx = [PSUM(f"pS{tag}{i}", 1, T) for i in range(2)]
        pS = [c.__enter__() for c in pS_ctx]
        s_pS = [Slot(f"pS{i}") for i in range(2)]
        for dc in range(DC):
            p.op("vector",
                 lambda E, dc=dc: E.tensor_copy(xb[:], xchunk(dc)),
                 reads=(xslots[dc],), writes=(s_xb,))
            p.op("scalar",
                 lambda E, dc=dc: E.activation(xsq[:], xchunk(dc), ACT.Square),
                 reads=(xslots[dc],), writes=(s_xsq,))
            for which in (0, 1):
                emits = []
                for f0, fl in fslices(T):
                    emits.append(
                        lambda E, which=which, dc=dc, f0=f0, fl=fl:
                        E.matmul(pS[which][:, f0:f0 + fl], onescol_b2[:],
                                 (xb if which == 0 else xsq)[:, f0:f0 + fl],
                                 start=(dc == 0), stop=(dc == DC - 1)))
                rd = (s_ones, s_xb) if which == 0 else (s_ones, s_xsq)
                if dc == DC - 1:
                    p.mm_group(emits, reads=rd, write=s_pS[which])
                else:
                    for e in emits[:-1]:
                        p.op("tensor", e, reads=rd,
                             war=(s_pS[which],) if dc == 0 else (), inc=False)
                    p.op("tensor", emits[-1], reads=rd,
                         war=(s_pS[which],) if dc == 0 else (), inc=True)
        r_rstd = statr[:, 0:T]
        r_mrstd = statr[:, T:2 * T]
        tmp0 = statr[:, 2 * T:3 * T]
        tmp1 = statr[:, 3 * T:4 * T]
        p.op("vector", lambda E: E.tensor_scalar_mul(tmp0, pS[0][:, 0:T], 1.0 / D),
             reads=(s_pS[0],), writes=(s_stat,))
        p.op("vector", lambda E: E.tensor_scalar_mul(tmp1, pS[1][:, 0:T], 1.0 / D),
             reads=(s_pS[1],), writes=(s_stat,))
        p.op("vector", lambda E: E.tensor_tensor(r_mrstd, tmp0, tmp0, ALU.mult),
             reads=(), writes=(s_stat,))
        p.op("vector", lambda E: E.scalar_tensor_tensor(tmp1, tmp1, LN_EPS, r_mrstd,
                                                        ALU.add, ALU.subtract),
             reads=(), writes=(s_stat,))
        p.op("vector", lambda E: E.reciprocal(tmp1, tmp1), reads=(), writes=(s_stat,))
        p.op("scalar", lambda E: E.activation(r_rstd, tmp1, ACT.Sqrt),
             reads=(s_stat,), writes=(s_stat,))
        p.op("vector", lambda E: E.tensor_tensor(r_mrstd, tmp0, r_rstd, ALU.mult),
             reads=(), writes=(s_stat,))
        p.op("vector", lambda E: E.tensor_copy(statb[:, 0:T], r_rstd),
             reads=(s_stat,), writes=(s_statb,))
        p.op("vector", lambda E: E.tensor_copy(statb[:, T:2 * T], r_mrstd),
             reads=(s_stat,), writes=(s_statb,))
        for c in reversed(pS_ctx):
            c.__exit__(None, None, None)
        p.barrier()
        pA_ctx = PSUM(f"pA{tag}", 128, T)
        pB_ctx = PSUM(f"pB{tag}", 128, T)
        pA_, pB_ = pA_ctx.__enter__(), pB_ctx.__enter__()
        s_pA, s_pB = Slot("pA"), Slot("pB")
        for dc in range(DC):
            lrow = lrow_fn(dc)
            emA, emB = [], []
            for f0, fl in fslices(T):
                emA.append(lambda E, lrow=lrow, f0=f0, fl=fl:
                           E.matmul(pA_[:, f0:f0 + fl], lrow,
                                    statb[0:1, f0:f0 + fl], start=True, stop=True))
                emB.append(lambda E, lrow=lrow, f0=f0, fl=fl:
                           E.matmul(pB_[:, f0:f0 + fl], lrow,
                                    statb[0:1, T + f0:T + f0 + fl],
                                    start=True, stop=True))
            p.mm_group(emA, reads=(lrow_slot(dc), s_statb), write=s_pA)
            p.mm_group(emB, reads=(lrow_slot(dc), s_statb), write=s_pB)
            p.op("vector",
                 lambda E, dc=dc: E.tensor_tensor(htmp[:], xchunk(dc), pA_[:, 0:T],
                                                  ALU.mult),
                 reads=(xslots[dc], s_pA), writes=(s_htmp,))
            p.op("vector",
                 lambda E, dc=dc:
                 E.scalar_tensor_tensor(hchunk(dc), htmp[:], modc(v_sa, dc),
                                        pB_[:, 0:T], ALU.add, ALU.subtract),
                 reads=(s_htmp, s_modfm, s_pB), writes=(hslots[dc],))
        pB_ctx.__exit__(None, None, None)
        pA_ctx.__exit__(None, None, None)
        p.barrier()
        ar.free(f"xsq_{tag}", slot_toks(s_xsq))
        ar.free(f"xb_{tag}", slot_toks(s_xb))
        ar.free(f"statr_{tag}", slot_toks(s_stat))
        ar.free(f"statb_{tag}", slot_toks(s_statb))
        ar.free(f"htmp_{tag}", slot_toks(s_htmp))

    layernorm(s_xfm, xfm, caplus_row, lambda dc: s_rows[dc], 0, s_h1, h1c, "ln1")
    dbg["h1"] = (s_h1, h1c)
    if BUILD_UPTO < 3:
        return _finish(nc, p, ar, dbg, out, identb, identr, s_identb, s_identr)

    tok_xoff = []
    for dc in range(DC):
        tok_xoff.append(
            p.dma("sync",
                  lambda E, dc=dc: E.dma_start(out=xfm_d[:, dc * T:(dc + 1) * T],
                                               in_=xfm(dc)),
                  reads=(s_xfm[dc],)))
    ar.free("x_fm", slot_toks(*s_xfm))
    ar.free("xtm_st0", [])
    ar.free("xtm_st1", [])
    ar.free("xtm_st2", [])

    ghosts = ar.take_ghosts()
    v_tm = ar.alloc("v_tm", [128, TC * H * 65], BF16)
    s_vtm = [Slot(f"v_tm{tc}", ghosts) for tc in range(TC)]

    def vtm(tc):
        return v_tm[:, tc * H * 65:(tc + 1) * H * 65]
    s_vones = Slot("vones", ghosts)
    p.op("vector", lambda E: E.memset(
        v_tm[:].rearrange("p (a e) -> p a e", e=65)[:, :, 64:65], 1.0),
        writes=(s_vones,))

    pW_ctx = PSUM("pW", 128, 512)
    pW = pW_ctx.__enter__()
    s_pW = Slot("pW")

    def prep_wT(mat_off, dstap, dst_slots, preloaded=0):
        for pan in range(8):
            i = pan % 3
            if pan >= preloaded:
                p.dma("gpsimd",
                      lambda E, pan=pan, i=i, mat_off=mat_off: E.dma_start(
                          out=wst[i][:],
                          in_=qkv_w[mat_off + pan * 128:
                                    mat_off + (pan + 1) * 128, :]),
                      writes=(s_wst[i],))
            for g in range(2):
                emits = []
                for k in range(4):
                    dc = g * 4 + k
                    emits.append(lambda E, i=i, dc=dc, k=k:
                                 E.matmul(pW[:, k * 128:(k + 1) * 128],
                                          wst[i][:, dc * 128:(dc + 1) * 128],
                                          identb[:], start=True, stop=True))
                p.mm_group(emits, reads=(s_wst[i], s_identb), write=s_pW)

                def ev(E, g=g, pan=pan, dstap=dstap):
                    return E.tensor_copy(
                        dstap.rearrange("p (c j) -> p c j", c=DC)[
                            :, g * 4:(g + 1) * 4, pan * 128:(pan + 1) * 128],
                        pW[:].rearrange("p (c j) -> p c j", c=4))
                p.op("vector", ev, reads=(s_pW,),
                     writes=tuple(dst_slots[g * 4 + k] for k in range(4)))

    wvT = ar.alloc("wvT", [128, DC * D], BF16)
    s_wvT = [Slot(f"wvT{dc}", ghosts) for dc in range(DC)]
    prep_wT(2 * D, wvT, s_wvT, preloaded=3)
    pV_ctx = [PSUM(f"pV{i}", 128, 1024) for i in range(2)]
    pV = [c.__enter__() for c in pV_ctx]
    s_pV = [Slot(f"pV{i}") for i in range(2)]
    for tc in range(TC):
        i = tc % 2
        emits = []
        for dc in range(DC):
            for j0, jl in fslices(D):
                emits.append(lambda E, tc=tc, dc=dc, j0=j0, jl=jl, i=i:
                             E.matmul(pV[i][:, j0:j0 + jl],
                                      h1c(dc)[:, tc * 128:(tc + 1) * 128],
                                      wvT[:, dc * D + j0:dc * D + j0 + jl],
                                      start=(dc == 0), stop=(dc == DC - 1)))
        p.mm_group(emits, reads=tuple(s_h1) + tuple(s_wvT), write=s_pV[i])

        def ev(E, tc=tc, i=i):
            return E.tensor_copy(
                vtm(tc).rearrange("p (h e) -> p h e", e=65)[:, :, 0:64],
                pV[i][:].rearrange("p (h e) -> p h e", e=64))
        p.op("vector", ev, reads=(s_pV[i],), writes=(s_vtm[tc],))
    for c in reversed(pV_ctx):
        c.__exit__(None, None, None)
    p.barrier()
    ar.free("wvT", slot_toks(*s_wvT))

    qpk = ar.alloc("qpk", [128, DC * T], BF16)
    kpk = ar.alloc("kpk", [128, DC * T], BF16)

    def qpkc(c):
        return qpk[:, c * T:(c + 1) * T]

    def kpkc(c):
        return kpk[:, c * T:(c + 1) * T]
    ghosts = ar.take_ghosts()
    s_qpk = [Slot(f"qpk{c}", ghosts) for c in range(DC)]
    s_kpk = [Slot(f"kpk{c}", ghosts) for c in range(DC)]
    wqT = ar.alloc("wqT", [128, DC * D], BF16)
    s_wqT = [Slot(f"wqT{dc}", ghosts) for dc in range(DC)]
    prep_wT(0, wqT, s_wqT)

    pQ_ctx = [PSUM(f"pQ{i}", 128, 512) for i in range(2)]
    pQ = [c.__enter__() for c in pQ_ctx]
    s_pQ = [Slot(f"pQ{i}") for i in range(2)]
    qkn = [0]

    def qk_prod(wT, s_wT, dst, s_dst):
        for c in range(DC):
            for f0, fl in fslices(T):
                i = qkn[0] % 2
                qkn[0] += 1
                emits = []
                for dc in range(DC):
                    emits.append(lambda E, c=c, dc=dc, f0=f0, fl=fl, i=i, wT=wT:
                                 E.matmul(pQ[i][:, 0:fl],
                                          wT[:, dc * D + c * 128:dc * D + (c + 1) * 128],
                                          h1c(dc)[:, f0:f0 + fl],
                                          start=(dc == 0), stop=(dc == DC - 1)))
                p.mm_group(emits, reads=tuple(s_h1) + tuple(s_wT), write=s_pQ[i])
                p.op("vector",
                     lambda E, c=c, f0=f0, fl=fl, i=i, dst=dst:
                     E.tensor_copy(dst(c)[:, f0:f0 + fl], pQ[i][:, 0:fl]),
                     reads=(s_pQ[i],), writes=(s_dst[c],))

    qk_prod(wqT, s_wqT, qpkc, s_qpk)
    ar.free("wqT", slot_toks(*s_wqT))
    wkT = ar.alloc("wkT", [128, DC * D], BF16)
    s_wkT = [Slot(f"wkT{dc}", ar.take_ghosts()) for dc in range(DC)]
    prep_wT(D, wkT, s_wkT)
    qk_prod(wkT, s_wkT, kpkc, s_kpk)
    for c in reversed(pQ_ctx):
        c.__exit__(None, None, None)
    pW_ctx.__exit__(None, None, None)
    p.barrier()
    ar.free("wkT", slot_toks(*s_wkT))
    ar.free("wst0", slot_toks(s_wst[0]))
    ar.free("wst1", slot_toks(s_wst[1]))
    ar.free("wst2", slot_toks(s_wst[2]))
    ar.free("h1", slot_toks(*s_h1))
    dbg["qpk"] = (s_qpk, qpkc)
    dbg["kpk"] = (s_kpk, kpkc)
    if BUILD_UPTO < 4:
        return _finish(nc, p, ar, dbg, out, identb, identr, s_identb, s_identr)

    ghosts = ar.take_ghosts()
    qsq = ar.alloc("qsq", [128, T], BF16)
    stat = ar.alloc("stat", [16, 4 * T], F32)
    statbh = ar.alloc("statbh", [16, T], BF16)
    s_qsq = Slot("qsq", ghosts)
    s_stat = Slot("stat4", ghosts)
    s_statbh = Slot("statbh", ghosts)
    r_rstd_q = stat[:, 0 * T:1 * T]
    r_rstd_k = stat[:, 1 * T:2 * T]
    r_tmp = stat[:, 2 * T:3 * T]
    r_tmp2 = stat[:, 3 * T:4 * T]
    s_bias = Slot("bias_t")
    s_cvec = Slot("cvec")
    s_crow = Slot("crow")

    for side, (pkc, s_pk, wpat, s_wpat, wsqp, s_wsqp, r_rstd, cvcol) in enumerate((
            (qpkc, s_qpk, wqpat, s_wqpat, wqsq, s_wqsq, r_rstd_q, 0),
            (kpkc, s_kpk, wkpat, s_wkpat, wksq, s_wksq, r_rstd_k, 1))):
        pSS_ctx = [PSUM(f"pSS{side}_{i}", 16, T) for i in range(2)]
        pSS = [c.__enter__() for c in pSS_ctx]
        s_pSS = [Slot(f"pSS{i}") for i in range(2)]
        for c in range(DC):
            p.op("vector",
                 lambda E, c=c, pkc=pkc: E.tensor_tensor(qsq[:], pkc(c), pkc(c),
                                                         ALU.mult),
                 reads=(s_pk[c],), writes=(s_qsq,))
            for which, pat, s_pat in ((0, sqpat, s_sqpat), (1, wsqp, s_wsqp)):
                emits = []
                for f0, fl in fslices(T):
                    emits.append(lambda E, c=c, f0=f0, fl=fl, which=which, pat=pat:
                                 E.matmul(pSS[which][:, f0:f0 + fl],
                                          pat[:, c * 16:(c + 1) * 16],
                                          qsq[:, f0:f0 + fl],
                                          start=(c == 0), stop=(c == DC - 1)))
                rd = (s_qsq, s_pat)
                if c == DC - 1:
                    p.mm_group(emits, reads=rd, write=s_pSS[which])
                else:
                    for e in emits[:-1]:
                        p.op("tensor", e, reads=rd,
                             war=(s_pSS[which],) if c == 0 else (), inc=False)
                    p.op("tensor", emits[-1], reads=rd,
                         war=(s_pSS[which],) if c == 0 else (), inc=True)
        p.op("vector", lambda E, r_rstd=r_rstd:
             E.tensor_scalar(r_rstd, pSS[0][:, 0:T], 1.0 / DH, RMS_EPS,
                             ALU.mult, ALU.add),
             reads=(s_pSS[0],), writes=(s_stat,))
        p.op("vector", lambda E, r_rstd=r_rstd: E.reciprocal(r_rstd, r_rstd),
             reads=(), writes=(s_stat,))
        p.op("scalar", lambda E, r_rstd=r_rstd: E.activation(r_rstd, r_rstd,
                                                             ACT.Sqrt),
             reads=(s_stat,), writes=(s_stat,))
        p.op("vector", lambda E, r_rstd=r_rstd:
             E.tensor_tensor(r_tmp, r_rstd, r_rstd, ALU.mult),
             reads=(), writes=(s_stat,))
        p.op("vector", lambda E:
             E.tensor_tensor(r_tmp2, pSS[1][:, 0:T], r_tmp, ALU.mult),
             reads=(s_pSS[1],), writes=(s_stat,))
        p.op("vector", lambda E, cvcol=cvcol:
             E.reduce_max(cvec[:, cvcol:cvcol + 1], r_tmp2, axis=AX),
             reads=(s_stat,), writes=(s_cvec,))
        for c_ in reversed(pSS_ctx):
            c_.__exit__(None, None, None)
        p.barrier()
        pRB_ctx = PSUM(f"pRB{side}", 128, T)
        pRB = pRB_ctx.__enter__()
        s_pRB = Slot("pRB")
        p.op("vector", lambda E, r_rstd=r_rstd: E.tensor_copy(statbh[:, 0:T], r_rstd),
             reads=(s_stat,), writes=(s_statbh,))
        for c in range(DC):
            emits = []
            for f0, fl in fslices(T):
                emits.append(lambda E, c=c, f0=f0, fl=fl, wpat=wpat:
                             E.matmul(pRB[:, f0:f0 + fl],
                                      wpat[:, c * 128:(c + 1) * 128],
                                      statbh[0:16, f0:f0 + fl],
                                      start=True, stop=True))
            p.mm_group(emits, reads=(s_wpat, s_statbh), write=s_pRB)
            p.op("vector",
                 lambda E, c=c, pkc=pkc: E.tensor_tensor(pkc(c), pkc(c),
                                                         pRB[:, 0:T], ALU.mult),
                 reads=(s_pRB,), writes=(s_pk[c],))
        pRB_ctx.__exit__(None, None, None)
        p.barrier()
    p.op("vector", lambda E: E.tensor_tensor(cvec[:, 2:3], cvec[:, 0:1],
                                             cvec[:, 1:2], ALU.mult),
         reads=(s_cvec,), writes=(s_cvec,))
    p.op("scalar", lambda E: E.activation(cvec[:, 3:4], cvec[:, 2:3], ACT.Sqrt),
         reads=(s_cvec,), writes=(s_cvec,))
    pC_ctx = PSUM("pC", 128, 32)
    pC = pC_ctx.__enter__()
    s_pC = Slot("pC")
    cvr = ar.alloc("cvr", [16, 4], F32)
    s_cvr = Slot("cvr", ghosts)
    p.op("vector", lambda E: E.tensor_copy(cvr[:], cvec[:]),
         reads=(s_cvec,), writes=(s_cvr,))
    p.mm_group([lambda E: E.matmul(pC[0:1, 0:16], cvr[:, 3:4],
                                   identr[0:16, 0:16], start=True, stop=True)],
               reads=(s_cvr, s_identr), write=s_pC)
    p.op("vector", lambda E: E.tensor_scalar_mul(crow[:], pC[0:1, 0:16], -1.0),
         reads=(s_pC,), writes=(s_crow,))
    crowr = ar.alloc("crowr", [1, 16], F32)
    s_crowr = Slot("crowr", ghosts)
    p.op("vector", lambda E: E.tensor_copy(crowr[:], crow[:]),
         reads=(s_crow,), writes=(s_crowr,))
    p.mm_group([lambda E: E.matmul(pC[:, 128:144], onesrow_r[:], crowr[:, 0:16],
                                   start=True, stop=True)],
               reads=(s_crowr, s_ones), write=s_pC)
    p.op("vector", lambda E: E.tensor_copy(bias_t[:], pC[:, 128:144]),
         reads=(s_pC,), writes=(s_bias,))
    pC_ctx.__exit__(None, None, None)
    p.barrier()
    ar.free("qsq", slot_toks(s_qsq))
    ar.free("stat", slot_toks(s_stat))
    ar.free("statbh", slot_toks(s_statbh))
    ar.free("cvr", slot_toks(s_cvr))
    ar.free("crowr", slot_toks(s_crowr))
    dbg["qn"] = (s_qpk, qpkc)
    dbg["kn"] = (s_kpk, kpkc)
    if BUILD_UPTO < 5:
        return _finish(nc, p, ar, dbg, out, identb, identr, s_identb, s_identr)

    s_w1bf = [Slot(f"w1bf{g}") for g in range(24)]
    s_w2bf = [Slot(f"w2bf{q}") for q in range(24)]

    def stage_chunk(k):
        if k < 24:
            p.dma("gpsimd",
                  lambda E, g=k: E.dma_start(
                      out=w1bf[:, g * 4096:(g + 1) * 4096],
                      in_=w1s[:, g * 4096:(g + 1) * 4096]),
                  writes=(s_w1bf[k],))
        elif k < 48:
            p.dma("gpsimd",
                  lambda E, q=k - 24: E.dma_start(
                      out=w2bf[:, q * 4096:(q + 1) * 4096],
                      in_=w2s[:, q * 4096:(q + 1) * 4096]),
                  writes=(s_w2bf[k - 24],))
    ghosts = ar.take_ghosts()
    attn = ar.alloc("attn", [128, DC * T], BF16, end=True)

    def attnc(c):
        return attn[:, c * T:(c + 1) * T]
    s_attn = [Slot(f"attn{c}", ghosts) for c in range(DC)]
    qx = [ar.alloc(f"qx{par}", [128, T], BF16) for par in range(2)]
    s_qx = [Slot(f"qx{par}", ghosts) for par in range(2)]
    p.op("vector", lambda E: E.memset(qx[0][64:128, :], 0.0), writes=(s_qx[0],))
    p.op("vector", lambda E: E.memset(qx[1][0:64, :], 0.0), writes=(s_qx[1],))
    pt = [ar.alloc(f"pt{i}", [128, 1024], BF16) for i in range(3)]
    s_pt = [Slot(f"pt{i}", ghosts) for i in range(3)]
    ast = [ar.alloc(f"ast{i}", [64, T], BF16) for i in range(2)]
    s_ast = [Slot(f"ast{i}", ghosts) for i in range(2)]
    dtmp = [ar.alloc(f"dtmp{i}", [1, T], BF16) for i in range(2)]
    s_dtmp = [Slot(f"dtmp{i}", ghosts) for i in range(2)]
    dnm = ar.alloc("dnm", [16, T], BF16)
    s_dnm = [Slot(f"dnm{h}", ghosts) for h in range(H)]
    dnr = ar.alloc("dnr", [16, T], BF16)
    s_dnr = Slot("dnr", ghosts)

    psp_ctx = [PSUM(f"psp{i}", 128, 1024) for i in range(2)]
    psp = [c.__enter__() for c in psp_ctx]
    s_psp = [Slot(f"psp{i}") for i in range(2)]
    ppv_ctx = PSUM("ppv", 65, T)
    ppv = ppv_ctx.__enter__()

    SPW = min(SPW_CAP, T)
    NHH = T // SPW
    s_ppv = [Slot(f"ppv{nh}") for nh in range(NHH)]

    def qcopy(h):
        c, par = h // 2, h % 2
        p.dma("sync",
              lambda E, c=c, par=par: E.dma_start(
                  out=qx[par][par * 64:(par + 1) * 64, 0:T],
                  in_=qpkc(c)[par * 64:(par + 1) * 64, 0:T]),
              reads=(s_qpk[c],), writes=(s_qx[par],))

    def flush_pv(h, mc, nh, j):
        emits = []
        for f0, fl in fslices(SPW):
            emits.append(
                lambda E, mc=mc, nh=nh, j=j, h=h, f0=f0, fl=fl:
                E.matmul(ppv[0:65, nh * SPW + f0:nh * SPW + f0 + fl],
                         vtm(mc)[:, h * 65:(h + 1) * 65],
                         pt[j][:, f0:f0 + fl],
                         start=(mc == 0), stop=(mc == TC - 1)))
        rd = (s_pt[j], s_vtm[mc])
        if mc == TC - 1:
            p.mm_group(emits, reads=rd, write=s_ppv[nh])
        else:
            for e in emits[:-1]:
                p.op("tensor", e, reads=rd,
                     war=(s_ppv[nh],) if mc == 0 else (), inc=False)
            p.op("tensor", emits[-1], reads=rd,
                 war=(s_ppv[nh],) if mc == 0 else (), inc=True)

    qcopy(0)
    sub = 0
    for h in range(H):
        c, par = h // 2, h % 2
        if h + 1 < H:
            qcopy(h + 1)
        for k in range(3 * h, 3 * h + 3):
            stage_chunk(k)
        pend = None
        for mc in range(TC):
            for nh in range(NHH):
                i = sub % 2
                j = sub % 3
                sub += 1
                emits = []
                for f0, fl in fslices(SPW):
                    emits.append(
                        lambda E, c=c, mc=mc, nh=nh, f0=f0, fl=fl, i=i, par=par:
                        E.matmul(psp[i][:, f0:f0 + fl],
                                 kpkc(c)[:, mc * 128:(mc + 1) * 128],
                                 qx[par][:, nh * SPW + f0:nh * SPW + f0 + fl],
                                 start=True, stop=True))
                p.mm_group(emits, reads=(s_kpk[c], s_qx[par]), write=s_psp[i])
                p.op("scalar",
                     lambda E, i=i, j=j, h=h: E.activation(
                         pt[j][:, 0:SPW], psp[i][:, 0:SPW], ACT.Exp,
                         bias=bias_t[:, h:h + 1], scale=1.0),
                     reads=(s_psp[i], s_bias), writes=(s_pt[j],))
                if pend is not None:
                    flush_pv(*pend)
                pend = (h, mc, nh, j)
        flush_pv(*pend)
        di = h % 2
        p.op("scalar", lambda E, di=di: E.activation(dtmp[di][:, 0:T],
                                                     ppv[64:65, 0:T], ACT.Copy),
             reads=tuple(s_ppv), writes=(s_dtmp[di],))
        p.dma("sync",
              lambda E, h=h, di=di: E.dma_start(out=dnm[h:h + 1, 0:T],
                                                in_=dtmp[di][:, 0:T]),
              reads=(s_dtmp[di],), writes=(s_dnm[h],))
        if par == 0:
            for nh in range(NHH):
                p.op("vector",
                     lambda E, nh=nh, c=c: E.tensor_copy(
                         attnc(c)[0:64, nh * SPW:(nh + 1) * SPW],
                         ppv[0:64, nh * SPW:(nh + 1) * SPW]),
                     reads=(s_ppv[nh],), writes=(s_attn[c],))
        else:
            ai = (h // 2) % 2
            for nh in range(NHH):
                p.op("vector",
                     lambda E, nh=nh, ai=ai: E.tensor_copy(
                         ast[ai][0:64, nh * SPW:(nh + 1) * SPW],
                         ppv[0:64, nh * SPW:(nh + 1) * SPW]),
                     reads=(s_ppv[nh],), writes=(s_ast[ai],))
            p.dma("sync",
                  lambda E, c=c, ai=ai: E.dma_start(out=attnc(c)[64:128, :],
                                                    in_=ast[ai][0:64, :]),
                  reads=(s_ast[ai],), writes=(s_attn[c],))
    ppv_ctx.__exit__(None, None, None)
    for c in reversed(psp_ctx):
        c.__exit__(None, None, None)
    p.op("vector", lambda E: E.reciprocal(dnr[0:16, 0:T], dnm[0:16, 0:T]),
         reads=tuple(s_dnm), writes=(s_dnr,))
    pbc_ctx = [PSUM(f"pbc{i}", 128, SPW) for i in range(2)]
    pbc = [c.__enter__() for c in pbc_ctx]
    s_pbc = [Slot(f"pbc{i}") for i in range(2)]
    nn = 0
    for c in range(DC):
        for nh in range(NHH):
            i = nn % 2
            nn += 1
            emits = []
            for f0, fl in fslices(SPW):
                emits.append(lambda E, c=c, nh=nh, f0=f0, fl=fl, i=i:
                             E.matmul(pbc[i][:, f0:f0 + fl],
                                      selp[0:16, c * 128:(c + 1) * 128],
                                      dnr[0:16, nh * SPW + f0:nh * SPW + f0 + fl],
                                      start=True, stop=True))
            p.mm_group(emits, reads=(s_dnr, s_selp), write=s_pbc[i])
            p.op("vector",
                 lambda E, c=c, nh=nh, i=i: E.tensor_tensor(
                     attnc(c)[:, nh * SPW:(nh + 1) * SPW],
                     attnc(c)[:, nh * SPW:(nh + 1) * SPW],
                     pbc[i][:, 0:SPW], ALU.mult),
                 reads=(s_pbc[i],), writes=(s_attn[c],))
    for ctx in reversed(pbc_ctx):
        ctx.__exit__(None, None, None)
    p.barrier()
    ar.free("qpk", slot_toks(*s_qpk))
    ar.free("kpk", slot_toks(*s_kpk))
    ar.free("v_tm", slot_toks(*s_vtm) + slot_toks(s_vones))
    ar.free("qx0", slot_toks(s_qx[0]))
    ar.free("qx1", slot_toks(s_qx[1]))
    ar.free("pt0", slot_toks(s_pt[0]))
    ar.free("pt1", slot_toks(s_pt[1]))
    ar.free("pt2", slot_toks(s_pt[2]))
    ar.free("ast0", slot_toks(s_ast[0]))
    ar.free("ast1", slot_toks(s_ast[1]))
    ar.free("dtmp0", slot_toks(s_dtmp[0]))
    ar.free("dtmp1", slot_toks(s_dtmp[1]))
    ar.free("dnm", slot_toks(*s_dnm))
    ar.free("dnr", slot_toks(s_dnr))
    dbg["attn"] = (s_attn, attnc)
    if BUILD_UPTO < 6:
        return _finish(nc, p, ar, dbg, out, identb, identr, s_identb, s_identr)

    ghosts = ar.take_ghosts()
    pwT = ar.alloc("pwT", [128, DC * D], BF16, end=True)
    s_pwT = [Slot(f"pwT{dc}", ghosts) for dc in range(DC)]
    wst2 = [ar.alloc(f"wst2_{i}", [128, D], BF16, end=True)
            for i in range(3)]
    s_wst2 = [Slot(f"wst2_{i}", ghosts) for i in range(3)]
    xb2 = ar.alloc("x_fm2", [128, DC * T], F32)

    def xfm2(dc, xb2=xb2):
        return xb2[:, dc * T:(dc + 1) * T]
    s_xfm2 = [Slot(f"x2_{dc}", ghosts + tok_xoff) for dc in range(DC)]
    for dc in range(DC):
        p.dma("sync",
              lambda E, dc=dc: E.dma_start(out=xfm2(dc),
                                           in_=xfm_d[:, dc * T:(dc + 1) * T]),
              writes=(s_xfm2[dc],))

    pW2_ctx = PSUM("pW2", 128, 512)
    pW2 = pW2_ctx.__enter__()
    s_pW2 = Slot("pW2")
    for pan in range(8):
        i = pan % 3
        p.dma("gpsimd",
              lambda E, pan=pan, i=i: E.dma_start(
                  out=wst2[i][:], in_=proj_w[pan * 128:(pan + 1) * 128, :]),
              writes=(s_wst2[i],))
        for g in range(2):
            emits = []
            for k in range(4):
                dc = g * 4 + k
                emits.append(lambda E, i=i, dc=dc, k=k:
                             E.matmul(pW2[:, k * 128:(k + 1) * 128],
                                      wst2[i][:, dc * 128:(dc + 1) * 128],
                                      identb[:], start=True, stop=True))
            p.mm_group(emits, reads=(s_wst2[i], s_identb), write=s_pW2)

            def ev(E, g=g, pan=pan):
                return E.tensor_copy(
                    pwT[:].rearrange("p (c j) -> p c j", c=DC)[
                        :, g * 4:(g + 1) * 4, pan * 128:(pan + 1) * 128],
                    pW2[:].rearrange("p (c j) -> p c j", c=4))
            p.op("vector", ev, reads=(s_pW2,),
                 writes=tuple(s_pwT[g * 4 + k] for k in range(4)))
    pW2_ctx.__exit__(None, None, None)
    p.barrier()

    ptmp = ar.alloc("ptmp", [128, 512], F32, end=True)
    s_ptmp = Slot("ptmp", ghosts)
    pP_ctx = [PSUM(f"pP{i}", 128, 512) for i in range(2)]
    pP = [c.__enter__() for c in pP_ctx]
    s_pP = [Slot(f"pP{i}") for i in range(2)]
    n2 = 0
    for jc in range(DC):
        for f0, fl in fslices(T):
            i = n2 % 2
            n2 += 1
            emits = []
            for dc in range(DC):
                emits.append(lambda E, jc=jc, dc=dc, f0=f0, fl=fl, i=i:
                             E.matmul(pP[i][:, 0:fl],
                                      pwT[:, dc * D + jc * 128:dc * D + (jc + 1) * 128],
                                      attnc(dc)[:, f0:f0 + fl],
                                      start=(dc == 0), stop=(dc == DC - 1)))
            p.mm_group(emits, reads=tuple(s_attn) + tuple(s_pwT), write=s_pP[i])
            p.op("vector",
                 lambda E, jc=jc, f0=f0, fl=fl, i=i:
                 E.tensor_scalar(ptmp[:, 0:fl], pP[i][:, 0:fl],
                                 modc(6, jc), modc(2, jc), ALU.add, ALU.mult),
                 reads=(s_pP[i], s_modfm), writes=(s_ptmp,))
            p.op("vector",
                 lambda E, jc=jc, f0=f0, fl=fl:
                 E.tensor_tensor(xfm2(jc)[:, f0:f0 + fl], xfm2(jc)[:, f0:f0 + fl],
                                 ptmp[:, 0:fl], ALU.add),
                 reads=(s_ptmp,), writes=(s_xfm2[jc],))
    for c in reversed(pP_ctx):
        c.__exit__(None, None, None)
    p.barrier()
    ar.free("attn", slot_toks(*s_attn))
    ar.free("pwT", slot_toks(*s_pwT))
    ar.free("wst2_0", slot_toks(s_wst2[0]))
    ar.free("wst2_1", slot_toks(s_wst2[1]))
    ar.free("wst2_2", slot_toks(s_wst2[2]))
    ar.free("ptmp", slot_toks(s_ptmp))
    dbg["x1"] = (s_xfm2, xfm2)
    if BUILD_UPTO < 7:
        return _finish(nc, p, ar, dbg, out, identb, identr, s_identb, s_identr)

    h2 = ar.alloc("h2", [128, DC * T], BF16, end=True)

    def h2c(dc):
        return h2[:, dc * T:(dc + 1) * T]
    s_h2 = [Slot(f"h2_{dc}", ar.take_ghosts()) for dc in range(DC)]
    layernorm(s_xfm2, xfm2, cbplus_row, lambda dc: s_rows[8 + dc], 3, s_h2, h2c,
              "ln2")
    dbg["h2"] = (s_h2, h2c)

    ghosts = ar.take_ghosts()
    x1st = [ar.alloc(f"x1st{i}", [128, D], F32) for i in range(2)]
    s_x1st = [Slot(f"x1st{i}", ghosts) for i in range(2)]
    pX_ctx = PSUM("pX", 128, 1024)
    pX = pX_ctx.__enter__()
    s_pX = Slot("pX")
    tok_x1 = []
    for tc in range(TC):
        i = tc % 2
        emits = []
        for dc in range(DC):
            emits.append(lambda E, dc=dc, tc=tc:
                         E.matmul(pX[:, dc * 128:(dc + 1) * 128],
                                  xfm2(dc)[:, tc * 128:(tc + 1) * 128],
                                  identr[:], start=True, stop=True))
        p.mm_group(emits, reads=tuple(s_xfm2) + (s_identr,), write=s_pX)
        p.op("vector", lambda E, i=i: E.tensor_copy(x1st[i][:], pX[:]),
             reads=(s_pX,), writes=(s_x1st[i],))
        tok_x1.append(
            p.dma("sync",
                  lambda E, tc=tc, i=i: E.dma_start(
                      out=x1tm_d[tc * 128:(tc + 1) * 128, :], in_=x1st[i][:]),
                  reads=(s_x1st[i],)))
    pX_ctx.__exit__(None, None, None)
    p.barrier()
    ar.free("x_fm2", slot_toks(*s_xfm2))
    ar.free("x1st0", slot_toks(s_x1st[0]))
    ar.free("x1st1", slot_toks(s_x1st[1]))
    if BUILD_UPTO < 8:
        return _finish(nc, p, ar, dbg, out, identb, identr, s_identb, s_identr)

    ghosts = ar.take_ghosts()
    s_rwT = Slot("rwT")
    rw_b = ar.alloc("rw_b", [128, D], BF16)
    s_rwb = Slot("rw_b", ghosts)
    p.op("vector", lambda E: E.memset(rw_b[:], 0.0), writes=(s_rwb,))
    p.dma("gpsimd", lambda E: E.dma_start(out=rw_b[0:8, :], in_=router_w),
          writes=(s_rwb,))
    pR_ctx = PSUM("pR", 128, 160)
    pR = pR_ctx.__enter__()
    s_pR = Slot("pR")
    s_pR2 = Slot("pR2")
    if BUILD_UPTO < 8.02:
        p.barrier()
        return _finish(nc, p, ar, dbg, out, identb, identr, s_identb, s_identr)
    for dc in range(DC):
        p.mm_group([lambda E, dc=dc: E.matmul(pR[:, 0:128],
                                              rw_b[:, dc * 128:(dc + 1) * 128],
                                              identb[:], start=True,
                                              stop=True)],
                   reads=(s_rwb, s_identb), write=s_pR)
        p.op("vector", lambda E, dc=dc: E.tensor_copy(rwT[:, dc * 8:(dc + 1) * 8],
                                                      pR[:, 0:8]),
             reads=(s_pR,), writes=(s_rwT,))
    if BUILD_UPTO < 8.1:
        p.barrier()
        return _finish(nc, p, ar, dbg, out, identb, identr, s_identb, s_identr)
    s_probs = Slot("probs")
    for tc in range(TC):
        which = tc % 2
        sl = s_pR if which == 0 else s_pR2
        lo = 128 + which * 128
        emits = []
        for dc in range(DC):
            emits.append(lambda E, tc=tc, dc=dc, lo=lo:
                         E.matmul(pR[:, lo:lo + 8],
                                  h2c(dc)[:, tc * 128:(tc + 1) * 128],
                                  rwT[:, dc * 8:(dc + 1) * 8],
                                  start=(dc == 0), stop=(dc == DC - 1)))
        p.mm_group(emits, reads=tuple(s_h2) + (s_rwT,), write=sl)
        p.op("scalar",
             lambda E, tc=tc, lo=lo: E.activation(probs[:, tc * 8:(tc + 1) * 8],
                                                  pR[:, lo:lo + 8], ACT.Exp,
                                                  scale=-1.0),
             reads=(sl,), writes=(s_probs,))
        p.op("vector",
             lambda E, tc=tc: E.tensor_scalar_add(probs[:, tc * 8:(tc + 1) * 8],
                                                  probs[:, tc * 8:(tc + 1) * 8],
                                                  1.0),
             reads=(), writes=(s_probs,))
        p.op("vector",
             lambda E, tc=tc: E.reciprocal(probs[:, tc * 8:(tc + 1) * 8],
                                           probs[:, tc * 8:(tc + 1) * 8]),
             reads=(), writes=(s_probs,))
    if BUILD_UPTO < 8.2:
        p.barrier()
        return _finish(nc, p, ar, dbg, out, identb, identr, s_identb, s_identr)
    s_comb = Slot("comb")
    s_m8 = Slot("m8")
    s_tt = Slot("tt")
    for tc in range(TC):
        pr = probs[:, tc * 8:(tc + 1) * 8]
        cb = comb[:, tc * 8:(tc + 1) * 8]
        p.op("vector", lambda E, pr=pr: E.max(out=m8[:], in_=pr),
             reads=(s_probs,), writes=(s_m8,))
        p.op("vector", lambda E: E.tensor_tensor(tt1[:, 0:1], m8[:, 0:1],
                                                 m8[:, 1:2], ALU.add),
             reads=(s_m8,), writes=(s_tt,))
        p.op("vector", lambda E: E.tensor_scalar_add(tt1[:, 0:1], tt1[:, 0:1],
                                                     1e-20),
             reads=(), writes=(s_tt,))
        p.op("vector", lambda E: E.reciprocal(tt1[:, 1:2], tt1[:, 0:1]),
             reads=(), writes=(s_tt,))
        p.op("vector", lambda E: E.tensor_tensor(tt1[:, 2:3], m8[:, 0:1],
                                                 tt1[:, 1:2], ALU.mult),
             reads=(s_m8,), writes=(s_tt,))
        p.op("vector", lambda E: E.tensor_tensor(tt1[:, 3:4], m8[:, 1:2],
                                                 tt1[:, 1:2], ALU.mult),
             reads=(s_m8,), writes=(s_tt,))
        p.op("vector", lambda E, pr=pr: E.tensor_scalar(tt2[:], pr, m8[:, 0:1],
                                                        None, ALU.is_equal),
             reads=(s_probs, s_m8), writes=(s_tt,))
        p.op("vector", lambda E, cb=cb: E.tensor_scalar(cb, tt2[:], tt1[:, 2:3],
                                                        None, ALU.mult),
             reads=(s_tt,), writes=(s_comb,))
        p.op("vector", lambda E: E.tensor_scalar(tt3[:], tt2[:], -1.0, 1.0,
                                                 ALU.mult, ALU.add),
             reads=(s_tt,), writes=(s_tt,))
        p.op("vector", lambda E, pr=pr: E.tensor_scalar(tt2[:], pr, m8[:, 1:2],
                                                        None, ALU.is_equal),
             reads=(s_probs, s_m8), writes=(s_tt,))
        p.op("vector", lambda E: E.tensor_scalar(tt2[:], tt2[:], tt1[:, 3:4],
                                                 None, ALU.mult),
             reads=(), writes=(s_tt,))
        p.op("vector", lambda E: E.tensor_tensor(tt2[:], tt2[:], tt3[:], ALU.mult),
             reads=(), writes=(s_tt,))
        p.op("vector", lambda E, cb=cb: E.tensor_tensor(cb, cb, tt2[:], ALU.add),
             reads=(s_tt,), writes=(s_comb,))
    if BUILD_UPTO < 8.4:
        p.barrier()
        return _finish(nc, p, ar, dbg, out, identb, identr, s_identb, s_identr)
    p.op("scalar", lambda E: E.activation(comb[:], comb[:], ACT.Sqrt),
         reads=(s_comb,), writes=(s_comb,))
    pC2_ctx = PSUM("pC2", 8, T)
    pC2 = pC2_ctx.__enter__()
    s_pC2 = Slot("pC2")
    emits = []
    for tc in range(TC):
        emits.append(lambda E, tc=tc:
                     E.matmul(pC2[0:8, tc * 128:(tc + 1) * 128],
                              comb[:, tc * 8:(tc + 1) * 8], identr[:],
                              start=True, stop=True))
    p.mm_group(emits, reads=(s_comb, s_identr), write=s_pC2)
    sqckm = ar.alloc("sqckm", [8, T], F32)
    s_sqckm = Slot("sqckm", ghosts)
    p.op("vector", lambda E: E.tensor_copy(sqckm[:, 0:T], pC2[0:8, 0:T]),
         reads=(s_pC2,), writes=(s_sqckm,))
    pC2_ctx.__exit__(None, None, None)
    tok_sqc = p.dma("sync", lambda E: E.dma_start(out=sqc_d[:, 0:T],
                                                  in_=sqckm[:, 0:T]),
                    reads=(s_sqckm,))
    p.barrier()
    pG_ctx = PSUM("pG", 128, 1024)
    pG = pG_ctx.__enter__()
    s_pG = Slot("pG")
    emits = []
    for dc in range(DC):
        emits.append(lambda E, dc=dc:
                     E.matmul(pG[:, dc * 128:(dc + 1) * 128], onesrow_b[:],
                              gb_row(dc), start=True, stop=True))
    p.mm_group(emits, reads=tuple(s_rows[16:24]) + (s_ones,), write=s_pG)
    s_gbb = Slot("gbb")
    p.op("vector", lambda E: E.tensor_copy(gbb[:], pG[:]),
         reads=(s_pG,), writes=(s_gbb,))
    pG_ctx.__exit__(None, None, None)
    pR_ctx.__exit__(None, None, None)
    p.barrier()
    ar.free("rw_b", slot_toks(s_rwb))
    ar.free("sqckm", slot_toks(s_sqckm))
    if BUILD_UPTO < 9:
        return _finish(nc, p, ar, dbg, out, identb, identr, s_identb, s_identr)

    ghosts = ar.take_ghosts() + [tok_sqc]
    h_all = ar.alloc("h_all", [128, NWC * FG], BF16)

    def hallw(wc):
        return h_all[:, wc * FG:(wc + 1) * FG]
    s_hall = [Slot(f"hall{wc}", ghosts) for wc in range(NWC)]
    cbt = [ar.alloc(f"cbt{i}", [128, FG], F32) for i in range(2)]
    s_cbt = [Slot(f"cbt{i}", ghosts) for i in range(2)]
    WG = 4
    wsh = ar.alloc("wsh", [128, 2 * DC * WG * 128], BF16)
    w1t = [wsh[:, i * 4096:(i + 1) * 4096] for i in range(2)]
    s_w1t = [Slot(f"w1t{i}", ghosts) for i in range(2)]
    htmp8 = [ar.alloc(f"htmp8{i}", [128, FG], BF16) for i in range(2)]
    s_htmp8 = [Slot(f"htmp8{i}", ghosts) for i in range(2)]
    w2g = [wsh[:, i * 4096:(i + 1) * 4096] for i in range(2)]
    s_w2g = [Slot(f"w2g{i}", ghosts) for i in range(2)]
    x1t = [ar.alloc(f"x1t{i}", [128, D], F32) for i in range(2)]
    s_x1t = [Slot(f"x1t{i}", ghosts + tok_x1) for i in range(2)]
    ost = [ar.alloc(f"ost{i}", [128, D], F32) for i in range(2)]
    s_ost = [Slot(f"ost{i}", ghosts) for i in range(2)]
    final = []
    NWG = NWC // WG
    GT = FG // 128
    for ti in range(T // FG):
        t0 = ti * FG
        pH_ctx = [PSUM(f"pH{ti}_{i}", 128, FG) for i in range(2)]
        pH = [c.__enter__() for c in pH_ctx]
        s_pH = [Slot(f"pH{i}") for i in range(2)]
        for e in range(2):
            p.dma("gpsimd",
                  lambda E, e=e, t0=t0: E.dma_start(
                      out=cbt[e][:].unsqueeze(1),
                      in_=sqc_d[e:e + 1, t0:t0 + FG].partition_broadcast(128)),
                  reads=(), writes=(s_cbt[e],))
        WGW = WG * 128
        for g in range(min(2, NWG)):
            p.dma("gpsimd",
                  lambda E, g=g: E.dma_start(
                      out=w1t[g % 2][:, 0:DC * WGW],
                      in_=w1bf[:, g * 4096:(g + 1) * 4096]),
                  reads=(s_w1bf[g],), writes=(s_w1t[g % 2],),
                  war=(s_w2g[g % 2],))
        for wc in range(NWC):
            e = WC_EXPERT[wc]
            g, wi = wc // WG, wc % WG
            i = wc % 2
            emits = []
            for dc in range(DC):
                emits.append(
                    lambda E, dc=dc, i=i, g=g, wi=wi, t0=t0, pH=pH, WGW=WGW:
                    E.matmul(pH[i][:, 0:FG],
                             w1t[g % 2][:, dc * WGW + wi * 128:dc * WGW + (wi + 1) * 128],
                             h2c(dc)[:, t0:t0 + FG],
                             start=(dc == 0), stop=(dc == DC - 1)))
            p.mm_group(emits, reads=tuple(s_h2) + (s_w1t[g % 2],), write=s_pH[i])
            p.op("vector",
                 lambda E, i=i, e=e, pH=pH: E.scalar_tensor_tensor(
                     htmp8[i][:, 0:FG], pH[i][:, 0:FG], 0.0, cbt[e % 2][:, 0:FG],
                     ALU.max, ALU.mult),
                 reads=(s_pH[i], s_cbt[e % 2]), writes=(s_htmp8[i],))
            p.op("scalar",
                 lambda E, i=i, wc=wc: E.activation(hallw(wc), htmp8[i][:, 0:FG],
                                                    ACT.Square),
                 reads=(s_htmp8[i],), writes=(s_hall[wc],))
            if wi == WG - 1 and g + 2 < NWG:
                p.dma("gpsimd",
                      lambda E, g=g: E.dma_start(
                          out=w1t[g % 2][:, 0:DC * WGW],
                          in_=w1bf[:, (g + 2) * 4096:(g + 3) * 4096]),
                      reads=(s_w1bf[g + 2],), writes=(s_w1t[g % 2],))
            if wc + 1 < NWC and WC_EXPERT[wc + 1] == e + 1 and e + 2 < 8:
                p.dma("gpsimd",
                      lambda E, e=e, t0=t0: E.dma_start(
                          out=cbt[e % 2][:].unsqueeze(1),
                          in_=sqc_d[e + 2:e + 3, t0:t0 + FG].partition_broadcast(128)),
                      reads=(), writes=(s_cbt[e % 2],))
        for c in reversed(pH_ctx):
            c.__exit__(None, None, None)
        p.barrier()
        pY_ctx = [PSUM(f"pY{ti}_{i}", 128, 1024)
                  for i in range(GT)]
        pY = [c.__enter__() for c in pY_ctx]
        s_pY = [Slot(f"pY{i}") for i in range(GT)]
        for tj in range(min(2, GT)):
            tc = ti * GT + tj
            p.dma("sync",
                  lambda E, tc=tc, tj=tj: E.dma_start(
                      out=x1t[tj % 2][:], in_=x1tm_d[tc * 128:(tc + 1) * 128, :]),
                  writes=(s_x1t[tj % 2],))
        for q in range(min(2, NWC // 4)):
            p.dma("scalar",
                  lambda E, q=q: E.dma_start(
                      out=w2g[q % 2][:], in_=w2bf[:, q * 4096:(q + 1) * 4096]),
                  reads=(s_w2bf[q],), writes=(s_w2g[q % 2],),
                  war=(s_w1t[q % 2],))
        for wc in range(NWC):
            q, wj = wc // 4, wc % 4
            for tj in range(GT):
                emits = []
                for j0, jl in fslices(D):
                    emits.append(
                        lambda E, wc=wc, tj=tj, j0=j0, jl=jl, q=q, wj=wj, pY=pY:
                        E.matmul(pY[tj][:, j0:j0 + jl],
                                 hallw(wc)[:, tj * 128:(tj + 1) * 128],
                                 w2g[q % 2][:, wj * D + j0:wj * D + j0 + jl],
                                 start=(wc == 0), stop=(wc == NWC - 1)))
                rd = (s_hall[wc], s_w2g[q % 2])
                if wc == NWC - 1:
                    p.mm_group(emits, reads=rd, write=s_pY[tj])
                else:
                    for e_ in emits[:-1]:
                        p.op("tensor", e_, reads=rd,
                             war=(s_pY[tj],) if wc == 0 else (), inc=False)
                    p.op("tensor", emits[-1], reads=rd,
                         war=(s_pY[tj],) if wc == 0 else (), inc=True)
            if wj == 3 and (q + 2) * 4 < NWC:
                p.dma("scalar",
                      lambda E, q=q: E.dma_start(
                          out=w2g[q % 2][:],
                          in_=w2bf[:, (q + 2) * 4096:(q + 3) * 4096]),
                      reads=(s_w2bf[q + 2],), writes=(s_w2g[q % 2],))
        for tj in range(GT):
            tc = ti * GT + tj
            oi = tj % 2
            p.op("vector",
                 lambda E, tj=tj, oi=oi, pY=pY: E.tensor_tensor(ost[oi][:], pY[tj][:],
                                                                gbb[:], ALU.mult),
                 reads=(s_pY[tj], s_gbb), writes=(s_ost[oi],))
            p.op("vector",
                 lambda E, oi=oi: E.tensor_tensor(ost[oi][:], ost[oi][:],
                                                  x1t[oi][:], ALU.add),
                 reads=(s_x1t[oi],), writes=(s_ost[oi],))
            final.append(p.dma("sync",
                               lambda E, tc=tc, oi=oi: E.dma_start(
                                   out=out[tc * 128:(tc + 1) * 128, :],
                                   in_=ost[oi][:]),
                               reads=(s_ost[oi],)))
            if tj + 2 < GT:
                tc2 = ti * GT + tj + 2
                p.dma("sync",
                      lambda E, tc2=tc2, oi=oi: E.dma_start(
                          out=x1t[oi][:],
                          in_=x1tm_d[tc2 * 128:(tc2 + 1) * 128, :]),
                      writes=(s_x1t[oi],))
        for c in reversed(pY_ctx):
            c.__exit__(None, None, None)
        p.barrier()
    p.finalize(final)
    global LAST_AR
    LAST_AR = ar
    with nc.Block() as block:
        with nc.allow_low_precision(reason="f32r/bf16 by design"):
            p.run_sections(block)
    return nc


LAST_AR = None


def _finish(nc, p, ar, dbg, out, identb, identr, s_identb, s_identr):
    slots, chunkf = dbg[DEV_TAP]
    ghosts = ar.take_ghosts()
    outst = [ar.alloc(f"outst{i}", [128, D], F32) for i in range(2)]
    s_outst = [Slot(f"outst{i}", ghosts) for i in range(2)]
    final = []
    pO_ctx = nc.psum_tensor("pO", [128, 1024], F32)
    pO = pO_ctx.__enter__()
    s_pO = Slot("pO")
    use_b = chunkf(0).dtype == BF16
    iden, s_iden = (identb, s_identb) if use_b else (identr, s_identr)
    for tc in range(TC):
        i = tc % 2
        emits = []
        for dc in range(DC):
            emits.append(lambda E, dc=dc, tc=tc, iden=iden:
                         E.matmul(pO[:, dc * 128:(dc + 1) * 128],
                                  chunkf(dc)[:, tc * 128:(tc + 1) * 128],
                                  iden[:], start=True, stop=True))
        p.mm_group(emits, reads=tuple(slots) + (s_iden,), write=s_pO)
        p.op("vector", lambda E, i=i: E.tensor_copy(outst[i][:], pO[:]),
             reads=(s_pO,), writes=(s_outst[i],))
        final.append(
            p.dma("sync",
                  lambda E, tc=tc, i=i: E.dma_start(
                      out=out[tc * 128:(tc + 1) * 128, :], in_=outst[i][:]),
                  reads=(s_outst[i],)))
    pO_ctx.__exit__(None, None, None)
    p.finalize(final)
    with nc.Block() as block:
        with nc.allow_low_precision(reason="f32r/bf16 by design"):
            p.run_sections(block)
    return nc


def _consts():
    return {
        "ident_b": np.eye(128, dtype=ml_dtypes.bfloat16),
        "ident_r": np.eye(128, dtype=np.float32),
        "eyerows": np.eye(16, dtype=np.float32).reshape(1, 256),
        "selpat": _selpat(),
    }


def _selpat():
    s = np.zeros((16, DC * 128), dtype=ml_dtypes.bfloat16)
    for h in range(16):
        s[h, (h // 2) * 128 + (h % 2) * 64:(h // 2) * 128 + (h % 2) * 64 + 64] = 1
    return s


def make_in_maps(L0_attn_x, L0_attn_qkv_w, L0_attn_proj_w, L0_attn_proj_b,
                 L0_attn_q_norm_w, L0_attn_k_norm_w, L0_ffn_router_w, L0_ffn_w1,
                 L0_ffn_w2, L0_c, L0_wmod):
    x = np.ascontiguousarray(np.asarray(L0_attn_x, dtype=np.float32))
    shared = {
        "qkv_w": np.ascontiguousarray(np.asarray(L0_attn_qkv_w, np.float32)),
        "proj_w": np.ascontiguousarray(np.asarray(L0_attn_proj_w, np.float32)),
        "proj_b": np.asarray(L0_attn_proj_b, np.float32).reshape(1, D),
        "q_norm_w": np.asarray(L0_attn_q_norm_w, np.float32).reshape(1, DH),
        "k_norm_w": np.asarray(L0_attn_k_norm_w, np.float32).reshape(1, DH),
        "router_w": np.ascontiguousarray(np.asarray(L0_ffn_router_w, np.float32)),
        "w1s": np.ascontiguousarray(
            np.asarray(L0_ffn_w1, np.float32).reshape(8, 128, 24, 4, 128)
            .transpose(1, 2, 0, 3, 4).reshape(128, 24 * 4096)),
        "w2s": np.ascontiguousarray(
            np.asarray(L0_ffn_w2, np.float32).reshape(96, 128, 1024)
            .transpose(1, 0, 2).reshape(128, 96 * 1024)),
        "wmod": np.ascontiguousarray(np.asarray(L0_wmod, np.float32)),
    }
    shared.update(_consts())
    carr = np.asarray(L0_c, np.float32)
    in_maps = []
    for b in range(x.shape[0]):
        m = dict(shared)
        m["x"] = np.ascontiguousarray(x[b][:T])
        m["c_row"] = np.ascontiguousarray(carr[b:b + 1])
        in_maps.append(m)
    return in_maps


def kernel(**inputs):
    in_maps = make_in_maps(**inputs)
    nc = build_program()
    res = run_bass_kernel_spmd(nc, in_maps, core_ids=list(range(len(in_maps))))
    return np.stack([np.asarray(r["out"], np.float32) for r in res.results], axis=0)


# version 20 (final)
# speedup vs baseline: 1.0386x; 1.0156x over previous
import numpy as np
import ml_dtypes
import concourse.bass as bass
import concourse.mybir as mybir
from concourse.bass_utils import run_bass_kernel_spmd

F32 = mybir.dt.float32
F32R = mybir.dt.float32r
BF16 = mybir.dt.bfloat16
ALU = mybir.AluOpType
ACT = mybir.ActivationFunctionType
AX = mybir.AxisListType.X

NCORES = 8
D = 1024
H = 16
DH = 64
DC = D // 128
WIDTHS = [1024] * 4 + [2048] * 4
TOTW = sum(WIDTHS)
NWC = TOTW // 128
WC_EXPERT = [e for e in range(8) for _ in range(WIDTHS[e] // 128)]
LN_EPS = 1e-5
RMS_EPS = 1e-6

T = 2048
TC = T // 128
FG = 512

BUILD_UPTO = 99
DEV_TAP = None
SPW_CAP = 1024


def set_T(t):
    global T, TC, FG
    T = t
    TC = T // 128
    FG = min(512, T)


class Slot:
    __slots__ = ("w", "rs", "name")

    def __init__(self, name="", ghosts=()):
        self.w = None
        self.rs = list(ghosts)
        self.name = name


class Prog:
    CENG = ("tensor", "vector", "scalar", "gpsimd", "sync")

    def __init__(self, nc, dma_pool=14):
        self.nc = nc
        self.instrs = {e: [] for e in self.CENG}
        self._sem_ctxs = []
        self.sems = {}
        for e in self.CENG:
            self._mksem("e_" + e)
        self.dma_sems = {}
        for q in ("sync", "gpsimd", "scalar"):
            names = []
            for i in range(dma_pool):
                n = f"d_{q}{i}"
                self._mksem(n)
                names.append(n)
            self.dma_sems[q] = names
        self.cnt = {e: 0 for e in self.CENG}
        self.wm = {e: {} for e in self.CENG}
        self.dcnt = {n: 0 for q in self.dma_sems for n in self.dma_sems[q]}
        self.dwaited = {n: 0 for n in self.dcnt}
        self.dlive = {n: [] for n in self.dcnt}
        self.dorder = {n: [] for n in self.dcnt}
        self.drr = {q: 0 for q in self.dma_sems}
        self.erun = {e: [] for e in self.CENG}
        self.econs = {e: 0 for e in self.CENG}

    def _mksem(self, name):
        ctx = self.nc.semaphore(name)
        h = ctx.__enter__()
        self._sem_ctxs.append(ctx)
        self.sems[name] = h

    def _wait(self, eng, tok, waited_dma=None):
        sem, val = tok
        if self.wm[eng].get(sem, 0) >= val:
            return
        if sem in self.dcnt:
            assert val == self.dcnt[sem], (
                f"dma wait on {sem} for {val} but counter at {self.dcnt[sem]}")
            self.dwaited[sem] = max(self.dwaited[sem], val)
            if waited_dma is not None:
                waited_dma.append(sem)
        else:
            eng2 = sem[2:]
            if eng2 in self.econs:
                self.econs[eng2] = max(self.econs[eng2], val)
        self.wm[eng][sem] = val
        self.erun[eng] = []
        h = self.sems[sem]
        self.instrs[eng].append(lambda E, h=h, v=val: E.wait_ge(h, v))

    def _pre_inc(self, eng):
        run = self.erun[eng]
        if run and self.econs[eng] >= run[0][1]:
            h = self.sems["e_" + eng]
            v = self.cnt[eng]
            self.instrs[eng].append(lambda E, h=h, v=v: E.wait_ge(h, v))
            self.wm[eng]["e_" + eng] = v
            self.erun[eng] = []

    def _deps(self, reads, writes, war=()):
        deps = []
        for s in reads:
            if s.w is not None:
                deps.append(s.w)
        for s in tuple(writes) + tuple(war):
            deps.extend(s.rs)
            if s.w is not None:
                deps.append(s.w)
        best = {}
        for t in deps:
            if t[0] not in best or best[t[0]][1] < t[1]:
                best[t[0]] = t
        return list(best.values())

    def op(self, eng, emit, reads=(), writes=(), war=(), inc=True):
        wd = []
        for tok in self._deps(reads, writes, war):
            self._wait(eng, tok, waited_dma=wd)
        sem = "e_" + eng
        if inc:
            self._pre_inc(eng)
            self.cnt[eng] += 1
            for t in self.erun[eng]:
                t[1] = self.cnt[eng]
            tok = [sem, self.cnt[eng]]
            self.erun[eng].append(tok)
            h = self.sems[sem]
            self.instrs[eng].append(lambda E, f=emit, h=h: f(E).then_inc(h, 1))
        else:
            tok = [sem, self.cnt[eng]]
            self.instrs[eng].append(lambda E, f=emit: f(E))
        for s in reads:
            s.rs.append(tok)
        for s in writes:
            s.w = tok
            s.rs = []
        for dsem in wd:
            self.dorder[dsem].append(tok)
        return tok

    def mm_group(self, emits, reads=(), write=None, war=()):
        writes = (write,) if write is not None else ()
        wd = []
        for tok in self._deps(reads, writes, war):
            self._wait("tensor", tok, waited_dma=wd)
        sem = "e_tensor"
        h = self.sems[sem]
        for f in emits[:-1]:
            self.instrs["tensor"].append(lambda E, f=f: f(E))
        self._pre_inc("tensor")
        self.cnt["tensor"] += 1
        for t in self.erun["tensor"]:
            t[1] = self.cnt["tensor"]
        tok = [sem, self.cnt["tensor"]]
        self.erun["tensor"].append(tok)
        f = emits[-1]
        self.instrs["tensor"].append(lambda E, f=f, h=h: f(E).then_inc(h, 1))
        for s in reads:
            s.rs.append(tok)
        if write is not None:
            write.w = tok
            write.rs = []
        for dsem in wd:
            self.dorder[dsem].append(tok)
        return tok

    def dma(self, queue, emit, reads=(), writes=(), war=()):
        wd = []
        for tok in self._deps(reads, writes, war):
            self._wait(queue, tok, waited_dma=wd)
        pool = self.dma_sems[queue]
        for _ in range(len(pool)):
            n = pool[self.drr[queue] % len(pool)]
            self.drr[queue] += 1
            if self.dwaited[n] >= self.dcnt[n]:
                break
        else:
            n = pool[self.drr[queue] % len(pool)]
            self.drr[queue] += 1
        if self.dcnt[n] > 0:
            for t in list(self.dorder[n]):
                self._wait(queue, t, waited_dma=wd)
            self.dorder[n] = []
        self.dcnt[n] += 16
        for t in self.dlive[n]:
            t[1] = self.dcnt[n]
        tok = [n, self.dcnt[n]]
        self.dlive[n].append(tok)
        h = self.sems[n]
        self.instrs[queue].append(lambda E, f=emit, h=h: f(E).then_inc(h, 16))
        for s in reads:
            s.rs.append(tok)
        for s in writes:
            s.w = tok
            s.rs = []
        for dsem in wd:
            self.dorder[dsem].append(tok)
        return tok

    def barrier(self, engines=("tensor", "vector", "scalar")):
        toks = []
        for e in engines:
            self._pre_inc(e)
            self.cnt[e] += 1
            for t in self.erun[e]:
                t[1] = self.cnt[e]
            h = self.sems["e_" + e]

            def mk(E, h=h):
                ins = E.engine_nop() if hasattr(E, "engine_nop") else E.nop()
                return ins.then_inc(h, 1)
            self.instrs[e].append(mk)
            tok = ["e_" + e, self.cnt[e]]
            self.erun[e].append(tok)
            toks.append(tok)
        for e in engines:
            for tok in toks:
                self._wait(e, tok)

    def finalize(self, final_tokens):
        for tok in final_tokens:
            self._wait("sync", tok)

    def run_sections(self, block):
        for e in self.CENG:
            lst = self.instrs[e]

            def runner(E, lst=lst):
                for f in lst:
                    f(E)
            getattr(block, e)(runner)


def fslices(total, step=512):
    return [(i, min(step, total - i)) for i in range(0, total, step)]


class Arena:

    def __init__(self, nc, kib):
        self.t = nc.alloc_sbuf_tensor("arena", [128, kib * 1024], mybir.dt.uint8)
        self.free_list = [[0, kib * 1024]]
        self.regions = {}
        self.ghosts = []

    def alloc(self, name, shape, dtype, end=False):
        esz = {F32: 4, F32R: 4, BF16: 2}[dtype]
        size = (shape[1] * esz + 31) // 32 * 32
        frs = self.free_list if not end else list(reversed(self.free_list))
        for fr in frs:
            if fr[1] >= size:
                if end:
                    off = fr[0] + fr[1] - size
                    fr[1] -= size
                else:
                    off = fr[0]
                    fr[0] += size
                    fr[1] -= size
                break
        else:
            raise MemoryError(f"arena: no room for {name} ({size}B) "
                              f"free={self.free_list} regions={self.regions}")
        self.free_list = [f for f in self.free_list if f[1] > 0]
        self.regions[name] = (off, size)
        return self.t[:][0:shape[0], off:off + shape[1] * esz].bitcast(dtype)

    def free(self, name, tokens=()):
        off, size = self.regions.pop(name)
        self.free_list.append([off, size])
        self.free_list.sort()
        merged = []
        for fr in self.free_list:
            if merged and merged[-1][0] + merged[-1][1] == fr[0]:
                merged[-1][1] += fr[1]
            else:
                merged.append(fr)
        self.free_list = merged
        self.ghosts.extend(t for t in tokens if t is not None)

    def take_ghosts(self):
        return list(self.ghosts)


def slot_toks(*slots):
    out = []
    for s in slots:
        if s.w is not None:
            out.append(s.w)
        out.extend(s.rs)
    return out


def build_program():
    nc = bass.Bass("TRN2", target_bir_lowering=False)

    def PSUM(name, rows, cols):
        return nc.psum_tensor(name, [rows, (cols + 511) // 512 * 512], F32)
    p = Prog(nc)
    A = nc.alloc_sbuf_tensor
    ar = Arena(nc, 176)
    dbg = {}

    x_in = nc.dram_tensor("x", [T, D], F32, kind="ExternalInput").ap()
    qkv_w = nc.dram_tensor("qkv_w", [3 * D, D], F32, kind="ExternalInput").ap()
    proj_w = nc.dram_tensor("proj_w", [D, D], F32, kind="ExternalInput").ap()
    proj_b = nc.dram_tensor("proj_b", [1, D], F32, kind="ExternalInput").ap()
    q_norm_w = nc.dram_tensor("q_norm_w", [1, DH], F32, kind="ExternalInput").ap()
    k_norm_w = nc.dram_tensor("k_norm_w", [1, DH], F32, kind="ExternalInput").ap()
    router_w = nc.dram_tensor("router_w", [8, D], F32, kind="ExternalInput").ap()
    w1s = nc.dram_tensor("w1s", [128, 24 * 4096], F32, kind="ExternalInput").ap()
    w2s = nc.dram_tensor("w2s", [128, 96 * 1024], F32, kind="ExternalInput").ap()
    w1bf = nc.dram_tensor("w1bf", [128, 24 * 4096], BF16, kind="Internal").ap()
    w2bf = nc.dram_tensor("w2bf", [128, 96 * 1024], BF16, kind="Internal").ap()
    c_row = nc.dram_tensor("c_row", [1, D], F32, kind="ExternalInput").ap()
    wmod = nc.dram_tensor("wmod", [D, 6 * D], F32, kind="ExternalInput").ap()
    ident_b = nc.dram_tensor("ident_b", [128, 128], BF16, kind="ExternalInput").ap()
    ident_r = nc.dram_tensor("ident_r", [128, 128], F32, kind="ExternalInput").ap()
    eyerows = nc.dram_tensor("eyerows", [1, 256], F32, kind="ExternalInput").ap()
    selpat = nc.dram_tensor("selpat", [16, DC * 128], BF16,
                            kind="ExternalInput").ap()
    out = nc.dram_tensor("out", [T, D], F32, kind="ExternalOutput").ap()
    xfm_d = nc.dram_tensor("xfm_d", [128, DC * T], F32, kind="Internal").ap()
    x1tm_d = nc.dram_tensor("x1tm_d", [T, D], F32, kind="Internal").ap()
    sqc_d = nc.dram_tensor("sqc_d", [8, T], F32, kind="Internal").ap()

    identb = A("identb", [128, 128], BF16)
    identr = A("identr", [128, 128], F32)
    eyer = A("eyer", [1, 256], F32)
    selp = A("selp", [16, DC * 128], BF16)
    onesrow_r = A("onesrow_r", [1, 128], F32)
    onesrow_b = A("onesrow_b", [1, 128], BF16)
    onescol_b2 = A("onescol_b2", [128, 1], BF16)
    mod_fm = A("mod_fm", [128, 64], F32)
    mod_acc = A("mod_acc", [128, 48], F32)
    modfm_r = A("modfm_r", [128, 64], F32)
    modT = A("modT", [64, 128], BF16)
    rows_r = A("rows_r", [1, 24 * 128], BF16)
    sgn = A("sgn", [128, 16], F32)
    silc = A("silc", [128, DC], F32)
    wqpat = A("wqpat", [16, 1024], BF16)
    wkpat = A("wkpat", [16, 1024], BF16)
    sqpat = A("sqpat", [128, 16 * DC], BF16)
    wqsq = A("wqsq", [128, 16 * DC], BF16)
    wksq = A("wksq", [128, 16 * DC], BF16)
    bias_t = A("bias_t", [128, 16], F32)
    cvec = A("cvec", [16, 4], F32)
    crow = A("crow", [1, 16], F32)
    rwT = A("rwT", [128, 64], BF16)
    probs = A("probs", [128, 8 * TC], F32)
    comb = A("comb", [128, 8 * TC], F32)
    m8 = A("m8", [128, 8], F32)
    tt1 = A("tt1", [128, 8], F32)
    tt2 = A("tt2", [128, 8], F32)
    tt3 = A("tt3", [128, 8], F32)
    gbb = A("gbb", [128, D], F32)

    def modc(v, dc):
        return mod_fm[:, dc * 8 + v: dc * 8 + v + 1]

    def caplus_row(dc):
        return rows_r[:, (0 * 8 + dc) * 128:(0 * 8 + dc + 1) * 128]

    def cbplus_row(dc):
        return rows_r[:, (1 * 8 + dc) * 128:(1 * 8 + dc + 1) * 128]

    def gb_row(dc):
        return rows_r[:, (2 * 8 + dc) * 128:(2 * 8 + dc + 1) * 128]

    s_identb, s_identr, s_eyer = Slot("identb"), Slot("identr"), Slot("eyer")
    s_ones = Slot("ones")
    p.dma("sync", lambda E: E.dma_start(out=identb[:], in_=ident_b), writes=(s_identb,))
    p.dma("sync", lambda E: E.dma_start(out=identr[:], in_=ident_r), writes=(s_identr,))
    p.dma("sync", lambda E: E.dma_start(out=eyer[:], in_=eyerows), writes=(s_eyer,))
    s_selp = Slot("selp")
    p.dma("sync", lambda E: E.dma_start(out=selp[:], in_=selpat), writes=(s_selp,))
    p.op("vector", lambda E: E.memset(onesrow_r[:], 1.0), writes=(s_ones,))
    p.op("vector", lambda E: E.memset(onesrow_b[:], 1.0), writes=(s_ones,))
    p.op("vector", lambda E: E.memset(onescol_b2[:], 1.0), writes=(s_ones,))

    x_fm = ar.alloc("x_fm", [128, DC * T], F32)

    def xfm(dc):
        return x_fm[:, dc * T:(dc + 1) * T]
    s_xfm = [Slot(f"x_fm{dc}") for dc in range(DC)]

    xtm_st = [ar.alloc(f"xtm_st{i}", [128, D], F32) for i in range(3)]
    s_xtm = [Slot(f"xtm{i}") for i in range(3)]
    pT_ctx = [PSUM(f"pT{i}", 128, 512) for i in range(2)]
    pT = [c.__enter__() for c in pT_ctx]
    s_pT = [Slot(f"pT{i}") for i in range(2)]

    for tc in range(TC):
        i = tc % 3
        p.dma("sync",
              lambda E, tc=tc, i=i: E.dma_start(
                  out=xtm_st[i][:], in_=x_in[tc * 128:(tc + 1) * 128, :]),
              writes=(s_xtm[i],))
        for g in range(2):
            pj = (tc * 2 + g) % 2
            emits = []
            for k in range(4):
                dc = g * 4 + k
                emits.append(lambda E, i=i, dc=dc, pj=pj, k=k:
                             E.matmul(pT[pj][:, k * 128:(k + 1) * 128],
                                      xtm_st[i][:, dc * 128:(dc + 1) * 128],
                                      identr[:], start=True, stop=True))
            p.mm_group(emits, reads=(s_xtm[i], s_identr), write=s_pT[pj])

            def ev(E, pj=pj, g=g, tc=tc, xf=x_fm):
                return E.tensor_copy(
                    xf[:].rearrange("p (c t) -> p c t", c=DC)[
                        :, g * 4:(g + 1) * 4, tc * 128:(tc + 1) * 128],
                    pT[pj][:].rearrange("p (c t) -> p c t", c=4))
            p.op("vector", ev, reads=(s_pT[pj],),
                 writes=tuple(s_xfm[g * 4 + k] for k in range(4)))
    for c in reversed(pT_ctx):
        c.__exit__(None, None, None)
    p.barrier()
    dbg["x_fm"] = (s_xfm, xfm)

    s_modfm = Slot("mod_fm")
    p.op("vector", lambda E: E.memset(mod_fm[:], 0.0), writes=(s_modfm,))
    c_sb = ar.alloc("c_sb", [1, D], F32)
    csil = ar.alloc("csil", [1, D], F32)
    pb_sb = ar.alloc("pb_sb", [1, D], F32)
    s_c = Slot("c", ar.take_ghosts())
    p.dma("sync", lambda E: E.dma_start(out=c_sb[:], in_=c_row), writes=(s_c,))
    s_csil = Slot("csil", ar.take_ghosts())
    p.op("scalar", lambda E: E.activation(csil[:], c_sb[:], ACT.Silu),
         reads=(s_c,), writes=(s_csil,))
    s_silc = Slot("silc")
    pM_ctx = PSUM("pM", 128, 128)
    pM = pM_ctx.__enter__()
    s_pM = Slot("pM")
    emits = [
        (lambda E, dc=dc: E.matmul(pM[:, dc:dc + 1],
                                   csil[:, dc * 128:(dc + 1) * 128],
                                   onesrow_r[:, 0:1], start=True, stop=True))
        for dc in range(DC)]
    p.mm_group(emits, reads=(s_csil, s_ones), write=s_pM)
    p.op("vector", lambda E: E.tensor_copy(silc[:], pM[:, 0:DC]),
         reads=(s_pM,), writes=(s_silc,))

    wmod_st = [ar.alloc(f"wmod_st{i}", [128, 6 * D], F32) for i in range(2)]
    s_wmod = [Slot(f"wmod{i}") for i in range(2)]
    s_macc = Slot("mod_acc")
    pMD_ctx = PSUM("pMD", 128, 64)
    pMD = pMD_ctx.__enter__()
    s_pMD = Slot("pMD")
    for dci in range(DC):
        i = dci % 2
        p.dma("sync",
              lambda E, dci=dci, i=i: E.dma_start(
                  out=wmod_st[i][:], in_=wmod[dci * 128:(dci + 1) * 128, :]),
              writes=(s_wmod[i],))
        emits = []
        for v in range(6):
            for dc in range(DC):
                emits.append(lambda E, i=i, dci=dci, v=v, dc=dc:
                             E.matmul(pMD[:, dc * 8 + v:dc * 8 + v + 1],
                                      wmod_st[i][:, v * D + dc * 128:v * D + (dc + 1) * 128],
                                      silc[:, dci:dci + 1],
                                      start=True, stop=True))
        p.mm_group(emits, reads=(s_silc, s_wmod[i]), write=s_pMD)
        idx = pMD[:, 0:64].rearrange("p (c v) -> p c v", c=DC)[:, :, 0:6]
        dst = mod_acc[:].rearrange("p (c v) -> p c v", c=DC, v=6)
        if dci == 0:
            p.op("vector", lambda E, idx=idx, dst=dst: E.tensor_copy(dst, idx),
                 reads=(s_pMD,), writes=(s_macc,))
        else:
            p.op("vector", lambda E, idx=idx, dst=dst:
                 E.tensor_tensor(dst, dst, idx, ALU.add),
                 reads=(s_pMD,), writes=(s_macc,))
    ar.free("wmod_st0", slot_toks(s_wmod[0]))
    ar.free("wmod_st1", slot_toks(s_wmod[1]))

    s_pb = Slot("pb", ar.take_ghosts())
    p.dma("sync", lambda E: E.dma_start(out=pb_sb[:], in_=proj_b), writes=(s_pb,))
    p.op("vector",
         lambda E: E.tensor_copy(
             mod_fm[:].rearrange("p (c v) -> p c v", c=DC)[:, :, 0:6],
             mod_acc[:].rearrange("p (c v) -> p c v", c=DC, v=6)),
         reads=(s_macc,), writes=(s_modfm,))
    emits = [(lambda E, dc=dc: E.matmul(pMD[:, dc * 8 + 6:dc * 8 + 7],
                                        pb_sb[:, dc * 128:(dc + 1) * 128],
                                        onesrow_r[:, 0:1], start=True, stop=True))
             for dc in range(DC)]
    p.mm_group(emits, reads=(s_pb, s_ones), write=s_pMD)
    p.op("vector",
         lambda E: E.tensor_copy(
             mod_fm[:].rearrange("p (c v) -> p c v", c=DC)[:, :, 6:7],
             pMD[:, 0:64].rearrange("p (c v) -> p c v", c=DC)[:, :, 6:7]),
         reads=(s_pMD,), writes=(s_modfm,))

    s_sgn = Slot("sgn")

    def sgnv(j):
        return sgn[:, j * 8:(j + 1) * 8].rearrange("p (a b) -> p a b", b=1)
    for j, v in enumerate((2, 5)):
        view = mod_fm[:].rearrange("p (c v) -> p c v", c=DC)[:, :, v:v + 1]
        p.op("vector",
             lambda E, j=j, view=view:
             E.tensor_scalar(sgnv(j), view, 0.0, 2.0, ALU.is_ge, ALU.mult),
             reads=(s_modfm,), writes=(s_sgn,))
        p.op("vector",
             lambda E, j=j: E.tensor_scalar_add(sgnv(j), sgnv(j), -1.0),
             reads=(), writes=(s_sgn,))
        p.op("vector",
             lambda E, j=j, view=view:
             E.tensor_tensor(view, view, sgnv(j), ALU.mult),
             reads=(s_sgn,), writes=(s_modfm,))
        p.op("vector",
             lambda E, view=view:
             E.tensor_scalar_max(view, view, 0.25),
             reads=(), writes=(s_modfm,))
        p.op("vector",
             lambda E, j=j, view=view:
             E.tensor_tensor(view, view, sgnv(j), ALU.mult),
             reads=(s_sgn,), writes=(s_modfm,))
    for v in (1, 4):
        view = mod_fm[:].rearrange("p (c v) -> p c v", c=DC)[:, :, v:v + 1]
        p.op("vector",
             lambda E, view=view: E.tensor_scalar_add(view, view, 1.0),
             reads=(), writes=(s_modfm,))

    s_modT = Slot("modT")
    s_modfm_r = Slot("modfm_r")
    p.op("vector", lambda E: E.tensor_copy(modfm_r[:], mod_fm[:]),
         reads=(s_modfm,), writes=(s_modfm_r,))
    p.mm_group([lambda E: E.matmul(pM[0:64, 0:128], modfm_r[:], identr[:],
                                   start=True, stop=True)],
               reads=(s_modfm_r, s_identr), write=s_pM)
    p.op("vector", lambda E: E.tensor_copy(modT[:], pM[0:64, 0:128]),
         reads=(s_pM,), writes=(s_modT,))
    s_rows = [Slot(f"rows{i}") for i in range(24)]
    for si, v in enumerate((1, 4, 5)):
        for dc in range(DC):
            seg = si * 8 + dc
            p.dma("gpsimd",
                  lambda E, seg=seg, v=v, dc=dc: E.dma_start(
                      out=rows_r[:, seg * 128:(seg + 1) * 128],
                      in_=modT[dc * 8 + v:dc * 8 + v + 1, :]),
                  reads=(s_modT,), writes=(s_rows[seg],))

    wnorm = ar.alloc("wnorm", [1, 10 * 128], F32)
    wn_q = ar.alloc("wn_q", [1, DH], F32)
    wn_k = ar.alloc("wn_k", [1, DH], F32)
    s_wnq, s_wnk = Slot("wn_q", ar.take_ghosts()), Slot("wn_k", ar.take_ghosts())
    p.dma("sync", lambda E: E.dma_start(out=wn_q[:], in_=q_norm_w), writes=(s_wnq,))
    p.dma("sync", lambda E: E.dma_start(out=wn_k[:], in_=k_norm_w), writes=(s_wnk,))
    s_wnorm = Slot("wnorm", ar.take_ghosts())
    p.op("vector", lambda E: E.memset(wnorm[:], 0.0), writes=(s_wnorm,))
    sc = DH ** -0.5

    def wseg(i, hi):
        off = i * 128 + (DH if hi else 0)
        return wnorm[:, off:off + DH]
    p.op("vector", lambda E: E.tensor_scalar_mul(wseg(0, 0), wn_q[:], sc),
         reads=(s_wnq,), writes=(s_wnorm,))
    p.op("vector", lambda E: E.tensor_scalar_mul(wseg(1, 1), wn_q[:], sc),
         reads=(s_wnq,), writes=(s_wnorm,))
    p.op("vector", lambda E: E.tensor_tensor(wseg(2, 0), wseg(0, 0), wseg(0, 0),
                                             ALU.mult),
         reads=(), writes=(s_wnorm,))
    p.op("vector", lambda E: E.tensor_tensor(wseg(3, 1), wseg(0, 0), wseg(0, 0),
                                             ALU.mult),
         reads=(), writes=(s_wnorm,))
    p.op("vector", lambda E: E.tensor_copy(wseg(4, 0), wn_k[:]),
         reads=(s_wnk,), writes=(s_wnorm,))
    p.op("vector", lambda E: E.tensor_copy(wseg(5, 1), wn_k[:]),
         reads=(s_wnk,), writes=(s_wnorm,))
    p.op("vector", lambda E: E.tensor_tensor(wseg(6, 0), wseg(4, 0), wseg(4, 0),
                                             ALU.mult),
         reads=(), writes=(s_wnorm,))
    p.op("vector", lambda E: E.tensor_tensor(wseg(7, 1), wseg(4, 0), wseg(4, 0),
                                             ALU.mult),
         reads=(), writes=(s_wnorm,))
    p.op("vector", lambda E: E.memset(wseg(8, 0), 1.0), reads=(), writes=(s_wnorm,))
    p.op("vector", lambda E: E.memset(wseg(9, 1), 1.0), reads=(), writes=(s_wnorm,))

    s_wqpat, s_wkpat = Slot("wqpat"), Slot("wkpat")
    s_sqpat, s_wqsq, s_wksq = Slot("sqpat"), Slot("wqsq"), Slot("wksq")
    for dst, sl, ra, rb in ((wqpat, s_wqpat, 0, 1), (wkpat, s_wkpat, 4, 5)):
        for c in range(DC):
            emits = [
                (lambda E, c=c, ra=ra:
                 E.matmul(pM[0:16, 0:128], eyer[:, 16 * (2 * c):16 * (2 * c) + 16],
                          wnorm[:, ra * 128:(ra + 1) * 128], start=True, stop=False)),
                (lambda E, c=c, rb=rb:
                 E.matmul(pM[0:16, 0:128],
                          eyer[:, 16 * (2 * c + 1):16 * (2 * c + 1) + 16],
                          wnorm[:, rb * 128:(rb + 1) * 128], start=False, stop=True)),
            ]
            p.mm_group(emits, reads=(s_eyer, s_wnorm), write=s_pM)
            p.op("vector",
                 lambda E, c=c, dst=dst: E.tensor_copy(
                     dst[:, c * 128:(c + 1) * 128], pM[0:16, 0:128]),
                 reads=(s_pM,), writes=(sl,))
    for dst, sl, ra, rb in ((sqpat, s_sqpat, 8, 9), (wqsq, s_wqsq, 2, 3),
                            (wksq, s_wksq, 6, 7)):
        for c in range(DC):
            emits = [
                (lambda E, c=c, ra=ra:
                 E.matmul(pM[:, 0:16], wnorm[:, ra * 128:(ra + 1) * 128],
                          eyer[:, 16 * (2 * c):16 * (2 * c) + 16],
                          start=True, stop=False)),
                (lambda E, c=c, rb=rb:
                 E.matmul(pM[:, 0:16], wnorm[:, rb * 128:(rb + 1) * 128],
                          eyer[:, 16 * (2 * c + 1):16 * (2 * c + 1) + 16],
                          start=False, stop=True)),
            ]
            p.mm_group(emits, reads=(s_eyer, s_wnorm), write=s_pM)
            p.op("vector",
                 lambda E, c=c, dst=dst: E.tensor_copy(
                     dst[:, c * 16:(c + 1) * 16], pM[:, 0:16]),
                 reads=(s_pM,), writes=(sl,))

    pMD_ctx.__exit__(None, None, None)
    pM_ctx.__exit__(None, None, None)
    p.barrier()
    ar.free("wnorm", slot_toks(s_wnorm))
    ar.free("wn_q", slot_toks(s_wnq))
    ar.free("wn_k", slot_toks(s_wnk))
    ar.free("c_sb", slot_toks(s_c))
    ar.free("csil", slot_toks(s_csil))
    ar.free("pb_sb", slot_toks(s_pb))

    h1 = ar.alloc("h1", [128, DC * T], BF16)

    def h1c(dc):
        return h1[:, dc * T:(dc + 1) * T]
    s_h1 = [Slot(f"h1_{dc}", ar.take_ghosts()) for dc in range(DC)]

    def layernorm(xslots, xchunk, lrow_fn, lrow_slot, v_sa, hslots, hchunk, tag):
        ghosts = ar.take_ghosts()
        xsq = ar.alloc(f"xsq_{tag}", [128, T], BF16)
        xb = ar.alloc(f"xb_{tag}", [128, T], BF16)
        statr = ar.alloc(f"statr_{tag}", [1, 4 * T], F32)
        statb = ar.alloc(f"statb_{tag}", [1, 2 * T], BF16)
        htmp = ar.alloc(f"htmp_{tag}", [128, T], F32)
        s_xsq = Slot("xsq", ghosts)
        s_xb = Slot("xb", ghosts)
        s_stat = Slot("statrows", ghosts)
        s_statb = Slot("statb", ghosts)
        s_htmp = Slot("htmp", ghosts)
        pS_ctx = [PSUM(f"pS{tag}{i}", 1, T) for i in range(2)]
        pS = [c.__enter__() for c in pS_ctx]
        s_pS = [Slot(f"pS{i}") for i in range(2)]
        for dc in range(DC):
            p.op("vector",
                 lambda E, dc=dc: E.tensor_copy(xb[:], xchunk(dc)),
                 reads=(xslots[dc],), writes=(s_xb,))
            p.op("scalar",
                 lambda E, dc=dc: E.activation(xsq[:], xchunk(dc), ACT.Square),
                 reads=(xslots[dc],), writes=(s_xsq,))
            for which in (0, 1):
                emits = []
                for f0, fl in fslices(T):
                    emits.append(
                        lambda E, which=which, dc=dc, f0=f0, fl=fl:
                        E.matmul(pS[which][:, f0:f0 + fl], onescol_b2[:],
                                 (xb if which == 0 else xsq)[:, f0:f0 + fl],
                                 start=(dc == 0), stop=(dc == DC - 1)))
                rd = (s_ones, s_xb) if which == 0 else (s_ones, s_xsq)
                if dc == DC - 1:
                    p.mm_group(emits, reads=rd, write=s_pS[which])
                else:
                    for e in emits[:-1]:
                        p.op("tensor", e, reads=rd,
                             war=(s_pS[which],) if dc == 0 else (), inc=False)
                    p.op("tensor", emits[-1], reads=rd,
                         war=(s_pS[which],) if dc == 0 else (), inc=True)
        r_rstd = statr[:, 0:T]
        r_mrstd = statr[:, T:2 * T]
        tmp0 = statr[:, 2 * T:3 * T]
        tmp1 = statr[:, 3 * T:4 * T]
        p.op("vector", lambda E: E.tensor_scalar_mul(tmp0, pS[0][:, 0:T], 1.0 / D),
             reads=(s_pS[0],), writes=(s_stat,))
        p.op("vector", lambda E: E.tensor_scalar_mul(tmp1, pS[1][:, 0:T], 1.0 / D),
             reads=(s_pS[1],), writes=(s_stat,))
        p.op("vector", lambda E: E.tensor_tensor(r_mrstd, tmp0, tmp0, ALU.mult),
             reads=(), writes=(s_stat,))
        p.op("vector", lambda E: E.scalar_tensor_tensor(tmp1, tmp1, LN_EPS, r_mrstd,
                                                        ALU.add, ALU.subtract),
             reads=(), writes=(s_stat,))
        p.op("vector", lambda E: E.reciprocal(tmp1, tmp1), reads=(), writes=(s_stat,))
        p.op("scalar", lambda E: E.activation(r_rstd, tmp1, ACT.Sqrt),
             reads=(s_stat,), writes=(s_stat,))
        p.op("vector", lambda E: E.tensor_tensor(r_mrstd, tmp0, r_rstd, ALU.mult),
             reads=(), writes=(s_stat,))
        p.op("vector", lambda E: E.tensor_copy(statb[:, 0:T], r_rstd),
             reads=(s_stat,), writes=(s_statb,))
        p.op("vector", lambda E: E.tensor_copy(statb[:, T:2 * T], r_mrstd),
             reads=(s_stat,), writes=(s_statb,))
        for c in reversed(pS_ctx):
            c.__exit__(None, None, None)
        p.barrier()
        pA_ctx = PSUM(f"pA{tag}", 128, T)
        pB_ctx = PSUM(f"pB{tag}", 128, T)
        pA_, pB_ = pA_ctx.__enter__(), pB_ctx.__enter__()
        s_pA, s_pB = Slot("pA"), Slot("pB")
        for dc in range(DC):
            lrow = lrow_fn(dc)
            emA, emB = [], []
            for f0, fl in fslices(T):
                emA.append(lambda E, lrow=lrow, f0=f0, fl=fl:
                           E.matmul(pA_[:, f0:f0 + fl], lrow,
                                    statb[0:1, f0:f0 + fl], start=True, stop=True))
                emB.append(lambda E, lrow=lrow, f0=f0, fl=fl:
                           E.matmul(pB_[:, f0:f0 + fl], lrow,
                                    statb[0:1, T + f0:T + f0 + fl],
                                    start=True, stop=True))
            p.mm_group(emA, reads=(lrow_slot(dc), s_statb), write=s_pA)
            p.mm_group(emB, reads=(lrow_slot(dc), s_statb), write=s_pB)
            p.op("vector",
                 lambda E, dc=dc: E.tensor_tensor(htmp[:], xchunk(dc), pA_[:, 0:T],
                                                  ALU.mult),
                 reads=(xslots[dc], s_pA), writes=(s_htmp,))
            p.op("vector",
                 lambda E, dc=dc:
                 E.scalar_tensor_tensor(hchunk(dc), htmp[:], modc(v_sa, dc),
                                        pB_[:, 0:T], ALU.add, ALU.subtract),
                 reads=(s_htmp, s_modfm, s_pB), writes=(hslots[dc],))
        pB_ctx.__exit__(None, None, None)
        pA_ctx.__exit__(None, None, None)
        p.barrier()
        ar.free(f"xsq_{tag}", slot_toks(s_xsq))
        ar.free(f"xb_{tag}", slot_toks(s_xb))
        ar.free(f"statr_{tag}", slot_toks(s_stat))
        ar.free(f"statb_{tag}", slot_toks(s_statb))
        ar.free(f"htmp_{tag}", slot_toks(s_htmp))

    layernorm(s_xfm, xfm, caplus_row, lambda dc: s_rows[dc], 0, s_h1, h1c, "ln1")
    dbg["h1"] = (s_h1, h1c)
    if BUILD_UPTO < 3:
        return _finish(nc, p, ar, dbg, out, identb, identr, s_identb, s_identr)

    tok_xoff = []
    for dc in range(DC):
        tok_xoff.append(
            p.dma("sync",
                  lambda E, dc=dc: E.dma_start(out=xfm_d[:, dc * T:(dc + 1) * T],
                                               in_=xfm(dc)),
                  reads=(s_xfm[dc],)))
    ar.free("x_fm", slot_toks(*s_xfm))
    ar.free("xtm_st0", [])
    ar.free("xtm_st1", [])
    ar.free("xtm_st2", [])

    ghosts = ar.take_ghosts()
    wst = [ar.alloc(f"wst{i}", [128, D], BF16) for i in range(3)]
    s_wst = [Slot(f"wst{i}", ghosts) for i in range(3)]
    v_tm = ar.alloc("v_tm", [128, TC * H * 65], BF16)
    s_vtm = [Slot(f"v_tm{tc}", ghosts) for tc in range(TC)]

    def vtm(tc):
        return v_tm[:, tc * H * 65:(tc + 1) * H * 65]
    s_vones = Slot("vones", ghosts)
    p.op("vector", lambda E: E.memset(
        v_tm[:].rearrange("p (a e) -> p a e", e=65)[:, :, 64:65], 1.0),
        writes=(s_vones,))

    pW_ctx = PSUM("pW", 128, 512)
    pW = pW_ctx.__enter__()
    s_pW = Slot("pW")

    def prep_wT(mat_off, dstap, dst_slots):
        for pan in range(8):
            i = pan % 3
            p.dma("gpsimd",
                  lambda E, pan=pan, i=i, mat_off=mat_off: E.dma_start(
                      out=wst[i][:],
                      in_=qkv_w[mat_off + pan * 128:mat_off + (pan + 1) * 128, :]),
                  writes=(s_wst[i],))
            for g in range(2):
                emits = []
                for k in range(4):
                    dc = g * 4 + k
                    emits.append(lambda E, i=i, dc=dc, k=k:
                                 E.matmul(pW[:, k * 128:(k + 1) * 128],
                                          wst[i][:, dc * 128:(dc + 1) * 128],
                                          identb[:], start=True, stop=True))
                p.mm_group(emits, reads=(s_wst[i], s_identb), write=s_pW)

                def ev(E, g=g, pan=pan, dstap=dstap):
                    return E.tensor_copy(
                        dstap.rearrange("p (c j) -> p c j", c=DC)[
                            :, g * 4:(g + 1) * 4, pan * 128:(pan + 1) * 128],
                        pW[:].rearrange("p (c j) -> p c j", c=4))
                p.op("vector", ev, reads=(s_pW,),
                     writes=tuple(dst_slots[g * 4 + k] for k in range(4)))

    wvT = ar.alloc("wvT", [128, DC * D], BF16)
    s_wvT = [Slot(f"wvT{dc}", ghosts) for dc in range(DC)]
    prep_wT(2 * D, wvT, s_wvT)
    pV_ctx = [PSUM(f"pV{i}", 128, 1024) for i in range(2)]
    pV = [c.__enter__() for c in pV_ctx]
    s_pV = [Slot(f"pV{i}") for i in range(2)]
    for tc in range(TC):
        i = tc % 2
        emits = []
        for dc in range(DC):
            for j0, jl in fslices(D):
                emits.append(lambda E, tc=tc, dc=dc, j0=j0, jl=jl, i=i:
                             E.matmul(pV[i][:, j0:j0 + jl],
                                      h1c(dc)[:, tc * 128:(tc + 1) * 128],
                                      wvT[:, dc * D + j0:dc * D + j0 + jl],
                                      start=(dc == 0), stop=(dc == DC - 1)))
        p.mm_group(emits, reads=tuple(s_h1) + tuple(s_wvT), write=s_pV[i])

        def ev(E, tc=tc, i=i):
            return E.tensor_copy(
                vtm(tc).rearrange("p (h e) -> p h e", e=65)[:, :, 0:64],
                pV[i][:].rearrange("p (h e) -> p h e", e=64))
        p.op("vector", ev, reads=(s_pV[i],), writes=(s_vtm[tc],))
    for c in reversed(pV_ctx):
        c.__exit__(None, None, None)
    p.barrier()
    ar.free("wvT", slot_toks(*s_wvT))

    qpk = ar.alloc("qpk", [128, DC * T], BF16)
    kpk = ar.alloc("kpk", [128, DC * T], BF16)

    def qpkc(c):
        return qpk[:, c * T:(c + 1) * T]

    def kpkc(c):
        return kpk[:, c * T:(c + 1) * T]
    ghosts = ar.take_ghosts()
    s_qpk = [Slot(f"qpk{c}", ghosts) for c in range(DC)]
    s_kpk = [Slot(f"kpk{c}", ghosts) for c in range(DC)]
    wqT = ar.alloc("wqT", [128, DC * D], BF16)
    s_wqT = [Slot(f"wqT{dc}", ghosts) for dc in range(DC)]
    prep_wT(0, wqT, s_wqT)

    pQ_ctx = [PSUM(f"pQ{i}", 128, 512) for i in range(2)]
    pQ = [c.__enter__() for c in pQ_ctx]
    s_pQ = [Slot(f"pQ{i}") for i in range(2)]
    qkn = [0]

    def qk_prod(wT, s_wT, dst, s_dst):
        for c in range(DC):
            for f0, fl in fslices(T):
                i = qkn[0] % 2
                qkn[0] += 1
                emits = []
                for dc in range(DC):
                    emits.append(lambda E, c=c, dc=dc, f0=f0, fl=fl, i=i, wT=wT:
                                 E.matmul(pQ[i][:, 0:fl],
                                          wT[:, dc * D + c * 128:dc * D + (c + 1) * 128],
                                          h1c(dc)[:, f0:f0 + fl],
                                          start=(dc == 0), stop=(dc == DC - 1)))
                p.mm_group(emits, reads=tuple(s_h1) + tuple(s_wT), write=s_pQ[i])
                p.op("vector",
                     lambda E, c=c, f0=f0, fl=fl, i=i, dst=dst:
                     E.tensor_copy(dst(c)[:, f0:f0 + fl], pQ[i][:, 0:fl]),
                     reads=(s_pQ[i],), writes=(s_dst[c],))

    qk_prod(wqT, s_wqT, qpkc, s_qpk)
    ar.free("wqT", slot_toks(*s_wqT))
    wkT = ar.alloc("wkT", [128, DC * D], BF16)
    s_wkT = [Slot(f"wkT{dc}", ar.take_ghosts()) for dc in range(DC)]
    prep_wT(D, wkT, s_wkT)
    qk_prod(wkT, s_wkT, kpkc, s_kpk)
    for c in reversed(pQ_ctx):
        c.__exit__(None, None, None)
    pW_ctx.__exit__(None, None, None)
    p.barrier()
    ar.free("wkT", slot_toks(*s_wkT))
    ar.free("wst0", slot_toks(s_wst[0]))
    ar.free("wst1", slot_toks(s_wst[1]))
    ar.free("wst2", slot_toks(s_wst[2]))
    ar.free("h1", slot_toks(*s_h1))
    dbg["qpk"] = (s_qpk, qpkc)
    dbg["kpk"] = (s_kpk, kpkc)
    if BUILD_UPTO < 4:
        return _finish(nc, p, ar, dbg, out, identb, identr, s_identb, s_identr)

    ghosts = ar.take_ghosts()
    qsq = ar.alloc("qsq", [128, T], BF16)
    stat = ar.alloc("stat", [16, 4 * T], F32)
    statbh = ar.alloc("statbh", [16, T], BF16)
    s_qsq = Slot("qsq", ghosts)
    s_stat = Slot("stat4", ghosts)
    s_statbh = Slot("statbh", ghosts)
    r_rstd_q = stat[:, 0 * T:1 * T]
    r_rstd_k = stat[:, 1 * T:2 * T]
    r_tmp = stat[:, 2 * T:3 * T]
    r_tmp2 = stat[:, 3 * T:4 * T]
    s_bias = Slot("bias_t")
    s_cvec = Slot("cvec")
    s_crow = Slot("crow")

    for side, (pkc, s_pk, wpat, s_wpat, wsqp, s_wsqp, r_rstd, cvcol) in enumerate((
            (qpkc, s_qpk, wqpat, s_wqpat, wqsq, s_wqsq, r_rstd_q, 0),
            (kpkc, s_kpk, wkpat, s_wkpat, wksq, s_wksq, r_rstd_k, 1))):
        pSS_ctx = [PSUM(f"pSS{side}_{i}", 16, T) for i in range(2)]
        pSS = [c.__enter__() for c in pSS_ctx]
        s_pSS = [Slot(f"pSS{i}") for i in range(2)]
        for c in range(DC):
            p.op("vector",
                 lambda E, c=c, pkc=pkc: E.tensor_tensor(qsq[:], pkc(c), pkc(c),
                                                         ALU.mult),
                 reads=(s_pk[c],), writes=(s_qsq,))
            for which, pat, s_pat in ((0, sqpat, s_sqpat), (1, wsqp, s_wsqp)):
                emits = []
                for f0, fl in fslices(T):
                    emits.append(lambda E, c=c, f0=f0, fl=fl, which=which, pat=pat:
                                 E.matmul(pSS[which][:, f0:f0 + fl],
                                          pat[:, c * 16:(c + 1) * 16],
                                          qsq[:, f0:f0 + fl],
                                          start=(c == 0), stop=(c == DC - 1)))
                rd = (s_qsq, s_pat)
                if c == DC - 1:
                    p.mm_group(emits, reads=rd, write=s_pSS[which])
                else:
                    for e in emits[:-1]:
                        p.op("tensor", e, reads=rd,
                             war=(s_pSS[which],) if c == 0 else (), inc=False)
                    p.op("tensor", emits[-1], reads=rd,
                         war=(s_pSS[which],) if c == 0 else (), inc=True)
        p.op("vector", lambda E, r_rstd=r_rstd:
             E.tensor_scalar(r_rstd, pSS[0][:, 0:T], 1.0 / DH, RMS_EPS,
                             ALU.mult, ALU.add),
             reads=(s_pSS[0],), writes=(s_stat,))
        p.op("vector", lambda E, r_rstd=r_rstd: E.reciprocal(r_rstd, r_rstd),
             reads=(), writes=(s_stat,))
        p.op("scalar", lambda E, r_rstd=r_rstd: E.activation(r_rstd, r_rstd,
                                                             ACT.Sqrt),
             reads=(s_stat,), writes=(s_stat,))
        p.op("vector", lambda E, r_rstd=r_rstd:
             E.tensor_tensor(r_tmp, r_rstd, r_rstd, ALU.mult),
             reads=(), writes=(s_stat,))
        p.op("vector", lambda E:
             E.tensor_tensor(r_tmp2, pSS[1][:, 0:T], r_tmp, ALU.mult),
             reads=(s_pSS[1],), writes=(s_stat,))
        p.op("vector", lambda E, cvcol=cvcol:
             E.reduce_max(cvec[:, cvcol:cvcol + 1], r_tmp2, axis=AX),
             reads=(s_stat,), writes=(s_cvec,))
        for c_ in reversed(pSS_ctx):
            c_.__exit__(None, None, None)
        p.barrier()
        pRB_ctx = PSUM(f"pRB{side}", 128, T)
        pRB = pRB_ctx.__enter__()
        s_pRB = Slot("pRB")
        p.op("vector", lambda E, r_rstd=r_rstd: E.tensor_copy(statbh[:, 0:T], r_rstd),
             reads=(s_stat,), writes=(s_statbh,))
        for c in range(DC):
            emits = []
            for f0, fl in fslices(T):
                emits.append(lambda E, c=c, f0=f0, fl=fl, wpat=wpat:
                             E.matmul(pRB[:, f0:f0 + fl],
                                      wpat[:, c * 128:(c + 1) * 128],
                                      statbh[0:16, f0:f0 + fl],
                                      start=True, stop=True))
            p.mm_group(emits, reads=(s_wpat, s_statbh), write=s_pRB)
            p.op("vector",
                 lambda E, c=c, pkc=pkc: E.tensor_tensor(pkc(c), pkc(c),
                                                         pRB[:, 0:T], ALU.mult),
                 reads=(s_pRB,), writes=(s_pk[c],))
        pRB_ctx.__exit__(None, None, None)
        p.barrier()
    p.op("vector", lambda E: E.tensor_tensor(cvec[:, 2:3], cvec[:, 0:1],
                                             cvec[:, 1:2], ALU.mult),
         reads=(s_cvec,), writes=(s_cvec,))
    p.op("scalar", lambda E: E.activation(cvec[:, 3:4], cvec[:, 2:3], ACT.Sqrt),
         reads=(s_cvec,), writes=(s_cvec,))
    pC_ctx = PSUM("pC", 128, 32)
    pC = pC_ctx.__enter__()
    s_pC = Slot("pC")
    cvr = ar.alloc("cvr", [16, 4], F32)
    s_cvr = Slot("cvr", ghosts)
    p.op("vector", lambda E: E.tensor_copy(cvr[:], cvec[:]),
         reads=(s_cvec,), writes=(s_cvr,))
    p.mm_group([lambda E: E.matmul(pC[0:1, 0:16], cvr[:, 3:4],
                                   identr[0:16, 0:16], start=True, stop=True)],
               reads=(s_cvr, s_identr), write=s_pC)
    p.op("vector", lambda E: E.tensor_scalar_mul(crow[:], pC[0:1, 0:16], -1.0),
         reads=(s_pC,), writes=(s_crow,))
    crowr = ar.alloc("crowr", [1, 16], F32)
    s_crowr = Slot("crowr", ghosts)
    p.op("vector", lambda E: E.tensor_copy(crowr[:], crow[:]),
         reads=(s_crow,), writes=(s_crowr,))
    p.mm_group([lambda E: E.matmul(pC[:, 128:144], onesrow_r[:], crowr[:, 0:16],
                                   start=True, stop=True)],
               reads=(s_crowr, s_ones), write=s_pC)
    p.op("vector", lambda E: E.tensor_copy(bias_t[:], pC[:, 128:144]),
         reads=(s_pC,), writes=(s_bias,))
    pC_ctx.__exit__(None, None, None)
    p.barrier()
    ar.free("qsq", slot_toks(s_qsq))
    ar.free("stat", slot_toks(s_stat))
    ar.free("statbh", slot_toks(s_statbh))
    ar.free("cvr", slot_toks(s_cvr))
    ar.free("crowr", slot_toks(s_crowr))
    dbg["qn"] = (s_qpk, qpkc)
    dbg["kn"] = (s_kpk, kpkc)
    if BUILD_UPTO < 5:
        return _finish(nc, p, ar, dbg, out, identb, identr, s_identb, s_identr)

    s_w1bf = [Slot(f"w1bf{g}") for g in range(24)]
    s_w2bf = [Slot(f"w2bf{q}") for q in range(24)]

    def stage_chunk(k):
        if k < 24:
            p.dma("gpsimd",
                  lambda E, g=k: E.dma_start(
                      out=w1bf[:, g * 4096:(g + 1) * 4096],
                      in_=w1s[:, g * 4096:(g + 1) * 4096]),
                  writes=(s_w1bf[k],))
        elif k < 48:
            p.dma("gpsimd",
                  lambda E, q=k - 24: E.dma_start(
                      out=w2bf[:, q * 4096:(q + 1) * 4096],
                      in_=w2s[:, q * 4096:(q + 1) * 4096]),
                  writes=(s_w2bf[k - 24],))
    ghosts = ar.take_ghosts()
    attn = ar.alloc("attn", [128, DC * T], BF16, end=True)

    def attnc(c):
        return attn[:, c * T:(c + 1) * T]
    s_attn = [Slot(f"attn{c}", ghosts) for c in range(DC)]
    qx = [ar.alloc(f"qx{par}", [128, T], BF16) for par in range(2)]
    s_qx = [Slot(f"qx{par}", ghosts) for par in range(2)]
    p.op("vector", lambda E: E.memset(qx[0][64:128, :], 0.0), writes=(s_qx[0],))
    p.op("vector", lambda E: E.memset(qx[1][0:64, :], 0.0), writes=(s_qx[1],))
    pt = [ar.alloc(f"pt{i}", [128, 1024], BF16) for i in range(3)]
    s_pt = [Slot(f"pt{i}", ghosts) for i in range(3)]
    ast = [ar.alloc(f"ast{i}", [64, T], BF16) for i in range(2)]
    s_ast = [Slot(f"ast{i}", ghosts) for i in range(2)]
    dtmp = [ar.alloc(f"dtmp{i}", [1, T], BF16) for i in range(2)]
    s_dtmp = [Slot(f"dtmp{i}", ghosts) for i in range(2)]
    dnm = ar.alloc("dnm", [16, T], BF16)
    s_dnm = [Slot(f"dnm{h}", ghosts) for h in range(H)]
    dnr = ar.alloc("dnr", [16, T], BF16)
    s_dnr = Slot("dnr", ghosts)

    psp_ctx = [PSUM(f"psp{i}", 128, 1024) for i in range(2)]
    psp = [c.__enter__() for c in psp_ctx]
    s_psp = [Slot(f"psp{i}") for i in range(2)]
    ppv_ctx = PSUM("ppv", 65, T)
    ppv = ppv_ctx.__enter__()

    SPW = min(SPW_CAP, T)
    NHH = T // SPW
    s_ppv = [Slot(f"ppv{nh}") for nh in range(NHH)]

    def qcopy(h):
        c, par = h // 2, h % 2
        p.dma("sync",
              lambda E, c=c, par=par: E.dma_start(
                  out=qx[par][par * 64:(par + 1) * 64, 0:T],
                  in_=qpkc(c)[par * 64:(par + 1) * 64, 0:T]),
              reads=(s_qpk[c],), writes=(s_qx[par],))

    def flush_pv(h, mc, nh, j):
        emits = []
        for f0, fl in fslices(SPW):
            emits.append(
                lambda E, mc=mc, nh=nh, j=j, h=h, f0=f0, fl=fl:
                E.matmul(ppv[0:65, nh * SPW + f0:nh * SPW + f0 + fl],
                         vtm(mc)[:, h * 65:(h + 1) * 65],
                         pt[j][:, f0:f0 + fl],
                         start=(mc == 0), stop=(mc == TC - 1)))
        rd = (s_pt[j], s_vtm[mc])
        if mc == TC - 1:
            p.mm_group(emits, reads=rd, write=s_ppv[nh])
        else:
            for e in emits[:-1]:
                p.op("tensor", e, reads=rd,
                     war=(s_ppv[nh],) if mc == 0 else (), inc=False)
            p.op("tensor", emits[-1], reads=rd,
                 war=(s_ppv[nh],) if mc == 0 else (), inc=True)

    def dcopy(h):
        di = h % 2
        p.op("scalar", lambda E, di=di: E.activation(dtmp[di][:, 0:T],
                                                     ppv[64:65, 0:T], ACT.Copy),
             reads=tuple(s_ppv), writes=(s_dtmp[di],))
        p.dma("sync",
              lambda E, h=h, di=di: E.dma_start(out=dnm[h:h + 1, 0:T],
                                                in_=dtmp[di][:, 0:T]),
              reads=(s_dtmp[di],), writes=(s_dnm[h],))

    qcopy(0)
    sub = 0
    pend_copy = None
    for h in range(H):
        c, par = h // 2, h % 2
        if h + 1 < H:
            qcopy(h + 1)
        for k in range(3 * h, 3 * h + 3):
            stage_chunk(k)
        pend = None
        for mc in range(TC):
            for nh in range(NHH):
                i = sub % 2
                j = sub % 3
                sub += 1
                emits = []
                for f0, fl in fslices(SPW):
                    emits.append(
                        lambda E, c=c, mc=mc, nh=nh, f0=f0, fl=fl, i=i, par=par:
                        E.matmul(psp[i][:, f0:f0 + fl],
                                 kpkc(c)[:, mc * 128:(mc + 1) * 128],
                                 qx[par][:, nh * SPW + f0:nh * SPW + f0 + fl],
                                 start=True, stop=True))
                p.mm_group(emits, reads=(s_kpk[c], s_qx[par]), write=s_psp[i])
                p.op("scalar",
                     lambda E, i=i, j=j, h=h: E.activation(
                         pt[j][:, 0:SPW], psp[i][:, 0:SPW], ACT.Exp,
                         bias=bias_t[:, h:h + 1], scale=1.0),
                     reads=(s_psp[i], s_bias), writes=(s_pt[j],))
                if pend_copy is not None:
                    dcopy(pend_copy)
                    pend_copy = None
                if pend is not None:
                    flush_pv(*pend)
                pend = (h, mc, nh, j)
        flush_pv(*pend)
        if h + 1 < H:
            pend_copy = h
        else:
            dcopy(h)
        if par == 0:
            for nh in range(NHH):
                p.op("vector",
                     lambda E, nh=nh, c=c: E.tensor_copy(
                         attnc(c)[0:64, nh * SPW:(nh + 1) * SPW],
                         ppv[0:64, nh * SPW:(nh + 1) * SPW]),
                     reads=(s_ppv[nh],), writes=(s_attn[c],))
        else:
            ai = (h // 2) % 2
            for nh in range(NHH):
                p.op("vector",
                     lambda E, nh=nh, ai=ai: E.tensor_copy(
                         ast[ai][0:64, nh * SPW:(nh + 1) * SPW],
                         ppv[0:64, nh * SPW:(nh + 1) * SPW]),
                     reads=(s_ppv[nh],), writes=(s_ast[ai],))
            p.dma("sync",
                  lambda E, c=c, ai=ai: E.dma_start(out=attnc(c)[64:128, :],
                                                    in_=ast[ai][0:64, :]),
                  reads=(s_ast[ai],), writes=(s_attn[c],))
    ppv_ctx.__exit__(None, None, None)
    for c in reversed(psp_ctx):
        c.__exit__(None, None, None)
    p.op("vector", lambda E: E.reciprocal(dnr[0:16, 0:T], dnm[0:16, 0:T]),
         reads=tuple(s_dnm), writes=(s_dnr,))
    pbc_ctx = [PSUM(f"pbc{i}", 128, SPW) for i in range(2)]
    pbc = [c.__enter__() for c in pbc_ctx]
    s_pbc = [Slot(f"pbc{i}") for i in range(2)]
    nn = 0
    for c in range(DC):
        for nh in range(NHH):
            i = nn % 2
            nn += 1
            emits = []
            for f0, fl in fslices(SPW):
                emits.append(lambda E, c=c, nh=nh, f0=f0, fl=fl, i=i:
                             E.matmul(pbc[i][:, f0:f0 + fl],
                                      selp[0:16, c * 128:(c + 1) * 128],
                                      dnr[0:16, nh * SPW + f0:nh * SPW + f0 + fl],
                                      start=True, stop=True))
            p.mm_group(emits, reads=(s_dnr, s_selp), write=s_pbc[i])
            p.op("vector",
                 lambda E, c=c, nh=nh, i=i: E.tensor_tensor(
                     attnc(c)[:, nh * SPW:(nh + 1) * SPW],
                     attnc(c)[:, nh * SPW:(nh + 1) * SPW],
                     pbc[i][:, 0:SPW], ALU.mult),
                 reads=(s_pbc[i],), writes=(s_attn[c],))
    for ctx in reversed(pbc_ctx):
        ctx.__exit__(None, None, None)
    p.barrier()
    ar.free("qpk", slot_toks(*s_qpk))
    ar.free("kpk", slot_toks(*s_kpk))
    ar.free("v_tm", slot_toks(*s_vtm) + slot_toks(s_vones))
    ar.free("qx0", slot_toks(s_qx[0]))
    ar.free("qx1", slot_toks(s_qx[1]))
    ar.free("pt0", slot_toks(s_pt[0]))
    ar.free("pt1", slot_toks(s_pt[1]))
    ar.free("pt2", slot_toks(s_pt[2]))
    ar.free("ast0", slot_toks(s_ast[0]))
    ar.free("ast1", slot_toks(s_ast[1]))
    ar.free("dtmp0", slot_toks(s_dtmp[0]))
    ar.free("dtmp1", slot_toks(s_dtmp[1]))
    ar.free("dnm", slot_toks(*s_dnm))
    ar.free("dnr", slot_toks(s_dnr))
    dbg["attn"] = (s_attn, attnc)
    if BUILD_UPTO < 6:
        return _finish(nc, p, ar, dbg, out, identb, identr, s_identb, s_identr)

    ghosts = ar.take_ghosts()
    pwT = ar.alloc("pwT", [128, DC * D], BF16, end=True)
    s_pwT = [Slot(f"pwT{dc}", ghosts) for dc in range(DC)]
    wst2 = [ar.alloc(f"wst2_{i}", [128, D], BF16, end=True)
            for i in range(3)]
    s_wst2 = [Slot(f"wst2_{i}", ghosts) for i in range(3)]
    xb2 = ar.alloc("x_fm2", [128, DC * T], F32)

    def xfm2(dc, xb2=xb2):
        return xb2[:, dc * T:(dc + 1) * T]
    s_xfm2 = [Slot(f"x2_{dc}", ghosts + tok_xoff) for dc in range(DC)]
    for dc in range(DC):
        p.dma("sync",
              lambda E, dc=dc: E.dma_start(out=xfm2(dc),
                                           in_=xfm_d[:, dc * T:(dc + 1) * T]),
              writes=(s_xfm2[dc],))

    pW2_ctx = PSUM("pW2", 128, 512)
    pW2 = pW2_ctx.__enter__()
    s_pW2 = Slot("pW2")
    for pan in range(8):
        i = pan % 3
        p.dma("gpsimd",
              lambda E, pan=pan, i=i: E.dma_start(
                  out=wst2[i][:], in_=proj_w[pan * 128:(pan + 1) * 128, :]),
              writes=(s_wst2[i],))
        for g in range(2):
            emits = []
            for k in range(4):
                dc = g * 4 + k
                emits.append(lambda E, i=i, dc=dc, k=k:
                             E.matmul(pW2[:, k * 128:(k + 1) * 128],
                                      wst2[i][:, dc * 128:(dc + 1) * 128],
                                      identb[:], start=True, stop=True))
            p.mm_group(emits, reads=(s_wst2[i], s_identb), write=s_pW2)

            def ev(E, g=g, pan=pan):
                return E.tensor_copy(
                    pwT[:].rearrange("p (c j) -> p c j", c=DC)[
                        :, g * 4:(g + 1) * 4, pan * 128:(pan + 1) * 128],
                    pW2[:].rearrange("p (c j) -> p c j", c=4))
            p.op("vector", ev, reads=(s_pW2,),
                 writes=tuple(s_pwT[g * 4 + k] for k in range(4)))
    pW2_ctx.__exit__(None, None, None)
    p.barrier()

    ptmp = ar.alloc("ptmp", [128, 512], F32, end=True)
    s_ptmp = Slot("ptmp", ghosts)
    pP_ctx = [PSUM(f"pP{i}", 128, 512) for i in range(2)]
    pP = [c.__enter__() for c in pP_ctx]
    s_pP = [Slot(f"pP{i}") for i in range(2)]
    n2 = 0
    for jc in range(DC):
        for f0, fl in fslices(T):
            i = n2 % 2
            n2 += 1
            emits = []
            for dc in range(DC):
                emits.append(lambda E, jc=jc, dc=dc, f0=f0, fl=fl, i=i:
                             E.matmul(pP[i][:, 0:fl],
                                      pwT[:, dc * D + jc * 128:dc * D + (jc + 1) * 128],
                                      attnc(dc)[:, f0:f0 + fl],
                                      start=(dc == 0), stop=(dc == DC - 1)))
            p.mm_group(emits, reads=tuple(s_attn) + tuple(s_pwT), write=s_pP[i])
            p.op("vector",
                 lambda E, jc=jc, f0=f0, fl=fl, i=i:
                 E.tensor_scalar(ptmp[:, 0:fl], pP[i][:, 0:fl],
                                 modc(6, jc), modc(2, jc), ALU.add, ALU.mult),
                 reads=(s_pP[i], s_modfm), writes=(s_ptmp,))
            p.op("vector",
                 lambda E, jc=jc, f0=f0, fl=fl:
                 E.tensor_tensor(xfm2(jc)[:, f0:f0 + fl], xfm2(jc)[:, f0:f0 + fl],
                                 ptmp[:, 0:fl], ALU.add),
                 reads=(s_ptmp,), writes=(s_xfm2[jc],))
    for c in reversed(pP_ctx):
        c.__exit__(None, None, None)
    p.barrier()
    ar.free("attn", slot_toks(*s_attn))
    ar.free("pwT", slot_toks(*s_pwT))
    ar.free("wst2_0", slot_toks(s_wst2[0]))
    ar.free("wst2_1", slot_toks(s_wst2[1]))
    ar.free("wst2_2", slot_toks(s_wst2[2]))
    ar.free("ptmp", slot_toks(s_ptmp))
    dbg["x1"] = (s_xfm2, xfm2)
    if BUILD_UPTO < 7:
        return _finish(nc, p, ar, dbg, out, identb, identr, s_identb, s_identr)

    h2 = ar.alloc("h2", [128, DC * T], BF16, end=True)

    def h2c(dc):
        return h2[:, dc * T:(dc + 1) * T]
    s_h2 = [Slot(f"h2_{dc}", ar.take_ghosts()) for dc in range(DC)]
    layernorm(s_xfm2, xfm2, cbplus_row, lambda dc: s_rows[8 + dc], 3, s_h2, h2c,
              "ln2")
    dbg["h2"] = (s_h2, h2c)

    ghosts = ar.take_ghosts()
    x1st = [ar.alloc(f"x1st{i}", [128, D], F32) for i in range(2)]
    s_x1st = [Slot(f"x1st{i}", ghosts) for i in range(2)]
    pX_ctx = PSUM("pX", 128, 1024)
    pX = pX_ctx.__enter__()
    s_pX = Slot("pX")
    tok_x1 = []
    for tc in range(TC):
        i = tc % 2
        emits = []
        for dc in range(DC):
            emits.append(lambda E, dc=dc, tc=tc:
                         E.matmul(pX[:, dc * 128:(dc + 1) * 128],
                                  xfm2(dc)[:, tc * 128:(tc + 1) * 128],
                                  identr[:], start=True, stop=True))
        p.mm_group(emits, reads=tuple(s_xfm2) + (s_identr,), write=s_pX)
        p.op("vector", lambda E, i=i: E.tensor_copy(x1st[i][:], pX[:]),
             reads=(s_pX,), writes=(s_x1st[i],))
        tok_x1.append(
            p.dma("sync",
                  lambda E, tc=tc, i=i: E.dma_start(
                      out=x1tm_d[tc * 128:(tc + 1) * 128, :], in_=x1st[i][:]),
                  reads=(s_x1st[i],)))
    pX_ctx.__exit__(None, None, None)
    p.barrier()
    ar.free("x_fm2", slot_toks(*s_xfm2))
    ar.free("x1st0", slot_toks(s_x1st[0]))
    ar.free("x1st1", slot_toks(s_x1st[1]))
    if BUILD_UPTO < 8:
        return _finish(nc, p, ar, dbg, out, identb, identr, s_identb, s_identr)

    ghosts = ar.take_ghosts()
    s_rwT = Slot("rwT")
    rw_b = ar.alloc("rw_b", [128, D], BF16)
    s_rwb = Slot("rw_b", ghosts)
    p.op("vector", lambda E: E.memset(rw_b[:], 0.0), writes=(s_rwb,))
    p.dma("gpsimd", lambda E: E.dma_start(out=rw_b[0:8, :], in_=router_w),
          writes=(s_rwb,))
    pR_ctx = PSUM("pR", 128, 160)
    pR = pR_ctx.__enter__()
    s_pR = Slot("pR")
    s_pR2 = Slot("pR2")
    if BUILD_UPTO < 8.02:
        p.barrier()
        return _finish(nc, p, ar, dbg, out, identb, identr, s_identb, s_identr)
    for dc in range(DC):
        p.mm_group([lambda E, dc=dc: E.matmul(pR[:, 0:128],
                                              rw_b[:, dc * 128:(dc + 1) * 128],
                                              identb[:], start=True,
                                              stop=True)],
                   reads=(s_rwb, s_identb), write=s_pR)
        p.op("vector", lambda E, dc=dc: E.tensor_copy(rwT[:, dc * 8:(dc + 1) * 8],
                                                      pR[:, 0:8]),
             reads=(s_pR,), writes=(s_rwT,))
    if BUILD_UPTO < 8.1:
        p.barrier()
        return _finish(nc, p, ar, dbg, out, identb, identr, s_identb, s_identr)
    s_probs = Slot("probs")
    for tc in range(TC):
        which = tc % 2
        sl = s_pR if which == 0 else s_pR2
        lo = 128 + which * 128
        emits = []
        for dc in range(DC):
            emits.append(lambda E, tc=tc, dc=dc, lo=lo:
                         E.matmul(pR[:, lo:lo + 8],
                                  h2c(dc)[:, tc * 128:(tc + 1) * 128],
                                  rwT[:, dc * 8:(dc + 1) * 8],
                                  start=(dc == 0), stop=(dc == DC - 1)))
        p.mm_group(emits, reads=tuple(s_h2) + (s_rwT,), write=sl)
        p.op("scalar",
             lambda E, tc=tc, lo=lo: E.activation(probs[:, tc * 8:(tc + 1) * 8],
                                                  pR[:, lo:lo + 8], ACT.Exp,
                                                  scale=-1.0),
             reads=(sl,), writes=(s_probs,))
        p.op("vector",
             lambda E, tc=tc: E.tensor_scalar_add(probs[:, tc * 8:(tc + 1) * 8],
                                                  probs[:, tc * 8:(tc + 1) * 8],
                                                  1.0),
             reads=(), writes=(s_probs,))
        p.op("vector",
             lambda E, tc=tc: E.reciprocal(probs[:, tc * 8:(tc + 1) * 8],
                                           probs[:, tc * 8:(tc + 1) * 8]),
             reads=(), writes=(s_probs,))
    if BUILD_UPTO < 8.2:
        p.barrier()
        return _finish(nc, p, ar, dbg, out, identb, identr, s_identb, s_identr)
    s_comb = Slot("comb")
    s_m8 = Slot("m8")
    s_tt = Slot("tt")
    for tc in range(TC):
        pr = probs[:, tc * 8:(tc + 1) * 8]
        cb = comb[:, tc * 8:(tc + 1) * 8]
        p.op("vector", lambda E, pr=pr: E.max(out=m8[:], in_=pr),
             reads=(s_probs,), writes=(s_m8,))
        p.op("vector", lambda E: E.tensor_tensor(tt1[:, 0:1], m8[:, 0:1],
                                                 m8[:, 1:2], ALU.add),
             reads=(s_m8,), writes=(s_tt,))
        p.op("vector", lambda E: E.tensor_scalar_add(tt1[:, 0:1], tt1[:, 0:1],
                                                     1e-20),
             reads=(), writes=(s_tt,))
        p.op("vector", lambda E: E.reciprocal(tt1[:, 1:2], tt1[:, 0:1]),
             reads=(), writes=(s_tt,))
        p.op("vector", lambda E: E.tensor_tensor(tt1[:, 2:3], m8[:, 0:1],
                                                 tt1[:, 1:2], ALU.mult),
             reads=(s_m8,), writes=(s_tt,))
        p.op("vector", lambda E: E.tensor_tensor(tt1[:, 3:4], m8[:, 1:2],
                                                 tt1[:, 1:2], ALU.mult),
             reads=(s_m8,), writes=(s_tt,))
        p.op("vector", lambda E, pr=pr: E.tensor_scalar(tt2[:], pr, m8[:, 0:1],
                                                        None, ALU.is_equal),
             reads=(s_probs, s_m8), writes=(s_tt,))
        p.op("vector", lambda E, cb=cb: E.tensor_scalar(cb, tt2[:], tt1[:, 2:3],
                                                        None, ALU.mult),
             reads=(s_tt,), writes=(s_comb,))
        p.op("vector", lambda E: E.tensor_scalar(tt3[:], tt2[:], -1.0, 1.0,
                                                 ALU.mult, ALU.add),
             reads=(s_tt,), writes=(s_tt,))
        p.op("vector", lambda E, pr=pr: E.tensor_scalar(tt2[:], pr, m8[:, 1:2],
                                                        None, ALU.is_equal),
             reads=(s_probs, s_m8), writes=(s_tt,))
        p.op("vector", lambda E: E.tensor_scalar(tt2[:], tt2[:], tt1[:, 3:4],
                                                 None, ALU.mult),
             reads=(), writes=(s_tt,))
        p.op("vector", lambda E: E.tensor_tensor(tt2[:], tt2[:], tt3[:], ALU.mult),
             reads=(), writes=(s_tt,))
        p.op("vector", lambda E, cb=cb: E.tensor_tensor(cb, cb, tt2[:], ALU.add),
             reads=(s_tt,), writes=(s_comb,))
    if BUILD_UPTO < 8.4:
        p.barrier()
        return _finish(nc, p, ar, dbg, out, identb, identr, s_identb, s_identr)
    p.op("scalar", lambda E: E.activation(comb[:], comb[:], ACT.Sqrt),
         reads=(s_comb,), writes=(s_comb,))
    pC2_ctx = PSUM("pC2", 8, T)
    pC2 = pC2_ctx.__enter__()
    s_pC2 = Slot("pC2")
    emits = []
    for tc in range(TC):
        emits.append(lambda E, tc=tc:
                     E.matmul(pC2[0:8, tc * 128:(tc + 1) * 128],
                              comb[:, tc * 8:(tc + 1) * 8], identr[:],
                              start=True, stop=True))
    p.mm_group(emits, reads=(s_comb, s_identr), write=s_pC2)
    sqckm = ar.alloc("sqckm", [8, T], F32)
    s_sqckm = Slot("sqckm", ghosts)
    p.op("vector", lambda E: E.tensor_copy(sqckm[:, 0:T], pC2[0:8, 0:T]),
         reads=(s_pC2,), writes=(s_sqckm,))
    pC2_ctx.__exit__(None, None, None)
    tok_sqc = p.dma("sync", lambda E: E.dma_start(out=sqc_d[:, 0:T],
                                                  in_=sqckm[:, 0:T]),
                    reads=(s_sqckm,))
    p.barrier()
    pG_ctx = PSUM("pG", 128, 1024)
    pG = pG_ctx.__enter__()
    s_pG = Slot("pG")
    emits = []
    for dc in range(DC):
        emits.append(lambda E, dc=dc:
                     E.matmul(pG[:, dc * 128:(dc + 1) * 128], onesrow_b[:],
                              gb_row(dc), start=True, stop=True))
    p.mm_group(emits, reads=tuple(s_rows[16:24]) + (s_ones,), write=s_pG)
    s_gbb = Slot("gbb")
    p.op("vector", lambda E: E.tensor_copy(gbb[:], pG[:]),
         reads=(s_pG,), writes=(s_gbb,))
    pG_ctx.__exit__(None, None, None)
    pR_ctx.__exit__(None, None, None)
    p.barrier()
    ar.free("rw_b", slot_toks(s_rwb))
    ar.free("sqckm", slot_toks(s_sqckm))
    if BUILD_UPTO < 9:
        return _finish(nc, p, ar, dbg, out, identb, identr, s_identb, s_identr)

    ghosts = ar.take_ghosts() + [tok_sqc]
    h_all = ar.alloc("h_all", [128, NWC * FG], BF16)

    def hallw(wc):
        return h_all[:, wc * FG:(wc + 1) * FG]
    s_hall = [Slot(f"hall{wc}", ghosts) for wc in range(NWC)]
    cbt = [ar.alloc(f"cbt{i}", [128, FG], F32) for i in range(2)]
    s_cbt = [Slot(f"cbt{i}", ghosts) for i in range(2)]
    WG = 4
    wsh = ar.alloc("wsh", [128, 2 * DC * WG * 128], BF16)
    w1t = [wsh[:, i * 4096:(i + 1) * 4096] for i in range(2)]
    s_w1t = [Slot(f"w1t{i}", ghosts) for i in range(2)]
    htmp8 = [ar.alloc(f"htmp8{i}", [128, FG], BF16) for i in range(2)]
    s_htmp8 = [Slot(f"htmp8{i}", ghosts) for i in range(2)]
    w2g = [wsh[:, i * 4096:(i + 1) * 4096] for i in range(2)]
    s_w2g = [Slot(f"w2g{i}", ghosts) for i in range(2)]
    x1t = [ar.alloc(f"x1t{i}", [128, D], F32) for i in range(2)]
    s_x1t = [Slot(f"x1t{i}", ghosts + tok_x1) for i in range(2)]
    ost = [ar.alloc(f"ost{i}", [128, D], F32) for i in range(2)]
    s_ost = [Slot(f"ost{i}", ghosts) for i in range(2)]
    final = []
    NWG = NWC // WG
    GT = FG // 128
    for ti in range(T // FG):
        t0 = ti * FG
        pH_ctx = [PSUM(f"pH{ti}_{i}", 128, FG) for i in range(2)]
        pH = [c.__enter__() for c in pH_ctx]
        s_pH = [Slot(f"pH{i}") for i in range(2)]
        for e in range(2):
            p.dma("gpsimd",
                  lambda E, e=e, t0=t0: E.dma_start(
                      out=cbt[e][:].unsqueeze(1),
                      in_=sqc_d[e:e + 1, t0:t0 + FG].partition_broadcast(128)),
                  reads=(), writes=(s_cbt[e],))
        WGW = WG * 128
        for g in range(min(2, NWG)):
            p.dma("gpsimd",
                  lambda E, g=g: E.dma_start(
                      out=w1t[g % 2][:, 0:DC * WGW],
                      in_=w1bf[:, g * 4096:(g + 1) * 4096]),
                  reads=(s_w1bf[g],), writes=(s_w1t[g % 2],),
                  war=(s_w2g[g % 2],))
        for wc in range(NWC):
            e = WC_EXPERT[wc]
            g, wi = wc // WG, wc % WG
            i = wc % 2
            emits = []
            for dc in range(DC):
                emits.append(
                    lambda E, dc=dc, i=i, g=g, wi=wi, t0=t0, pH=pH, WGW=WGW:
                    E.matmul(pH[i][:, 0:FG],
                             w1t[g % 2][:, dc * WGW + wi * 128:dc * WGW + (wi + 1) * 128],
                             h2c(dc)[:, t0:t0 + FG],
                             start=(dc == 0), stop=(dc == DC - 1)))
            p.mm_group(emits, reads=tuple(s_h2) + (s_w1t[g % 2],), write=s_pH[i])
            p.op("vector",
                 lambda E, i=i, e=e, pH=pH: E.scalar_tensor_tensor(
                     htmp8[i][:, 0:FG], pH[i][:, 0:FG], 0.0, cbt[e % 2][:, 0:FG],
                     ALU.max, ALU.mult),
                 reads=(s_pH[i], s_cbt[e % 2]), writes=(s_htmp8[i],))
            p.op("scalar",
                 lambda E, i=i, wc=wc: E.activation(hallw(wc), htmp8[i][:, 0:FG],
                                                    ACT.Square),
                 reads=(s_htmp8[i],), writes=(s_hall[wc],))
            if wi == WG - 1 and g + 2 < NWG:
                p.dma("gpsimd",
                      lambda E, g=g: E.dma_start(
                          out=w1t[g % 2][:, 0:DC * WGW],
                          in_=w1bf[:, (g + 2) * 4096:(g + 3) * 4096]),
                      reads=(s_w1bf[g + 2],), writes=(s_w1t[g % 2],))
            if wc + 1 < NWC and WC_EXPERT[wc + 1] == e + 1 and e + 2 < 8:
                p.dma("gpsimd",
                      lambda E, e=e, t0=t0: E.dma_start(
                          out=cbt[e % 2][:].unsqueeze(1),
                          in_=sqc_d[e + 2:e + 3, t0:t0 + FG].partition_broadcast(128)),
                      reads=(), writes=(s_cbt[e % 2],))
        for c in reversed(pH_ctx):
            c.__exit__(None, None, None)
        p.barrier()
        pY_ctx = [PSUM(f"pY{ti}_{i}", 128, 1024)
                  for i in range(GT)]
        pY = [c.__enter__() for c in pY_ctx]
        s_pY = [Slot(f"pY{i}") for i in range(GT)]
        for tj in range(min(2, GT)):
            tc = ti * GT + tj
            p.dma("sync",
                  lambda E, tc=tc, tj=tj: E.dma_start(
                      out=x1t[tj % 2][:], in_=x1tm_d[tc * 128:(tc + 1) * 128, :]),
                  writes=(s_x1t[tj % 2],))
        for q in range(min(2, NWC // 4)):
            p.dma("scalar",
                  lambda E, q=q: E.dma_start(
                      out=w2g[q % 2][:], in_=w2bf[:, q * 4096:(q + 1) * 4096]),
                  reads=(s_w2bf[q],), writes=(s_w2g[q % 2],),
                  war=(s_w1t[q % 2],))
        for wc in range(NWC):
            q, wj = wc // 4, wc % 4
            for tj in range(GT):
                emits = []
                for j0, jl in fslices(D):
                    emits.append(
                        lambda E, wc=wc, tj=tj, j0=j0, jl=jl, q=q, wj=wj, pY=pY:
                        E.matmul(pY[tj][:, j0:j0 + jl],
                                 hallw(wc)[:, tj * 128:(tj + 1) * 128],
                                 w2g[q % 2][:, wj * D + j0:wj * D + j0 + jl],
                                 start=(wc == 0), stop=(wc == NWC - 1)))
                rd = (s_hall[wc], s_w2g[q % 2])
                if wc == NWC - 1:
                    p.mm_group(emits, reads=rd, write=s_pY[tj])
                else:
                    for e_ in emits[:-1]:
                        p.op("tensor", e_, reads=rd,
                             war=(s_pY[tj],) if wc == 0 else (), inc=False)
                    p.op("tensor", emits[-1], reads=rd,
                         war=(s_pY[tj],) if wc == 0 else (), inc=True)
            if wj == 3 and (q + 2) * 4 < NWC:
                p.dma("scalar",
                      lambda E, q=q: E.dma_start(
                          out=w2g[q % 2][:],
                          in_=w2bf[:, (q + 2) * 4096:(q + 3) * 4096]),
                      reads=(s_w2bf[q + 2],), writes=(s_w2g[q % 2],))
        for tj in range(GT):
            tc = ti * GT + tj
            oi = tj % 2
            p.op("vector",
                 lambda E, tj=tj, oi=oi, pY=pY: E.tensor_tensor(ost[oi][:], pY[tj][:],
                                                                gbb[:], ALU.mult),
                 reads=(s_pY[tj], s_gbb), writes=(s_ost[oi],))
            p.op("vector",
                 lambda E, oi=oi: E.tensor_tensor(ost[oi][:], ost[oi][:],
                                                  x1t[oi][:], ALU.add),
                 reads=(s_x1t[oi],), writes=(s_ost[oi],))
            final.append(p.dma("sync",
                               lambda E, tc=tc, oi=oi: E.dma_start(
                                   out=out[tc * 128:(tc + 1) * 128, :],
                                   in_=ost[oi][:]),
                               reads=(s_ost[oi],)))
            if tj + 2 < GT:
                tc2 = ti * GT + tj + 2
                p.dma("sync",
                      lambda E, tc2=tc2, oi=oi: E.dma_start(
                          out=x1t[oi][:],
                          in_=x1tm_d[tc2 * 128:(tc2 + 1) * 128, :]),
                      writes=(s_x1t[oi],))
        for c in reversed(pY_ctx):
            c.__exit__(None, None, None)
        p.barrier()
    p.finalize(final)
    global LAST_AR
    LAST_AR = ar
    with nc.Block() as block:
        with nc.allow_low_precision(reason="f32r/bf16 by design"):
            p.run_sections(block)
    return nc


LAST_AR = None


def _finish(nc, p, ar, dbg, out, identb, identr, s_identb, s_identr):
    slots, chunkf = dbg[DEV_TAP]
    ghosts = ar.take_ghosts()
    outst = [ar.alloc(f"outst{i}", [128, D], F32) for i in range(2)]
    s_outst = [Slot(f"outst{i}", ghosts) for i in range(2)]
    final = []
    pO_ctx = nc.psum_tensor("pO", [128, 1024], F32)
    pO = pO_ctx.__enter__()
    s_pO = Slot("pO")
    use_b = chunkf(0).dtype == BF16
    iden, s_iden = (identb, s_identb) if use_b else (identr, s_identr)
    for tc in range(TC):
        i = tc % 2
        emits = []
        for dc in range(DC):
            emits.append(lambda E, dc=dc, tc=tc, iden=iden:
                         E.matmul(pO[:, dc * 128:(dc + 1) * 128],
                                  chunkf(dc)[:, tc * 128:(tc + 1) * 128],
                                  iden[:], start=True, stop=True))
        p.mm_group(emits, reads=tuple(slots) + (s_iden,), write=s_pO)
        p.op("vector", lambda E, i=i: E.tensor_copy(outst[i][:], pO[:]),
             reads=(s_pO,), writes=(s_outst[i],))
        final.append(
            p.dma("sync",
                  lambda E, tc=tc, i=i: E.dma_start(
                      out=out[tc * 128:(tc + 1) * 128, :], in_=outst[i][:]),
                  reads=(s_outst[i],)))
    pO_ctx.__exit__(None, None, None)
    p.finalize(final)
    with nc.Block() as block:
        with nc.allow_low_precision(reason="f32r/bf16 by design"):
            p.run_sections(block)
    return nc


def _consts():
    return {
        "ident_b": np.eye(128, dtype=ml_dtypes.bfloat16),
        "ident_r": np.eye(128, dtype=np.float32),
        "eyerows": np.eye(16, dtype=np.float32).reshape(1, 256),
        "selpat": _selpat(),
    }


def _selpat():
    s = np.zeros((16, DC * 128), dtype=ml_dtypes.bfloat16)
    for h in range(16):
        s[h, (h // 2) * 128 + (h % 2) * 64:(h // 2) * 128 + (h % 2) * 64 + 64] = 1
    return s


def make_in_maps(L0_attn_x, L0_attn_qkv_w, L0_attn_proj_w, L0_attn_proj_b,
                 L0_attn_q_norm_w, L0_attn_k_norm_w, L0_ffn_router_w, L0_ffn_w1,
                 L0_ffn_w2, L0_c, L0_wmod):
    x = np.ascontiguousarray(np.asarray(L0_attn_x, dtype=np.float32))
    shared = {
        "qkv_w": np.ascontiguousarray(np.asarray(L0_attn_qkv_w, np.float32)),
        "proj_w": np.ascontiguousarray(np.asarray(L0_attn_proj_w, np.float32)),
        "proj_b": np.asarray(L0_attn_proj_b, np.float32).reshape(1, D),
        "q_norm_w": np.asarray(L0_attn_q_norm_w, np.float32).reshape(1, DH),
        "k_norm_w": np.asarray(L0_attn_k_norm_w, np.float32).reshape(1, DH),
        "router_w": np.ascontiguousarray(np.asarray(L0_ffn_router_w, np.float32)),
        "w1s": np.ascontiguousarray(
            np.asarray(L0_ffn_w1, np.float32).reshape(8, 128, 24, 4, 128)
            .transpose(1, 2, 0, 3, 4).reshape(128, 24 * 4096)),
        "w2s": np.ascontiguousarray(
            np.asarray(L0_ffn_w2, np.float32).reshape(96, 128, 1024)
            .transpose(1, 0, 2).reshape(128, 96 * 1024)),
        "wmod": np.ascontiguousarray(np.asarray(L0_wmod, np.float32)),
    }
    shared.update(_consts())
    carr = np.asarray(L0_c, np.float32)
    in_maps = []
    for b in range(x.shape[0]):
        m = dict(shared)
        m["x"] = np.ascontiguousarray(x[b][:T])
        m["c_row"] = np.ascontiguousarray(carr[b:b + 1])
        in_maps.append(m)
    return in_maps


def kernel(**inputs):
    in_maps = make_in_maps(**inputs)
    nc = build_program()
    res = run_bass_kernel_spmd(nc, in_maps, core_ids=list(range(len(in_maps))))
    return np.stack([np.asarray(r["out"], np.float32) for r in res.results], axis=0)
